# version 02
# baseline (speedup 1.0000x reference)
import numpy as np
import contextlib
import ml_dtypes
import concourse.bass as bass
import concourse.mybir as mybir
from concourse.bass_utils import run_bass_kernel_spmd

F32 = mybir.dt.float32
BF16 = mybir.dt.bfloat16
I32 = mybir.dt.int32
AF = mybir.ActivationFunctionType
ALU = mybir.AluOpType
AX = mybir.AxisListType

D = 1024
T = 2048
NC = 8
CH = 64
NCH = 64
HID = 3072

BF = ml_dtypes.bfloat16
DBG = set()
STOP_AFTER = None
SCAN_STEPS = 64
PAIRS = [[0, 1], [2, 3], [4, 5], [6, 7]]


_UPDATES = {}


class Sem:
    __slots__ = ("h", "n", "base")

    def __init__(self, h):
        self.h = h
        self.n = 0

    def inc(self, instr, k=1):
        instr.then_inc(self.h, k)
        self.n += k
        _UPDATES[id(instr)] = (instr, self, self.n)
        return self.n


_TRACKED = {
    "tensor_tensor", "tensor_scalar", "tensor_single_scalar", "tensor_copy",
    "tensor_add", "tensor_sub", "tensor_mul", "tensor_max", "reduce_sum",
    "reduce_max", "tensor_reduce", "scalar_tensor_tensor", "reciprocal",
    "memset", "activation", "copy", "mul", "add", "square", "sqrt",
    "dma_start",
}
_WRITE_KEYS = ("out", "accum_out", "ap")


def _ap_region(ap):
    try:
        name = ap.tensor.name
    except Exception:
        return None
    try:
        base = ap.base_partition()
        cnt = ap.partition_size()
    except Exception:
        base, cnt = 0, 128
    return (name, base, base + cnt)


def _regions_overlap(a, b):
    return a[0] == b[0] and a[1] < b[2] and b[1] < a[2]


class _EngineProxy:

    def __init__(self, eng, state):
        self._e = eng
        self._st = state

    def __getattr__(self, name):
        attr = getattr(self._e, name)
        if name not in _TRACKED or not callable(attr):
            return attr
        st = self._st

        def wrapper(*args, **kwargs):
            writes, reads = [], []
            import concourse.ap as _apmod
            APT = _apmod.AP
            for k, v in kwargs.items():
                if isinstance(v, APT):
                    r = _ap_region(v)
                    if r is None:
                        continue
                    (writes if k in _WRITE_KEYS else reads).append(r)
            for i, v in enumerate(args):
                if isinstance(v, APT):
                    r = _ap_region(v)
                    if r is None:
                        continue
                    if name == "dma_start" and "out" not in kwargs and i == 0:
                        writes.append(r)
                    elif i == 0 and "out" not in kwargs:
                        writes.append(r)
                    else:
                        reads.append(r)
            conflict = False
            for r in reads:
                if any(_regions_overlap(r, w) for w in st["writes"]):
                    conflict = True
            if not conflict:
                for w in writes:
                    if any(_regions_overlap(w, x)
                           for x in st["writes"] + st["reads"]):
                        conflict = True
            if conflict and st["last"] is not None:
                inst, k = st["last"]
                upd = _UPDATES.get(id(inst))
                if upd is not None and upd[0] is inst:
                    self._e.wait_ge(upd[1].h, upd[2])
                else:
                    assert k != 16, "dma without completion sem"
                    sem = st["sem"]
                    sem.inc(inst, k)
                    self._e.wait_ge(sem.h, sem.n)
                st["writes"], st["reads"] = [], []
            inst = attr(*args, **kwargs)
            st["last"] = (inst, 16 if name == "dma_start" else 1)
            st["writes"] += writes
            st["reads"] += reads
            return inst
        return wrapper


class B:
    def __init__(self):
        self.nc = bass.Bass("TRN2", target_bir_lowering=False, debug=False)
        self.es = contextlib.ExitStack()
        self.blk = None
        self.auto_state = {}
        self.sems = {}

    def din(self, name, shape, dtype=F32):
        return self.nc.dram_tensor(name, shape, dtype, kind="ExternalInput").ap()

    def dout(self, name, shape, dtype=F32):
        return self.nc.dram_tensor(name, shape, dtype, kind="ExternalOutput").ap()

    def dtmp(self, name, shape, dtype=F32):
        return self.nc.dram_tensor(name, shape, dtype).ap()

    def sbuf(self, name, shape, dtype):
        return self.es.enter_context(self.nc.sbuf_tensor(name, shape, dtype))

    def psum(self, name, shape, dtype=F32):
        return self.es.enter_context(self.nc.psum_tensor(name, shape, dtype))

    def sem(self, name):
        s = Sem(self.es.enter_context(self.nc.semaphore(name)))
        self.sems[name] = s
        return s

    def reuse(self, name):
        return self.sems[name]

    def on(self, engine_name, fn):
        if engine_name in ("vector", "scalar", "gpsimd"):
            st = self.auto_state.get(engine_name)
            if st is None:
                st = {"last": None, "writes": [], "reads": [],
                      "sem": self.sem("auto_" + engine_name)}
                self.auto_state[engine_name] = st

            def wrapped(e, fn=fn, st=st):
                fn(_EngineProxy(e, st))
            getattr(self.blk, engine_name)(wrapped)
        else:
            getattr(self.blk, engine_name)(fn)


class _Stop(Exception):
    pass


def build_program():
    try:
        return _build_program()
    except _Stop as s:
        return s.args[0]


def _build_program():
    b = B()
    nc = b.nc
    on = b.on
    sem = b.sem

    x_fm = b.din("x_fm", [D, T])
    wqkvT = b.din("wqkvT", [D, 3 * D])
    bqkv_c = b.din("bqkv_c", [128, 24])
    woT = b.din("woT", [D, D])
    bo_c = b.din("bo_c", [128, 8])
    wq_in = b.din("wq", [64, 1])
    wk_in = b.din("wk", [64, 1])
    gateWT = b.din("gateWT", [D, 8])
    e0T = b.din("e0T", [D, D])
    e1T = b.din("e1T", [D, D])
    outWT = b.din("outWT", [D, D])
    m0T = b.din("m0T", [D, D])
    vg1T = b.din("vg1T", [D, D])
    vg2T = b.din("vg2T", [D, D])
    vg2q = b.din("vg2q", [128, 8192])
    eag = b.din("eag", [D, 3])
    eab = b.din("eab", [3, 1])
    wgTi = b.din("wgTi", [D, HID], mybir.dt.int8)
    wuTi = b.din("wuTi", [D, HID], mybir.dt.int8)
    wdTi = b.din("wdTi", [HID, D], mybir.dt.int8)
    parity_in = b.din("parity", [1, 1])
    identf = b.din("identf", [128, 128])
    esel_in = b.din("esel", [64, 1024], BF16)
    l1msk = b.din("l1msk", [64, 4096], BF16)

    out_fm = b.dout("out_fm", [D, T])

    x1sp = b.dtmp("x1sp", [D, T])
    qt_sp = b.dtmp("qt_sp", [D, T], BF16)
    qkraw_sp = b.dtmp("qkraw_sp", [2 * D, T], BF16)
    ffnout_sp = b.dtmp("ffnout_sp", [D, T], BF16)
    moa_sp = b.dtmp("moa_sp", [D, T], BF16)

    kt_src = b.dtmp("kt_src", [D, T], BF16)
    v_src = b.dtmp("v_src", [T, D], BF16)
    kmax_src = b.dtmp("kmax_src", [1, 16])
    kt_dst = b.dtmp("kt_dst", [2, D, T], BF16)
    v_dst = b.dtmp("v_dst", [2, T, D], BF16)
    kmax_dst = b.dtmp("kmax_dst", [2, 1, 16])

    a2b_src = b.dtmp("a2b_src", [128, 512], BF16)
    a2b_dst = b.dtmp("a2b_dst", [2, 128, 512], BF16)
    a2f_src = b.dtmp("a2f_src", [2, 32])
    a2f_dst = b.dtmp("a2f_dst", [2, 2, 32])
    a3p_src = b.dtmp("a3p_src", [64, T], BF16)
    a3p_dst = b.dtmp("a3p_dst", [2, 64, T], BF16)
    a3b_src = b.dtmp("a3b_src", [64, 32])
    a3b_dst = b.dtmp("a3b_dst", [2, 64, 32])
    base_srcT = b.dtmp("base_srcT", [D, T], BF16)
    base_dstT = b.dtmp("base_dstT", [2, D, T], BF16)
    b2_srcT = b.dtmp("b2_srcT", [D, T], BF16)
    b2_dstT = b.dtmp("b2_dstT", [2, D, T], BF16)

    bmt_hbm = b.dtmp("bmt_hbm", [64, D], BF16)
    mkt_hbm = b.dtmp("mkt_hbm", [64, D], BF16)
    w1bt_hbm = b.dtmp("w1bt_hbm", [64, D], BF16)
    w1mt_hbm = b.dtmp("w1mt_hbm", [64, D], BF16)
    uvrow_hbm = b.dtmp("uvrow_hbm", [2, D], BF16)

    XN = b.sbuf("XN", [128, NC * T], BF16)
    HA = b.sbuf("HA", [128, NC * T], BF16)
    HB = b.sbuf("HB", [128, NC * T], BF16)
    MOW = b.sbuf("MOW", [128, 8192], BF16)
    QP = b.sbuf("QP", [128, 2 * T], BF16)
    WB1 = b.sbuf("WB1", [128, 2 * 1024], BF16)
    WB3 = b.sbuf("WB3", [128, 3072], BF16)
    EB = b.sbuf("EB", [128, 2 * T], BF16)
    KHV = b.sbuf("KHV", [128, 2 * T], BF16)
    VO = b.sbuf("VO", [128, 6208], BF16)
    CHK = b.sbuf("CHK", [128, T], F32)
    CH2 = b.sbuf("CH2", [128, 1024], F32)
    ROWS = b.sbuf("ROWS", [128, T], F32)
    RW2 = b.sbuf("RW2", [128, T], BF16)
    SM = b.sbuf("SM", [128, 128], F32)
    SMB = b.sbuf("SMB", [128, 1024], BF16)
    ONE = b.sbuf("ONE", [128, 128], BF16)
    IDF = b.sbuf("IDF", [128, 128], F32)
    ESEL = b.sbuf("ESEL", [64, 1024], BF16)
    PT = b.sbuf("PT", [64, 4096], BF16)
    UT = b.sbuf("UT", [64, D], BF16)
    VT = b.sbuf("VT", [64, D], BF16)
    PBG = b.sbuf("PBG", [64, 128], BF16)
    KSB = b.sbuf("KSB", [128, 512], BF16)
    BMC = b.sbuf("BMC", [128, 512], BF16)

    PA = b.psum("PA", [128, 2048], F32)
    PB = b.psum("PB", [128, 2048], F32)

    b.blk = b.es.enter_context(nc.Block())

    s_bar = sem("s_bar")
    ENGS = ["tensor", "scalar", "vector", "gpsimd", "sync"]

    def barrier():
        target = s_bar.n + 5

        def ft(e):
            s_bar.inc(e.ldweights(ONE[0:1, 0:128]), 1)
            e.wait_ge(s_bar.h, target)
        b.on("tensor", ft)

        def fs(e):
            s_bar.inc(e.copy(out=SM[0:1, 98:99], in_=SM[0:1, 96:97]), 1)
            e.wait_ge(s_bar.h, target)
        b.on("scalar", fs)

        def fv(e):
            s_bar.inc(e.tensor_copy(out=SM[0:1, 99:100], in_=SM[0:1, 96:97]), 1)
            e.wait_ge(s_bar.h, target)
        b.on("vector", fv)

        def fg(e):
            s_bar.inc(e.memset(SM[0:1, 100:101], 0.0), 1)
            e.wait_ge(s_bar.h, target)
        b.on("gpsimd", fg)

        def fy(e):
            e.sem_inc(s_bar.h, 1)
            e.wait_ge(s_bar.h, target)
        b.on("sync", fy)
        s_bar.n = target
        for st in b.auto_state.values():
            st["writes"], st["reads"] = [], []

    s_st = sem("s_st")
    s_dbg = sem("s_dbg")
    s_cc = sem("s_cc")

    _CH = {n: sem("ch_" + n) for n in ("vector", "scalar", "tensor", "gpsimd")}

    def _mk_chain(name):
        s = _CH[name]

        def chain(e, instr):
            s.inc(instr, 1)
            e.wait_ge(s.h, s.n)
        return chain
    chv = _mk_chain("vector")
    chs = _mk_chain("scalar")

    def dbg(name, src_ap, shape, dtype):
        if name not in DBG:
            return
        d = b.dout("dbg_" + name, shape, dtype)

        def f(e):
            s_dbg.inc(e.dma_start(out=d, in_=src_ap), 16)
        on("sync", f)

    (R_M, R_ISG, R_MIV, R_TMP, R_TMP2, R_QM, R_KM, R_BND,
     R_ETA, R_ALP, R_GATE, R_A, R_B, R_KN2, R_GAM, R_NM,
     R_G, R_S1, R_S2, R_S3, R_ERR, R_T1, R_T2, R_T3, R_VN,
     R_M0N, R_NRM, R_WT) = range(28)
    ROWMAP = {
        R_M: (0, 0), R_TMP: (32, 0), R_ISG: (64, 0), R_MIV: (96, 0),
        R_QM: (0, 1024), R_KM: (0, 1056), R_TMP2: (0, 1072),
        R_BND: (0, 1120),
        R_ETA: (0, 1088), R_ALP: (0, 1152), R_GATE: (0, 1216),
        R_A: (0, 1280), R_B: (0, 1344), R_KN2: (0, 1408), R_GAM: (0, 1472),
        R_NM: (0, 1536), R_G: (0, 1540), R_S1: (0, 1544), R_S2: (0, 1548),
        R_S3: (0, 1552), R_M0N: (0, 1556), R_NSC: (0, 1560),
        R_ERR: (32, 0), R_T1: (32, 1024), R_T2: (64, 0), R_T3: (64, 1024),
        R_VN: (96, 0), R_NRM: (96, 0), R_WT: (96, 1024),
    } if False else None

    _RM = {
        R_M: (0, 0), R_TMP: (32, 0), R_ISG: (64, 0), R_MIV: (96, 0),
        R_QM: (0, 1024), R_KM: (0, 1056), R_TMP2: (0, 1072),
        R_BND: (0, 1120),
        R_ETA: (0, 1088), R_ALP: (0, 1152), R_GATE: (0, 1216),
        R_A: (0, 1280), R_B: (0, 1344), R_KN2: (0, 1408), R_GAM: (0, 1472),
        R_NM: (0, 1536), R_G: (0, 1540), R_S1: (0, 1544), R_S2: (0, 1548),
        R_S3: (0, 1552), R_M0N: (0, 1556),
        R_ERR: (32, 0), R_T1: (32, 1024), R_T2: (64, 0), R_T3: (64, 1024),
        R_VN: (96, 0), R_NRM: (96, 0), R_WT: (96, 1024),
    }

    def row(i, lo=0, hi=T):
        p, c0 = _RM[i]
        assert c0 + hi <= T
        return ROWS[p:p + 1, c0 + lo:c0 + hi]

    def rows(i, lo=0, hi=T, step=1):
        p, c0 = _RM[i]
        return ROWS[p:p + 1, c0 + lo:c0 + hi:step]

    def brow(i, lo=0, hi=T):
        assert i == 0
        return RW2[0:1, lo:hi]

    def rearr_w(w):
        return w.rearrange("(c p) f -> p c f", p=128)

    def xn_rhs(c, nt):
        return XN[:, c * T + nt * 512:c * T + (nt + 1) * 512]

    def stream_gemm(tag, n_out, w_view, rhs_fn, evac_emit, nt_count=4,
                    nt_size=512, wait_emit=None, sems=None):
        if sems is None:
            s_wp = [sem(tag + "_w0"), sem(tag + "_w1")]
            s_m = sem(tag + "_m")
            s_e = sem(tag + "_e")
        else:
            s_wp, s_m, s_e = [sems[0], sems[1]], sems[2], sems[3]
        wb = [s_wp[0].n, s_wp[1].n]
        mb = s_m.n
        ebb = s_e.n
        s_m.base = mb
        s_e.base = ebb

        def f(e):
            if wait_emit is not None:
                wait_emit(e)
            for fc in range(n_out):
                if fc >= 2:
                    e.wait_ge(s_m.h, mb + fc - 1)
                s_wp[fc % 2].inc(e.dma_start(
                    out=WB1[:, (fc % 2) * 1024:(fc % 2) * 1024 + 1024]
                        .rearrange("p (c f) -> p c f", c=8),
                    in_=w_view(fc)), 16)
        on("gpsimd", f)

        def f(e):
            for fc in range(n_out):
                e.wait_ge(s_wp[fc % 2].h, wb[fc % 2] + 16 * (fc // 2 + 1))
                if fc >= 2:
                    e.wait_ge(s_e.h, ebb + fc - 1)
                wv = WB1[:, (fc % 2) * 1024:(fc % 2) * 1024 + 1024] \
                    .rearrange("p (c f) -> p c f", c=8)
                dst = PA if fc % 2 == 0 else PB
                last = None
                for nt in range(nt_count):
                    for c in range(NC):
                        last = e.matmul(dst[:, nt * nt_size:(nt + 1) * nt_size],
                                        wv[:, c, :], rhs_fn(c, nt),
                                        start=(c == 0), stop=(c == NC - 1))
                s_m.inc(last, 1)
        on("tensor", f)
        evac_emit(s_m, s_e)
        return s_m, s_e

    s_ld = sem("s_ld")
    s_c0g = sem("s_c0g")
    s_c0s = sem("s_c0s")
    s_c0v = sem("s_c0v")

    def f(e):
        s_ld.inc(e.dma_start(out=SM[0:64, 0:1], in_=wq_in), 16)
        s_ld.inc(e.dma_start(out=SM[64:128, 0:1], in_=wq_in), 16)
        s_ld.inc(e.dma_start(out=SM[0:64, 1:2], in_=wk_in), 16)
        s_ld.inc(e.dma_start(out=SM[64:128, 1:2], in_=wk_in), 16)
        s_ld.inc(e.dma_start(out=SM[0:3, 2:3], in_=eab), 16)
        s_ld.inc(e.dma_start(out=SM[0:1, 3:4], in_=parity_in), 16)
        s_ld.inc(e.dma_start(out=SM[:, 8:32], in_=bqkv_c), 16)
        s_ld.inc(e.dma_start(out=SM[:, 32:40], in_=bo_c), 16)
        s_ld.inc(e.dma_start(out=IDF[:], in_=identf), 16)
        s_ld.inc(e.dma_start(out=ESEL[:], in_=esel_in), 16)
    on("sync", f)
    ld0 = s_ld.n

    def f(e):
        e.memset(ONE[:], 1.0)
        e.memset(SM[:, 96:97], 1e-5)
        e.memset(SM[:, 97:98], 1e-6)
        s_c0g.inc(e.memset(SMB[:, 0:512], 0.0), 1)
    on("gpsimd", f)
    c0a = s_c0g.n


    def f(e):
        e.wait_ge(s_ld.h, ld0)
        e.activation(out=SM[:, 4:5], in_=SM[:, 0:1], func=AF.Square)
        e.activation(out=SM[:, 5:6], in_=SM[:, 1:2], func=AF.Square)
        e.copy(out=SMB[:, 512:513], in_=SM[:, 0:1])
        s_c0s.inc(e.copy(out=SMB[:, 513:514], in_=SM[:, 1:2]), 1)
    on("scalar", f)
    c0b = s_c0s.n

    def f(e):
        e.wait_ge(s_c0g.h, c0a)
        e.wait_ge(s_c0s.h, c0b)
        for c in range(NC):
            bq, bk = c * 32, 256 + c * 32
            e.tensor_copy(out=SMB[0:64, bq + 2 * c:bq + 2 * c + 1], in_=SM[0:64, 4:5])
            e.tensor_copy(out=SMB[64:128, bq + 2 * c + 1:bq + 2 * c + 2],
                          in_=SM[64:128, 4:5])
            e.tensor_copy(out=SMB[0:64, bk + 2 * c:bk + 2 * c + 1], in_=SM[0:64, 5:6])
            s_c0v.inc(e.tensor_copy(out=SMB[64:128, bk + 2 * c + 1:bk + 2 * c + 2],
                                   in_=SM[64:128, 5:6]), 1)
    on("vector", f)
    c0c = s_c0v.n

    def f(e):
        e.wait_ge(s_c0v.h, c0c)
        last = None
        for c in range(NC):
            for half in (0, 1):
                col = c * 32 + 16 + 2 * c + half
                lo, hi = (0, 64) if half == 0 else (64, 128)
                e.memset(SMB[lo:hi, col:col + 1], 1.0 / 64.0)
                last = e.memset(SMB[lo:hi, 256 + col:256 + col + 1], 1.0 / 64.0)
        s_c0g.inc(last, 1)
    on("gpsimd", f)

    def emit_ln(x_hbm, pref):
        sxp = [sem(pref + "x0"), sem(pref + "x1")]
        sv = sem(pref + "v")
        sq = sem(pref + "q")
        sm = sem(pref + "m")
        xv = x_hbm.rearrange("(c p) t -> c p t", p=128)

        def f(e):
            for c in range(NC):
                if c >= 2:
                    e.wait_ge(sv.h, c - 1)
                buf = (EB if c % 2 == 0 else KHV)[:, 0:2 * T].bitcast(F32)[:, 0:T]
                sxp[c % 2].inc(e.dma_start(out=buf, in_=xv[c]), 16)
        on("sync", f)

        def f(e):
            for c in range(NC):
                e.wait_ge(sxp[c % 2].h, 16 * (c // 2 + 1))
                buf = (EB if c % 2 == 0 else KHV)[:, 0:2 * T].bitcast(F32)[:, 0:T]
                sv.inc(e.tensor_copy(out=XN[:, c * T:(c + 1) * T], in_=buf), 1)
        on("vector", f)

        def f(e):
            for c in range(NC):
                e.wait_ge(sv.h, c + 1)
                if c >= 2:
                    e.wait_ge(sm.h, 2 * (c - 1))
                buf = CHK[:, 0:T].bitcast(BF16)[:, 0:T] if c % 2 == 0 else CH2[:, 0:1024].bitcast(BF16)[:, 0:T]
                sq.inc(e.activation(out=buf, in_=XN[:, c * T:(c + 1) * T],
                                    func=AF.Square), 1)
        on("scalar", f)

        def f(e):
            for c in range(NC):
                e.wait_ge(sv.h, c + 1)
                for nt in range(4):
                    e.matmul(PA[:, nt * 512:(nt + 1) * 512], ONE[:],
                             XN[:, c * T + nt * 512:c * T + (nt + 1) * 512],
                             start=(c == 0), stop=(c == NC - 1))
                e.wait_ge(sq.h, c + 1)
                buf = CHK[:, 0:T].bitcast(BF16)[:, 0:T] if c % 2 == 0 else CH2[:, 0:1024].bitcast(BF16)[:, 0:T]
                last = None
                for nt in range(4):
                    last = e.matmul(PB[:, nt * 512:(nt + 1) * 512], ONE[:],
                                    buf[:, nt * 512:(nt + 1) * 512],
                                    start=(c == 0), stop=(c == NC - 1))
                sm.inc(last, 2)
        on("tensor", f)
        m1 = sm.n
        r_m = ROWS[0:1, 0:T]
        r_var = CHK[0:1, 0:T]
        r_isg = EB[0:1, 0:2 * T].bitcast(F32)[:, 0:T]
        r_miv = KHV[0:1, 0:2 * T].bitcast(F32)[:, 0:T]
        rb_isg = QP[0:1, 0:T]
        rb_miv = QP[0:1, T:2 * T]

        def f(e):
            e.wait_ge(sm.h, m1)
            e.mul(out=r_m, in_=PA[0:1, 0:T], mul=1.0 / 1024.0)
            sq.inc(e.mul(out=r_var, in_=PB[0:1, 0:T], mul=1.0 / 1024.0), 1)
        on("scalar", f)
        q1 = sq.n

        def f(e):
            e.wait_ge(sq.h, q1)
            e.tensor_tensor(out=r_isg, in0=r_m, in1=r_m, op=ALU.mult)
            sv.inc(e.tensor_sub(out=r_var, in0=r_var, in1=r_isg), 1)
        on("vector", f)
        v1 = sv.n

        def f(e):
            e.wait_ge(sv.h, v1)
            e.activation(out=r_isg, in_=r_var, func=AF.Ln,
                                bias=SM[0:1, 96:97])
            sq.inc(e.activation(out=r_isg, in_=r_isg, func=AF.Exp,
                                scale=-0.5), 1)
        on("scalar", f)
        q2 = sq.n

        def f(e):
            e.wait_ge(sq.h, q2)
            e.tensor_tensor(out=r_miv, in0=r_m, in1=r_isg, op=ALU.mult)
            e.tensor_copy(out=rb_isg, in_=r_isg)
            sv.inc(e.tensor_copy(out=rb_miv, in_=r_miv), 1)
        on("vector", f)
        v2 = sv.n

        def f(e):
            e.wait_ge(sv.h, v2)
            last = None
            for nt in range(4):
                e.matmul(PA[:, nt * 512:(nt + 1) * 512], ONE[0:1, :],
                         rb_isg[:, nt * 512:(nt + 1) * 512], start=True, stop=True)
                last = e.matmul(PB[:, nt * 512:(nt + 1) * 512], ONE[0:1, :],
                                rb_miv[:, nt * 512:(nt + 1) * 512], start=True,
                                stop=True)
            sm.inc(last, 1)
        on("tensor", f)
        m2 = sm.n

        def f(e):
            e.wait_ge(sm.h, m2)
            for c in range(NC):
                e.tensor_tensor(out=CHK[:, 0:T], in0=XN[:, c * T:(c + 1) * T],
                                       in1=PA[:, 0:T], op=ALU.mult)
                sv.inc(e.tensor_tensor(out=XN[:, c * T:(c + 1) * T],
                                       in0=CHK[:, 0:T], in1=PB[:, 0:T],
                                       op=ALU.subtract), 1)
        on("vector", f)


    def ckpt(name):
        if STOP_AFTER == name:
            def f(e):
                e.wait_ge(s_st.h, s_st.n)
                e.wait_ge(s_dbg.h, s_dbg.n)
                s_fin2 = sem("s_fin2")
                for dc in range(NC):
                    v = XN[:, dc * T:(dc + 1) * T].bitcast(F32)[:, 0:T // 2]
                    s_fin2.inc(e.dma_start(
                        out=out_fm.rearrange("(c p) t -> c p t", p=128)[dc][:, 0:T // 2],
                        in_=v), 16)
                    s_fin2.inc(e.dma_start(
                        out=out_fm.rearrange("(c p) t -> c p t", p=128)[dc][:, T // 2:T],
                        in_=v), 16)
                e.wait_ge(s_fin2.h, s_fin2.n)
            on("sync", f)
            barrier()
            raise _Stop(b)

    emit_ln(x_fm, "l0")
    barrier()
    ckpt("ln0")
    dbg("xn", XN[:, 0:T], [128, T], BF16)

    s_qkp = [sem("s_qk0"), sem("s_qk1")]

    def qk_evac(s_m, s_e):
        def f(e):
            for fc in range(16):
                e.wait_ge(s_m.h, fc + 1)
                if fc >= 2:
                    e.wait_ge(s_qkp[fc % 2].h, 16 * (fc // 2))
                src = PA if fc % 2 == 0 else PB
                stg = EB[:, (fc % 2) * T:(fc % 2) * T + T]
                e.activation(out=stg, in_=src[:, 0:T], func=AF.Identity,
                                    bias=SM[:, 8 + fc:9 + fc], scale=1.0)
                sqdst = (HA if fc < 8 else HB)[:, (fc % 8) * T:(fc % 8) * T + T]
                s_e.inc(e.activation(out=sqdst, in_=stg, func=AF.Square), 1)
        on("scalar", f)
        e0 = s_e.n - 16

        def f(e):
            for fc in range(16):
                e.wait_ge(s_e.h, e0 + fc + 1)
                stg = EB[:, (fc % 2) * T:(fc % 2) * T + T]
                s_qkp[fc % 2].inc(e.dma_start(
                    out=qkraw_sp.rearrange("(f p) t -> f p t", p=128)[fc],
                    in_=stg), 16)
        on("sync", f)

    stream_gemm("qkv", 16, lambda fc: rearr_w(wqkvT)[:, :, fc * 128:(fc + 1) * 128],
                xn_rhs, qk_evac)
    barrier()
    ckpt("qkv")

    s_p4m = sem("s_p4m")
    s_p4s = sem("s_p4s")
    s_p4v = sem("s_p4v")

    def f(e):
        last = None
        for ts in range(16):
            for c in range(NC):
                last = e.matmul(PA[:, ts * 32:(ts + 1) * 32],
                                HA[:, c * T + ts * 128:c * T + (ts + 1) * 128],
                                SMB[:, c * 32:(c + 1) * 32],
                                start=(c == 0), stop=(c == NC - 1))
                last = e.matmul(PB[:, ts * 32:(ts + 1) * 32],
                                HB[:, c * T + ts * 128:c * T + (ts + 1) * 128],
                                SMB[:, 256 + c * 32:256 + (c + 1) * 32],
                                start=(c == 0), stop=(c == NC - 1))
        s_p4m.inc(last, 1)
    on("tensor", f)
    p4_mm = s_p4m.n

    def f(e):
        e.wait_ge(s_p4m.h, p4_mm)
        for ts in range(16):
            e.activation(out=CHK[:, ts * 16:(ts + 1) * 16],
                         in_=PA[:, ts * 32 + 16:ts * 32 + 32], func=AF.Ln, bias=SM[:, 97:98])
            last_ln = e.activation(out=CH2[:, ts * 16:(ts + 1) * 16],
                         in_=PB[:, ts * 32 + 16:ts * 32 + 32], func=AF.Ln, bias=SM[:, 97:98])
        last_ln
        for ts in range(16):
            e.activation(out=CHK[:, 256 + ts * 16:256 + (ts + 1) * 16],
                         in_=CHK[:, ts * 16:(ts + 1) * 16], func=AF.Exp, scale=-0.5)
            e.activation(out=CH2[:, 256 + ts * 16:256 + (ts + 1) * 16],
                         in_=CH2[:, ts * 16:(ts + 1) * 16], func=AF.Exp, scale=-0.5)
            e.activation(out=CHK[:, 512 + ts * 16:512 + (ts + 1) * 16],
                         in_=CHK[:, ts * 16:(ts + 1) * 16], func=AF.Exp, scale=-1.0)
            s_p4s.inc(e.activation(out=CH2[:, 512 + ts * 16:512 + (ts + 1) * 16],
                                  in_=CH2[:, ts * 16:(ts + 1) * 16], func=AF.Exp,
                                  scale=-1.0), 1)
    on("scalar", f)
    p4_sc = s_p4s.n

    def f(e):
        e.wait_ge(s_p4s.h, p4_sc)
        for ts in range(16):
            e.tensor_tensor(out=CHK[:, 512 + ts * 16:512 + (ts + 1) * 16],
                            in0=CHK[:, 512 + ts * 16:512 + (ts + 1) * 16],
                            in1=PA[:, ts * 32:ts * 32 + 16], op=ALU.mult)
            e.tensor_tensor(out=CH2[:, 512 + ts * 16:512 + (ts + 1) * 16],
                            in0=CH2[:, 512 + ts * 16:512 + (ts + 1) * 16],
                            in1=PB[:, ts * 32:ts * 32 + 16], op=ALU.mult)
        e.tensor_copy(out=SM[0:1, 6:7], in_=SM[0:1, 96:97])
        for buf in (CHK, CH2):
            n = 16
            while n > 1:
                last_l = None
                for i in range(n // 2):
                    a0 = 512 + i * 16
                    a1 = 512 + (i + n // 2) * 16
                    last_l = e.tensor_tensor(out=buf[:, a0:a0 + 16],
                                             in0=buf[:, a0:a0 + 16],
                                             in1=buf[:, a1:a1 + 16], op=ALU.max)
                last_l
                n //= 2
        s_p4v.inc(e.tensor_copy(out=SM[0:1, 6:7], in_=SM[0:1, 96:97]), 1)
    on("vector", f)
    p4_ve = s_p4v.n

    def f(e):
        e.wait_ge(s_p4v.h, p4_ve)
        last = None
        for ts in range(16):
            last = e.transpose(PA[0:16, ts * 128:(ts + 1) * 128],
                               CHK[:, 256 + ts * 16:256 + (ts + 1) * 16], IDF[:])
            last = e.transpose(PB[0:16, ts * 128:(ts + 1) * 128],
                               CH2[:, 256 + ts * 16:256 + (ts + 1) * 16], IDF[:])
        s_p4m.inc(last, 1)
    on("tensor", f)
    p4_tp = s_p4m.n

    def f(e):
        e.wait_ge(s_p4m.h, p4_tp)
        e.tensor_copy(out=RW2[0:16, 0:T], in_=PA[0:16, 0:2048])
        s_p4v.inc(e.tensor_copy(out=QP[0:16, 0:T], in_=PB[0:16, 0:2048]), 1)
    on("vector", f)
    p4_v2 = s_p4v.n

    def f(e):
        e.wait_ge(s_p4v.h, p4_v2)
        last = e.transpose(PA[0:16, 0:128], CHK[:, 512:528], IDF[:])
        last = e.transpose(PB[0:16, 0:128], CH2[:, 512:528], IDF[:])
        s_p4m.inc(last, 1)
    on("tensor", f)
    p4_t2 = s_p4m.n

    def f(e):
        e.wait_ge(s_p4m.h, p4_t2)
        e.reduce_max(SM[0:16, 60:61], PA[0:16, 0:128], axis=AX.X)
        s_p4v.inc(e.reduce_max(SM[0:16, 61:62], PB[0:16, 0:128], axis=AX.X), 1)
    on("vector", f)
    p4_v3 = s_p4v.n

    def f(e):
        e.wait_ge(s_p4v.h, p4_v3)
        last = e.transpose(PA[0:1, 0:16], SM[0:16, 60:61], IDF[0:16, 0:16])
        last = e.transpose(PA[0:1, 64:80], SM[0:16, 61:62], IDF[0:16, 0:16])
        s_p4m.inc(last, 1)
    on("tensor", f)
    p4_t3 = s_p4m.n

    def f(e):
        e.wait_ge(s_p4m.h, p4_t3)
        e.tensor_copy(out=row(R_QM, 0, 16), in_=PA[0:1, 0:16])
        s_p4v.inc(e.tensor_copy(out=row(R_KM, 0, 16), in_=PA[0:1, 64:80]), 1)
    on("vector", f)
    p4_v4 = s_p4v.n

    def f(e):
        e.wait_ge(s_p4v.h, p4_v4)
        s_st.inc(e.dma_start(out=kmax_src, in_=row(R_KM, 0, 16)), 16)
    on("sync", f)
    barrier()
    ckpt("p4")
    dbg("irmsq", RW2[0:16, 0:T], [16, T], BF16)
    dbg("qn2max", row(R_QM, 0, 16), [1, 16], F32)
    dbg("kn2max", row(R_KM, 0, 16), [1, 16], F32)

    s_v = sem("s_v")
    s_vm = sem("s_vm")
    s_vev = sem("s_vev")
    s_vstp = [sem("s_vst0"), sem("s_vst1")]

    def f(e):
        e.wait_ge(s_p4m.h, p4_mm)
        s_v.inc(e.dma_start(
            out=HA[:, 0:8192].rearrange("p (c f) -> p c f", c=NC),
            in_=rearr_w(wqkvT)[:, :, 2048:3072]), 16)
    on("gpsimd", f)
    v_w = s_v.n

    def f(e):
        e.wait_ge(s_v.h, v_w)
        wv = HA[:, 0:8192].rearrange("p (c f) -> p c f", c=NC)
        for ts in range(16):
            if ts >= 2:
                e.wait_ge(s_vev.h, ts - 1)
            dst = PA if ts % 2 == 0 else PB
            last = None
            for nt in range(2):
                for c in range(NC):
                    last = e.matmul(dst[:, nt * 512:(nt + 1) * 512],
                                    XN[:, c * T + ts * 128:c * T + (ts + 1) * 128],
                                    wv[:, c, nt * 512:(nt + 1) * 512],
                                    start=(c == 0), stop=(c == NC - 1))
            s_vm.inc(last, 1)
    on("tensor", f)

    def f(e):
        for ts in range(16):
            e.wait_ge(s_vm.h, ts + 1)
            if ts >= 2:
                e.wait_ge(s_vstp[ts % 2].h, 16 * (ts // 2))
            src = PA if ts % 2 == 0 else PB
            stg = KHV[:, (ts % 2) * T:(ts % 2) * T + 1024]
            s_vev.inc(e.tensor_copy(out=stg, in_=src[:, 0:1024]), 1)
    on("vector", f)

    def f(e):
        for ts in range(16):
            e.wait_ge(s_vev.h, ts + 1)
            stg = KHV[:, (ts % 2) * T:(ts % 2) * T + 1024]
            s_vstp[ts % 2].inc(e.dma_start(
                out=v_src.rearrange("(s p) f -> s p f", p=128)[ts], in_=stg), 16)
    on("sync", f)
    barrier()
    ckpt("p3")

    s_p5p = [sem("s_p5a"), sem("s_p5b")]
    s_p5m = sem("s_p5m")
    s_p5v = sem("s_p5v")
    s_p5sp = [sem("s_p5s0"), sem("s_p5s1")]

    def f(e):
        e.wait_ge(s_qkp[0].h, s_qkp[0].n)
        e.wait_ge(s_qkp[1].h, s_qkp[1].n)
        for fc in range(16):
            if fc >= 2:
                e.wait_ge(s_p5v.h, 2 * (fc - 1))
                e.wait_ge(s_p5sp[fc % 2].h, 16 * (fc // 2))
            s_p5p[fc % 2].inc(e.dma_start(
                out=EB[:, (fc % 2) * T:(fc % 2) * T + T],
                in_=qkraw_sp.rearrange("(f p) t -> f p t", p=128)[fc]), 16)
    on("sync", f)

    def f(e):
        for fc in range(16):
            c = fc % 8
            if fc >= 2:
                e.wait_ge(s_p5v.h, 2 * (fc - 1))
            sel = ESEL[0:16, :]
            rows_ = RW2[0:16, :] if fc < 8 else QP[0:16, :]
            dst = PA if fc % 2 == 0 else PB
            last = None
            for nt in range(4):
                last = e.matmul(dst[:, nt * 512:(nt + 1) * 512],
                                sel[:, c * 128:(c + 1) * 128],
                                rows_[:, nt * 512:(nt + 1) * 512],
                                start=True, stop=True)
            s_p5m.inc(last, 1)
    on("tensor", f)

    def f(e):
        for fc in range(16):
            e.wait_ge(s_p5m.h, fc + 1)
            e.wait_ge(s_p5p[fc % 2].h, 16 * (fc // 2 + 1))
            if fc >= 2:
                e.wait_ge(s_p5sp[fc % 2].h, 16 * (fc // 2))
            src = PA if fc % 2 == 0 else PB
            stg = EB[:, (fc % 2) * T:(fc % 2) * T + T]
            wcol = SM[:, 0:1] if fc < 8 else SM[:, 1:2]
            e.tensor_tensor(out=stg, in0=stg, in1=src[:, 0:T], op=ALU.mult)
            s_p5v.inc(e.tensor_scalar(out=stg, in0=stg, scalar1=wcol,
                                      scalar2=None, op0=ALU.mult), 2)
    on("vector", f)
    p5v0 = s_p5v.n - 32

    def f(e):
        for fc in range(16):
            e.wait_ge(s_p5v.h, p5v0 + 2 * (fc + 1))
            stg = EB[:, (fc % 2) * T:(fc % 2) * T + T]
            if fc < 8:
                dst = qt_sp.rearrange("(f p) t -> f p t", p=128)[fc]
            else:
                dst = kt_src.rearrange("(f p) t -> f p t", p=128)[fc - 8]
            s_p5sp[fc % 2].inc(e.dma_start(out=dst, in_=stg), 16)
    on("gpsimd", f)
    barrier()
    ckpt("p5")

    def f(e):
        e.wait_ge(s_p5sp[0].h, s_p5sp[0].n)
        e.wait_ge(s_p5sp[1].h, s_p5sp[1].n)
        e.wait_ge(s_vstp[0].h, s_vstp[0].n)
        e.wait_ge(s_vstp[1].h, s_vstp[1].n)
        e.wait_ge(s_st.h, s_st.n)
        s_cc.inc(e.collective_compute("AllGather", ALU.bypass,
                                      replica_groups=PAIRS,
                                      ins=[kt_src], outs=[kt_dst]), 1)
        s_cc.inc(e.collective_compute("AllGather", ALU.bypass,
                                      replica_groups=PAIRS,
                                      ins=[v_src], outs=[v_dst]), 1)
        s_cc.inc(e.collective_compute("AllGather", ALU.bypass,
                                      replica_groups=PAIRS,
                                      ins=[kmax_src], outs=[kmax_dst]), 1)
    on("gpsimd", f)
    cc_l0 = s_cc.n

    s_gw = sem("s_gw")
    s_gm = sem("s_gm")
    s_gv = sem("s_gv")
    s_gg = sem("s_gg")
    s_g2 = sem("s_g2")

    def f(e):
        s_gw.inc(e.dma_start(out=SMB[:, 540:604].rearrange("p (c f) -> p c f", c=NC),
                             in_=rearr_w(gateWT)), 16)
    on("gpsimd", f)
    g_w = s_gw.n

    def f(e):
        e.wait_ge(s_gw.h, g_w)
        last = None
        for ts in range(16):
            for c in range(NC):
                last = e.matmul(PA[:, ts * 8:(ts + 1) * 8],
                                XN[:, c * T + ts * 128:c * T + (ts + 1) * 128],
                                SMB[:, 540 + c * 8:540 + (c + 1) * 8],
                                start=(c == 0), stop=(c == NC - 1))
        s_gm.inc(last, 1)
    on("tensor", f)
    g_mm = s_gm.n

    def f(e):
        for ts in range(16):
            e.wait_ge(s_gm.h, g_mm)
            if ts > 0:
                e.wait_ge(s_g2.h, 2 * ts)
            tile = PA[:, ts * 8:(ts + 1) * 8]
            e.reduce_max(SM[:, 56:57], tile, axis=AX.X)
            s_gv.inc(e.tensor_scalar(out=CH2[:, ts * 8:(ts + 1) * 8], in0=tile,
                                     scalar1=SM[:, 56:57], scalar2=None,
                                     op0=ALU.subtract), 1)
            e.wait_ge(s_g2.h, 2 * ts + 2)
            tile2 = CHK[:, ts * 8:(ts + 1) * 8]
            e.tensor_scalar(out=tile2, in0=tile2, scalar1=SM[:, 57:58],
                                   scalar2=None, op0=ALU.mult)
            e.tensor_scalar(out=CH2[:, 256 + ts * 8:264 + ts * 8],
                                   in0=tile2, scalar1=0.3, scalar2=None,
                                   op0=ALU.is_gt)
            e.reduce_sum(SM[:, 58:59], CH2[:, 256 + ts * 8:264 + ts * 8],
                                axis=AX.X)
            e.tensor_tensor(out=SM[:, 80 + ts:81 + ts], in0=SM[:, 58:59],
                                   in1=tile2[:, 1:2], op=ALU.mult)
            e.tensor_scalar(out=SM[:, 58:59], in0=SM[:, 58:59],
                                   scalar1=-1.0, scalar2=8.0, op0=ALU.mult,
                                   op1=ALU.add)
            s_gv.inc(e.tensor_tensor(out=SM[:, 64 + ts:65 + ts], in0=SM[:, 58:59],
                                     in1=tile2[:, 0:1], op=ALU.mult), 1)
    on("vector", f)

    def f(e):
        for ts in range(16):
            e.wait_ge(s_gv.h, 2 * ts + 1)
            act1 = e.activation(out=CHK[:, ts * 8:(ts + 1) * 8],
                                in_=CH2[:, ts * 8:(ts + 1) * 8], func=AF.Exp,
                                accum_out=SM[:, 57:58])
            s_g2.inc(act1, 1)
            act1
            e.activation(out=SM[:, 57:58], in_=SM[:, 57:58], func=AF.Ln)
            s_g2.inc(e.activation(out=SM[:, 57:58], in_=SM[:, 57:58], func=AF.Exp,
                                  scale=-1.0), 1)
    on("scalar", f)
    g_done = s_gv.n

    def f(e):
        e.wait_ge(s_gv.h, g_done)
        last = e.transpose(PB[0:16, 0:128], SM[:, 64:80], IDF[:])
        last = e.transpose(PB[0:16, 128:256], SM[:, 80:96], IDF[:])
        s_gm.inc(last, 1)
    on("tensor", f)
    g_tp = s_gm.n

    def f(e):
        e.wait_ge(s_gm.h, g_tp)
        s_gv.inc(e.tensor_copy(out=QP[0:16, 0:256], in_=PB[0:16, 0:256]), 1)
    on("vector", f)
    g_cp = s_gv.n

    def f(e):
        e.wait_ge(s_gv.h, g_cp)
        s_gg.inc(e.dma_start(out=RW2[0:1, 0:T], in_=QP[0:16, 0:128]), 16)
        s_gg.inc(e.dma_start(out=RW2[32:33, 0:T], in_=QP[0:16, 128:256]), 16)
    on("gpsimd", f)
    g_fl = s_gg.n

    def f(e):
        e.wait_ge(s_gg.h, g_fl)
        last = None
        for nt in range(4):
            e.matmul(PA[:, nt * 512:(nt + 1) * 512], ONE[0:1, :],
                     RW2[0:1, nt * 512:(nt + 1) * 512], start=True, stop=True)
            last = e.matmul(PB[:, nt * 512:(nt + 1) * 512], ONE[32:33, :],
                            RW2[32:33, nt * 512:(nt + 1) * 512], start=True,
                            stop=True)
        s_gm.inc(last, 1)
    on("tensor", f)
    g_r0 = s_gm.n

    def f(e):
        e.wait_ge(s_gm.h, g_r0)
        e.tensor_copy(out=QP[:, 0:T], in_=PA[:, 0:T])
        s_gv.inc(e.tensor_copy(out=QP[:, T:2 * T], in_=PB[:, 0:T]), 1)
    on("vector", f)
    barrier()
    ckpt("gating")
    dbg("srep0", QP[:, 0:T], [128, T], BF16)

    s_ex = sem("s_ex")
    s_exwp = [sem("s_exw0"), sem("s_exw1")]
    s_exv = sem("s_exv")

    def f(e):
        for hc in range(NC):
            for which in range(2):
                idx = 2 * hc + which
                if idx >= 2:
                    e.wait_ge(s_ex.h, idx - 1)
                src = e0T if which == 0 else e1T
                s_exwp[which].inc(e.dma_start(
                    out=WB1[:, which * 1024:which * 1024 + 1024]
                        .rearrange("p (c f) -> p c f", c=NC),
                    in_=rearr_w(src)[:, :, hc * 128:(hc + 1) * 128]), 16)
    on("gpsimd", f)

    def f(e):
        for hc in range(NC):
            for which in range(2):
                idx = 2 * hc + which
                e.wait_ge(s_exwp[which].h, 16 * (hc + 1))
                if hc >= 1 and which == 0:
                    e.wait_ge(s_exv.h, 3 * hc)
                wv = WB1[:, which * 1024:which * 1024 + 1024] \
                    .rearrange("p (c f) -> p c f", c=NC)
                dst = PA if which == 0 else PB
                last = None
                for nt in range(4):
                    for c in range(NC):
                        last = e.matmul(dst[:, nt * 512:(nt + 1) * 512], wv[:, c, :],
                                        XN[:, c * T + nt * 512:c * T + (nt + 1) * 512],
                                        start=(c == 0), stop=(c == NC - 1))
                s_ex.inc(last, 1)
    on("tensor", f)
    ex_mm0 = s_ex.n - 16

    def f(e):
        for hc in range(NC):
            e.wait_ge(s_ex.h, ex_mm0 + 2 * hc + 2)
            e.tensor_tensor(out=CHK[:, 0:T], in0=QP[:, 0:T], in1=PA[:, 0:T],
                            op=ALU.mult)
            e.tensor_tensor(out=VO[:, 0:4096].bitcast(F32)[:, 0:T],
                                   in0=QP[:, T:2 * T], in1=PB[:, 0:T],
                                   op=ALU.mult)
            s_exv.inc(e.tensor_add(out=HA[:, hc * T:(hc + 1) * T], in0=CHK[:, 0:T],
                                   in1=VO[:, 0:4096].bitcast(F32)[:, 0:T]), 3)
    on("vector", f)
    barrier()
    ckpt("moef")
    dbg("moeF", HA[:, 0:T], [128, T], BF16)

    s_mop = [sem("s_mo0"), sem("s_mo1")]

    def mo_evac(s_m, s_e):
        def f(e):
            for dc in range(NC):
                e.wait_ge(s_m.h, dc + 1)
                if dc >= 2:
                    e.wait_ge(s_mop[dc % 2].h, 16 * (dc // 2))
                src = PA if dc % 2 == 0 else PB
                s_e.inc(e.tensor_copy(out=EB[:, (dc % 2) * T:(dc % 2) * T + T],
                                      in_=src[:, 0:T]), 1)
        on("vector", f)
        e0 = s_e.n - NC

        def f(e):
            for dc in range(NC):
                e.wait_ge(s_e.h, e0 + dc + 1)
                s_mop[dc % 2].inc(e.dma_start(
                    out=moa_sp.rearrange("(c p) t -> c p t", p=128)[dc],
                    in_=EB[:, (dc % 2) * T:(dc % 2) * T + T]), 16)
        on("sync", f)

    stream_gemm("mow", NC, lambda dc: rearr_w(outWT)[:, :, dc * 128:(dc + 1) * 128],
                lambda c, nt: HA[:, c * T + nt * 512:c * T + (nt + 1) * 512],
                mo_evac)
    barrier()
    ckpt("moe")

    s_b = sem("s_b")
    s_bv = sem("s_bv")
    s_bs = sem("s_bs")
    s_bm = sem("s_bm")

    def f(e):
        e.wait_ge(s_cc.h, cc_l0)
        s_b.inc(e.dma_start(out=row(R_TMP2, 0, 16), in_=kmax_dst[0]), 16)
        s_b.inc(e.dma_start(out=row(R_TMP2, 16, 32), in_=kmax_dst[1]), 16)
    on("sync", f)
    b_ld = s_b.n

    def f(e):
        e.wait_ge(s_b.h, b_ld)
        e.tensor_tensor(out=row(R_TMP2, 0, 16), in0=row(R_TMP2, 0, 16),
                               in1=row(R_TMP2, 16, 32), op=ALU.max)
        s_bv.inc(e.tensor_tensor(out=row(R_BND, 0, 16), in0=row(R_QM, 0, 16),
                                 in1=row(R_TMP2, 0, 16), op=ALU.mult), 1)
    on("vector", f)
    b_v = s_bv.n

    def f(e):
        e.wait_ge(s_bv.h, b_v)
        s_bs.inc(e.activation(out=row(R_BND, 0, 16), in_=row(R_BND, 0, 16),
                              func=AF.Sqrt), 1)
    on("scalar", f)
    b_s = s_bs.n

    def f(e):
        e.wait_ge(s_bs.h, b_s)
        e.tensor_tensor(out=row(R_TMP2, 0, 8), in0=rows(R_BND, 0, 16, 2),
                               in1=rows(R_BND, 1, 16, 2), op=ALU.max)
        e.tensor_scalar(out=row(R_TMP2, 0, 8), in0=row(R_TMP2, 0, 8),
                               scalar1=-1.0, scalar2=60.0, op0=ALU.mult,
                               op1=ALU.add)
        s_bv.inc(e.tensor_copy(out=RW2[0:1, 0:8], in_=row(R_TMP2, 0, 8)), 1)
    on("vector", f)
    b_v2 = s_bv.n

    def f(e):
        e.wait_ge(s_bv.h, b_v2)
        last = None
        for pr in range(8):
            last = e.matmul(PA[:, 1536 + pr:1537 + pr], ONE[0:1, :],
                            RW2[0:1, pr:pr + 1], start=True, stop=True)
        s_bm.inc(last, 1)
    on("tensor", f)
    b_m = s_bm.n

    def f(e):
        e.wait_ge(s_bm.h, b_m)
        s_bv.inc(e.tensor_copy(out=SM[:, 48:56], in_=PA[:, 1536:1544]), 1)
    on("vector", f)
    barrier()
    ckpt("bnd")
    dbg("bnd", SM[:, 48:56], [128, 8], F32)

    s_apf = sem("s_apf")
    s_amm = sem("s_amm")
    s_aex = sem("s_aex")
    s_aevs = sem("s_aevs")
    s_aevv = sem("s_aevv")
    s_aevm = sem("s_aevm")

    s_vo = sem("s_vo")

    def f(e):
        e.memset(VO[:], 0.0)
        last = None
        for kt in range(32):
            last = e.memset(VO[:, kt * 65 + 64:kt * 65 + 65], 1.0)
            last = e.memset(VO[:, 2112 + kt * 128:2112 + kt * 128 + 1], 1.0)
        s_vo.inc(last, 1)
    on("gpsimd", f)

    KHVv = KHV[:, 0:4096]

    for pr in range(8):
        h0, h1 = 2 * pr, 2 * pr + 1

        def f(e, pr=pr, h0=h0, h1=h1):
            e.wait_ge(s_cc.h, cc_l0)
            e.wait_ge(s_vo.h, 1)
            e.wait_ge(s_p5sp[0].h, s_p5sp[0].n)
            e.wait_ge(s_p5sp[1].h, s_p5sp[1].n)
            if pr >= 1:
                e.wait_ge(s_aevv.h, 4 * pr)
                e.wait_ge(s_amm.h, 66 * pr)
            s_apf.inc(e.dma_start(
                out=KHVv[0:64, :].rearrange("p (g t) -> p g t", g=2),
                in_=kt_dst[:, h0 * 64:(h0 + 1) * 64, :].transpose([1, 0, 2])), 16)
            s_apf.inc(e.dma_start(
                out=KHVv[64:128, :].rearrange("p (g t) -> p g t", g=2),
                in_=kt_dst[:, h1 * 64:(h1 + 1) * 64, :].transpose([1, 0, 2])), 16)
            s_apf.inc(e.dma_start(
                out=QP[:, 0:T],
                in_=qt_sp.rearrange("(x p) t -> x p t", p=128)[pr]), 16)
            s_apf.inc(e.dma_start(
                out=VO[:, 0:2080].rearrange("p (kt f) -> p kt f", f=65)[:, :, 0:64],
                in_=v_dst.rearrange("g (s p) f -> p (g s) f", p=128)
                    [:, :, h0 * 64:(h0 + 1) * 64]), 16)
            e.wait_ge(s_apf.h, s_apf.n)
            s_apf.inc(e.dma_start(
                out=VO[:, 2112:6208].rearrange("p (kt f) -> p kt f", f=128)
                    [:, :, 64:128],
                in_=v_dst.rearrange("g (s p) f -> p (g s) f", p=128)
                    [:, :, h1 * 64:(h1 + 1) * 64]), 16)
        on("sync", f)
        pf_now = s_apf.n

        for qw in range(2):
            ex0 = s_aex.n
            mm0 = s_amm.n
            aev0 = s_aevv.n

            def f(e, pr=pr, qw=qw, ex0=ex0, pf_now=pf_now, aev0=aev0):
                e.wait_ge(s_apf.h, pf_now)
                e.wait_ge(s_aevv.h, aev0)
                for kt in range(32):
                    if kt >= 1:
                        e.wait_ge(s_aex.h, ex0 + kt)
                    qs0 = QP[0:64, qw * 1024:(qw + 1) * 1024]
                    qs1 = QP[64:128, qw * 1024:(qw + 1) * 1024]
                    last = None
                    for nt in range(2):
                        e.matmul(PA[:, nt * 512:(nt + 1) * 512],
                                 KHVv[0:64, kt * 128:(kt + 1) * 128],
                                 qs0[:, nt * 512:(nt + 1) * 512],
                                 start=True, stop=True, tile_position=(0, 0))
                        last = e.matmul(PA[:, 1024 + nt * 512:1024 + (nt + 1) * 512],
                                        KHVv[64:128, kt * 128:(kt + 1) * 128],
                                        qs1[:, nt * 512:(nt + 1) * 512],
                                        start=True, stop=True,
                                        tile_position=(64, 0))
                    s_amm.inc(last, 1)
                    if kt >= 1:
                        ek = kt - 1
                        ebuf = EB[:, (ek % 2) * T:(ek % 2) * T + T]
                        for nt in range(2):
                            e.matmul(PB[0:65, nt * 512:(nt + 1) * 512],
                                     VO[:, ek * 65:ek * 65 + 65],
                                     ebuf[:, nt * 512:(nt + 1) * 512],
                                     start=(ek == 0), stop=False)
                            e.matmul(PB[:, 1024 + nt * 512:1024 + (nt + 1) * 512],
                                     VO[:, 2112 + ek * 128:2112 + (ek + 1) * 128],
                                     ebuf[:, 1024 + nt * 512:1024 + (nt + 1) * 512],
                                     start=(ek == 0), stop=False)
                e.wait_ge(s_aex.h, ex0 + 32)
                ek = 31
                ebuf = EB[:, (ek % 2) * T:(ek % 2) * T + T]
                last = None
                for nt in range(2):
                    e.matmul(PB[0:65, nt * 512:(nt + 1) * 512],
                             VO[:, ek * 65:ek * 65 + 65],
                             ebuf[:, nt * 512:(nt + 1) * 512],
                             start=False, stop=True)
                    last = e.matmul(PB[:, 1024 + nt * 512:1024 + (nt + 1) * 512],
                                    VO[:, 2112 + ek * 128:2112 + (ek + 1) * 128],
                                    ebuf[:, 1024 + nt * 512:1024 + (nt + 1) * 512],
                                    start=False, stop=True)
                s_amm.inc(last, 1)
            on("tensor", f)
            mm_after = s_amm.n

            def f(e, pr=pr, qw=qw, mm0=mm0):
                if pr == 0 and qw == 0:
                    e.wait_ge(s_mop[0].h, s_mop[0].n)
                    e.wait_ge(s_mop[1].h, s_mop[1].n)
                for kt in range(32):
                    e.wait_ge(s_amm.h, mm0 + kt + 1)
                    ebuf = EB[:, (kt % 2) * T:(kt % 2) * T + T]
                    s_aex.inc(e.activation(out=ebuf, in_=PA[:, 0:2048], func=AF.Exp,
                                           bias=SM[:, 48 + pr:49 + pr], scale=1.0), 1)
            on("scalar", f)

            def f(e, mm_after=mm_after):
                e.wait_ge(s_amm.h, mm_after)
                e.activation(out=CHK[64:65, 0:1024], in_=PB[64:65, 0:1024],
                                    func=AF.Sqrt)
                e.activation(out=CH2[64:65, 0:1024], in_=CHK[64:65, 0:1024],
                             func=AF.Ln)
                e.activation(out=CHK[64:65, 1024:2048], in_=CH2[64:65, 0:1024],
                             func=AF.Exp, scale=-1.0)
                e.activation(out=CH2[0:1, 0:1024], in_=PB[0:1, 1024:2048],
                                    func=AF.Sqrt)
                e.activation(out=CHK[0:1, 0:1024], in_=CH2[0:1, 0:1024],
                             func=AF.Ln)
                s_aevs.inc(e.activation(out=CHK[0:1, 0:1024], in_=CHK[0:1, 0:1024],
                                        func=AF.Exp, scale=-1.0), 1)
            on("scalar", f)
            a1 = s_aevs.n

            def f(e, a1=a1):
                e.wait_ge(s_aevs.h, a1)
                e.tensor_tensor(out=RW2[64:65, 0:1024],
                                in0=CHK[64:65, 1024:2048],
                                in1=CHK[64:65, 1024:2048], op=ALU.mult)
                s_aevv.inc(e.tensor_tensor(out=RW2[0:1, 1024:2048],
                                           in0=CHK[0:1, 0:1024],
                                           in1=CHK[0:1, 0:1024], op=ALU.mult), 1)
            on("vector", f)
            a2 = s_aevv.n

            def f(e, a2=a2):
                e.wait_ge(s_aevv.h, a2)
                e.matmul(PA[:, 0:512], ONE[64:65, :], RW2[64:65, 0:512],
                         start=True, stop=True)
                e.matmul(PA[:, 512:1024], ONE[64:65, :], RW2[64:65, 512:1024],
                         start=True, stop=True)
                e.matmul(PA[:, 1024:1536], ONE[0:1, :], RW2[0:1, 1024:1536],
                         start=True, stop=True)
                last = e.matmul(PA[:, 1536:2048], ONE[0:1, :], RW2[0:1, 1536:2048],
                                start=True, stop=True)
                s_aevm.inc(last, 1)
            on("tensor", f)
            a3 = s_aevm.n

            dbg_here = pr == 0 and qw == 0 and ("avps" in DBG)
            if dbg_here:
                d_pb = b.dout("dbg_avps", [128, 2048], F32)
                d_e = b.dout("dbg_e31", [128, 2048], BF16)
                d_khv = b.dout("dbg_khv", [128, 4096], BF16)
                d_qp = b.dout("dbg_qp", [128, 2048], BF16)
                d_vo = b.dout("dbg_vo", [128, 6208], BF16)
                d_inv = b.dout("dbg_inv", [128, 2048], BF16)
                s_dbgv = sem("s_dbgv")

                def fdbg(e):
                    e.wait_ge(s_dbgv.h, 1)
                    s_dbg.inc(e.dma_start(out=d_pb, in_=CHK[:, 0:2048]), 16)
                    s_dbg.inc(e.dma_start(out=d_e, in_=EB[:, T:2 * T]), 16)
                    s_dbg.inc(e.dma_start(out=d_khv, in_=KHVv), 16)
                    s_dbg.inc(e.dma_start(out=d_qp, in_=QP[:, 0:T]), 16)
                    s_dbg.inc(e.dma_start(out=d_vo, in_=VO[:]), 16)
                    s_dbg.inc(e.dma_start(out=d_inv, in_=RW2[:, 0:2048]), 16)
                on("sync", fdbg)

            def f(e, pr=pr, qw=qw, a3=a3, dbg_here=dbg_here):
                e.wait_ge(s_aevm.h, a3)
                if dbg_here:
                    s_dbgv.inc(e.tensor_copy(out=CHK[:, 0:2048],
                                             in_=PB[:, 0:2048]), 1)
                    e.wait_ge(s_dbg.h, 96)
                z0 = HB[0:64, pr * T + qw * 1024:pr * T + (qw + 1) * 1024]
                z1 = HB[64:128, pr * T + qw * 1024:pr * T + (qw + 1) * 1024]
                e.tensor_copy(out=z0, in_=PB[0:64, 0:1024])
                e.tensor_tensor(out=z0, in0=z0, in1=PA[0:64, 0:1024], op=ALU.mult)
                e.tensor_copy(out=z1, in_=PB[64:128, 1024:2048])
                s_aevv.inc(e.tensor_tensor(out=z1, in0=z1, in1=PA[64:128, 1024:2048],
                                           op=ALU.mult), 2)
            on("vector", f)

    barrier()
    ckpt("attn")
    dbg("zt", HB[:, 0:T], [128, T], BF16)

    s_wb = sem("s_wb")

    def f(e):
        s_wb.inc(e.tensor_copy(out=SMB[:, 516:524], in_=SM[:, 24:32]), 1)
    on("vector", f)
    wb_v = s_wb.n

    def wobv_evac(s_m, s_e):
        def f(e):
            for dc in range(NC):
                e.wait_ge(s_m.h, dc + 1)
                src = PA if dc % 2 == 0 else PB
                s_e.inc(e.tensor_copy(out=SM[:, 40 + dc:41 + dc], in_=src[:, 0:1]), 1)
        on("vector", f)

    _, s_wbe = stream_gemm(
        "wobv", NC, lambda dc: rearr_w(woT)[:, :, dc * 128:(dc + 1) * 128],
        lambda c, nt: SMB[:, 516 + c:517 + c], wobv_evac,
        nt_count=1, nt_size=1,
        wait_emit=lambda e: e.wait_ge(s_wb.h, wb_v))

    def f(e):
        e.wait_ge(s_wbe.h, NC)
        s_wb.inc(e.tensor_add(out=SM[:, 40:48], in0=SM[:, 40:48],
                              in1=SM[:, 32:40]), 1)
    on("vector", f)
    barrier()

    s_rxp = [sem("s_rx0"), sem("s_rx1")]
    s_rmp = [sem("s_rm0"), sem("s_rm1")]
    s_rv = sem("s_rv")
    s_rsp = [sem("s_rs0"), sem("s_rs1")]

    def att_evac(s_m, s_e):
        def f(e):
            e.wait_ge(s_mop[0].h, s_mop[0].n)
            e.wait_ge(s_mop[1].h, s_mop[1].n)
            for dc in range(NC + 2):
                if 2 <= dc:
                    sdc = dc - 2
                    e.wait_ge(s_rv.h, sdc + 1)
                    xbuf = CHK[:, 0:T] if sdc % 2 == 0 else VO[:, 0:4096].bitcast(F32)[:, 0:T]
                    s_rsp[sdc % 2].inc(e.dma_start(
                        out=x1sp.rearrange("(c p) t -> c p t", p=128)[sdc],
                        in_=xbuf), 16)
                if dc < NC:
                    if dc >= 2:
                        e.wait_ge(s_rsp[dc % 2].h, 16 * (dc // 2))
                    xbuf = CHK[:, 0:T] if dc % 2 == 0 else VO[:, 0:4096].bitcast(F32)[:, 0:T]
                    s_rxp[dc % 2].inc(e.dma_start(
                        out=xbuf,
                        in_=x_fm.rearrange("(c p) t -> c p t", p=128)[dc]), 16)
                    s_rmp[dc % 2].inc(e.dma_start(
                        out=KHV[:, (dc % 2) * T:(dc % 2) * T + T],
                        in_=moa_sp.rearrange("(c p) t -> c p t", p=128)[dc]), 16)
        on("sync", f)

        def f(e):
            for dc in range(NC):
                e.wait_ge(s_m.h, dc + 1)
                if dc >= 1:
                    e.wait_ge(s_rv.h, dc)
                src = PA if dc % 2 == 0 else PB
                s_e.inc(e.activation(out=EB[:, 0:2 * T].bitcast(F32)[:, 0:T],
                                     in_=src[:, 0:T], func=AF.Identity,
                                     bias=SM[:, 40 + dc:41 + dc], scale=1.0), 1)
        on("scalar", f)
        sc0 = s_e.n - NC

        def f(e):
            for dc in range(NC):
                e.wait_ge(s_e.h, sc0 + dc + 1)
                e.wait_ge(s_rxp[dc % 2].h, 16 * (dc // 2 + 1))
                e.wait_ge(s_rmp[dc % 2].h, 16 * (dc // 2 + 1))
                if dc >= 2:
                    e.wait_ge(s_rsp[dc % 2].h, 16 * (dc // 2))
                xbuf = CHK[:, 0:T] if dc % 2 == 0 else VO[:, 0:4096].bitcast(F32)[:, 0:T]
                att = EB[:, 0:2 * T].bitcast(F32)[:, 0:T]
                e.tensor_add(out=att, in0=att, in1=xbuf)
                s_rv.inc(e.scalar_tensor_tensor(
                    out=xbuf, in0=KHV[:, (dc % 2) * T:(dc % 2) * T + T],
                    scalar=1.0, in1=att, op0=ALU.mult, op1=ALU.add), 1)
        on("vector", f)

    stream_gemm("att", NC, lambda dc: rearr_w(woT)[:, :, dc * 128:(dc + 1) * 128],
                lambda c, nt: HB[:, c * T + nt * 512:c * T + (nt + 1) * 512],
                att_evac)
    barrier()
    ckpt("wo")

    rsp_tot = (s_rsp[0].n, s_rsp[1].n)

    def f(e):
        e.wait_ge(s_rsp[0].h, rsp_tot[0])
        e.wait_ge(s_rsp[1].h, rsp_tot[1])
    on("sync", f)
    emit_ln(x1sp, "l1")
    barrier()
    ckpt("ln1")
    dbg("xnl1", XN[:, 0:T], [128, T], BF16)

    s_fwp = [sem("s_fw0"), sem("s_fw1")]
    s_fdp = [sem("s_fd0"), sem("s_fd1")]
    s_fam = sem("s_fam")
    s_fae = sem("s_fae")
    s_fav = sem("s_fav")
    s_fbm = sem("s_fbm")
    s_fbe = sem("s_fbe")

    wgv = wgTi.rearrange("(c p) h -> p c h", p=128)
    wuv = wuTi.rearrange("(c p) h -> p c h", p=128)
    wdv = wdTi.rearrange("(k p) d -> k p d", p=128)

    def ffn_wslot(idx):
        if idx % 2 == 0:
            return WB3[:, 0:1024], WB3[:, 1024:2048]
        return WB3[:, 2048:3072], WB1[:, 0:1024]

    def f(e):
        for tg in range(4):
            for hs in range(24):
                idx = tg * 24 + hs
                if idx >= 2:
                    e.wait_ge(s_fam.h, idx - 1)
                wg_s, wu_s = ffn_wslot(idx)
                s_fwp[idx % 2].inc(e.dma_start(
                    out=wg_s.rearrange("p (c f) -> p c f", c=8),
                    in_=wgv[:, :, hs * 128:(hs + 1) * 128]), 16)
                s_fwp[idx % 2].inc(e.dma_start(
                    out=wu_s.rearrange("p (c f) -> p c f", c=8),
                    in_=wuv[:, :, hs * 128:(hs + 1) * 128]), 16)
            for hs in range(24):
                e.wait_ge(s_fam.h, tg * 24 + 24)
                if tg > 0 or hs >= 2:
                    e.wait_ge(s_fbm.h, tg * 24 + hs - 1)
                wd_s = WB1[:, 1024:2048] if hs % 2 == 0 else WB3[:, 0:1024]
                s_fdp[hs % 2].inc(e.dma_start(
                    out=wd_s, in_=wdv[hs]), 16)
    on("gpsimd", f)

    def f(e):
        for tg in range(4):
            for hs in range(24):
                idx = tg * 24 + hs
                e.wait_ge(s_fwp[idx % 2].h, 32 * (idx // 2 + 1))
                if idx >= 2:
                    e.wait_ge(s_fav.h, idx - 1)
                if hs == 0 and tg >= 1:
                    e.wait_ge(s_fbe.h, tg)
                wg_s, wu_s = ffn_wslot(idx)
                wgt = wg_s.rearrange("p (c f) -> p c f", c=8)
                wut = wu_s.rearrange("p (c f) -> p c f", c=8)
                g_ps = PA[:, (idx % 2) * 1024:(idx % 2) * 1024 + 512]
                u_ps = PA[:, (idx % 2) * 1024 + 512:(idx % 2) * 1024 + 1024]
                last = None
                for c in range(NC):
                    rhs = XN[:, c * T + tg * 512:c * T + (tg + 1) * 512]
                    e.matmul(g_ps, wgt[:, c, :], rhs,
                             start=(c == 0), stop=(c == NC - 1))
                    last = e.matmul(u_ps, wut[:, c, :], rhs,
                                    start=(c == 0), stop=(c == NC - 1))
                s_fam.inc(last, 1)
            for hs in range(24):
                e.wait_ge(s_fdp[hs % 2].h, 16 * (tg * 12 + hs // 2 + 1))
                e.wait_ge(s_fav.h, tg * 24 + hs + 1)
                if hs == 0:
                    e.wait_ge(s_fav.h, tg * 24 + 24)
                    if tg >= 1:
                        e.wait_ge(s_fbe.h, tg)
                wd_s = WB1[:, 1024:2048] if hs % 2 == 0 else WB3[:, 0:1024]
                last = None
                for fs in range(NC):
                    dst = (PA if fs < 4 else PB)[:, (fs % 4) * 512:
                                                 (fs % 4) * 512 + 512]
                    last = e.matmul(dst, wd_s[:, fs * 128:(fs + 1) * 128],
                                    HA[:, hs * 512:(hs + 1) * 512],
                                    start=(hs == 0), stop=(hs == 23))
                s_fbm.inc(last, 1)
    on("tensor", f)

    def f(e):
        for tg in range(4):
            for hs in range(24):
                idx = tg * 24 + hs
                e.wait_ge(s_fam.h, idx + 1)
                if idx >= 2:
                    e.wait_ge(s_fav.h, idx - 1)
                g_ps = PA[:, (idx % 2) * 1024:(idx % 2) * 1024 + 512]
                s_fae.inc(e.activation(
                    out=EB[:, (idx % 2) * 512:(idx % 2) * 512 + 512],
                    in_=g_ps, func=AF.Silu, scale=1.0 / 127.0), 1)
    on("scalar", f)

    def f(e):
        for tg in range(4):
            for hs in range(24):
                idx = tg * 24 + hs
                e.wait_ge(s_fae.h, idx + 1)
                if tg >= 1 and hs == 0:
                    e.wait_ge(s_fbm.h, tg * 24)
                u_ps = PA[:, (idx % 2) * 1024 + 512:(idx % 2) * 1024 + 1024]
                s_fav.inc(e.tensor_tensor(
                    out=HA[:, hs * 512:(hs + 1) * 512],
                    in0=u_ps, in1=EB[:, (idx % 2) * 512:(idx % 2) * 512 + 512],
                    op=ALU.mult), 1)
            def g(e, tg=tg):
                e.wait_ge(s_fbm.h, tg * 24 + 24)
                last = None
                for fs in range(NC):
                    src = (PA if fs < 4 else PB)[:, (fs % 4) * 512:
                                                 (fs % 4) * 512 + 512]
                    last = e.tensor_scalar(
                        out=HB[:, fs * T + tg * 512:fs * T + (tg + 1) * 512],
                        in0=src, scalar1=1.0 / 16129.0, scalar2=None,
                        op0=ALU.mult)
                s_fbe.inc(last, 1)
            g(e)
    on("vector", f)
    barrier()
    ckpt("l1ffn")
    dbg("ffn", HB[:, 0:T], [128, T], BF16)

    s_p3g = b.reuse("s_fw0")
    s_p3s = b.reuse("s_fae")
    s_p3v = b.reuse("s_fav")
    s_p3m = b.reuse("s_fam")
    s_p3y = b.reuse("s_apf")
    s_bst = b.reuse("s_rx0")

    def f(e):
        s_p3y.inc(e.dma_start(out=SM[0:1, 61:62], in_=eab[0:1, :]), 16)
        s_p3y.inc(e.dma_start(out=SM[0:1, 62:63], in_=eab[1:2, :]), 16)
        s_p3y.inc(e.dma_start(out=SM[0:1, 63:64], in_=eab[2:3, :]), 16)
    on("sync", f)

    def f(e):
        s_p3g.inc(e.dma_start(
            out=SMB[:, 524:548].rearrange("p (c e) -> p c e", c=8),
            in_=eag.rearrange("(c p) e -> p c e", p=128)), 16)
        s_p3g.inc(e.memset(SMB[0:64, 548:549], 1.0 / 64.0), 1)
    on("gpsimd", f)
    p3c_g = s_p3g.n
    p3c_y = s_p3y.n

    nrm_m0 = s_p3m.n
    nrm_s0 = s_p3s.n

    def f(e):
        for c in range(NC):
            if c >= 2:
                e.wait_ge(s_p3m.h, nrm_m0 + c - 1)
            ebs = EB[:, (c % 2) * T:(c % 2) * T + T]
            s_p3s.inc(e.activation(out=ebs, in_=XN[:, c * T:(c + 1) * T],
                                   func=AF.Square), 1)
    on("scalar", f)

    def f(e):
        for c in range(NC):
            e.wait_ge(s_p3s.h, nrm_s0 + c + 1)
            ebs = EB[:, (c % 2) * T:(c % 2) * T + T]
            last = None
            for nt in range(4):
                last = e.matmul(PA[0:1, nt * 512:(nt + 1) * 512], ONE[:, 0:1],
                                ebs[:, nt * 512:(nt + 1) * 512],
                                start=(c == 0), stop=(c == NC - 1))
            s_p3m.inc(last, 1)
    on("tensor", f)
    nrm_m = s_p3m.n

    def f(e):
        e.wait_ge(s_p3m.h, nrm_m)
        s_p3s.inc(e.activation(out=CHK[0:1, 0:T], in_=PA[0:1, 0:T],
                               func=AF.Sqrt), 1)
    on("scalar", f)
    wt_s = s_p3s.n

    def f(e):
        e.wait_ge(s_p3s.h, wt_s)
        s_p3v.inc(e.tensor_scalar(out=CHK[0:1, 0:T], in0=CHK[0:1, 0:T],
                                  scalar1=1e-5, scalar2=None, op0=ALU.max), 1)
    on("vector", f)
    wt_v = s_p3v.n

    def f(e):
        e.wait_ge(s_p3v.h, wt_v)
        e.activation(out=CHK[0:1, 0:T], in_=CHK[0:1, 0:T], func=AF.Ln,
                     scale=64.0)
        s_p3s.inc(e.activation(out=CHK[0:1, 0:T], in_=CHK[0:1, 0:T],
                               func=AF.Exp, scale=-1.0), 1)
    on("scalar", f)
    wt_s2 = s_p3s.n

    def f(e):
        e.wait_ge(s_p3s.h, wt_s2)
        s_p3v.inc(e.tensor_copy(out=RW2[0:1, 0:T], in_=CHK[0:1, 0:T]), 1)
    on("vector", f)
    wt_v2 = s_p3v.n

    def f(e):
        e.wait_ge(s_p3v.h, wt_v2)
        last = None
        for nt in range(4):
            last = e.matmul(PB[:, nt * 512:(nt + 1) * 512], ONE[0:1, :],
                            RW2[0:1, nt * 512:(nt + 1) * 512],
                            start=True, stop=True)
        s_p3m.inc(last, 1)
    on("tensor", f)
    wtr_m = s_p3m.n

    def f(e):
        e.wait_ge(s_p3m.h, wtr_m)
        for c in range(NC):
            if c >= 2:
                e.wait_ge(s_p3s.h, wt_s2 + 32 * (c - 1))
            ebs = EB[:, (c % 2) * T:(c % 2) * T + T]
            s_p3v.inc(e.tensor_tensor(out=ebs, in0=XN[:, c * T:(c + 1) * T],
                                      in1=PB[:, 0:T], op=ALU.mult), 1)
    on("vector", f)
    xw_v = s_p3v.n - NC

    def f(e):
        e.wait_ge(s_p3g.h, p3c_g)
        for c in range(NC):
            e.wait_ge(s_p3v.h, xw_v + c + 1)
            ebs = EB[:, (c % 2) * T:(c % 2) * T + T]
            for j in range(32):
                s_p3s.inc(e.activation(out=SMB[:, 576:640],
                                       in_=ebs[:, j * 64:(j + 1) * 64],
                                       func=AF.Identity,
                                       accum_out=CH2[:, c * 32 + j:c * 32 + j + 1]), 1)
    on("scalar", f)
    kacc_s = s_p3s.n

    def f(e):
        e.wait_ge(s_p3s.h, kacc_s)
        last = None
        for ei in range(2):
            dst = PA if ei == 0 else PB
            for c in range(NC):
                col = 524 + c * 3 + ei
                for nt in range(4):
                    last = e.matmul(dst[0:1, nt * 512:(nt + 1) * 512],
                                    SMB[:, col:col + 1],
                                    XN[:, c * T + nt * 512:c * T + (nt + 1) * 512],
                                    start=(c == 0), stop=(c == NC - 1))
        s_p3m.inc(last, 1)
    on("tensor", f)
    ea_m = s_p3m.n

    def f(e):
        e.wait_ge(s_p3m.h, ea_m)
        e.wait_ge(s_p3y.h, p3c_y)
        e.activation(out=CHK[0:1, 0:T], in_=PA[0:1, 0:T], func=AF.Sigmoid,
                     bias=SM[0:1, 61:62])
        s_p3s.inc(e.activation(out=VO[:, 0:4096].bitcast(F32)[0:1, 0:T],
                               in_=PB[0:1, 0:T], func=AF.Sigmoid,
                               bias=SM[0:1, 62:63]), 1)
    on("scalar", f)
    sig_s = s_p3s.n

    def f(e):
        e.wait_ge(s_p3s.h, sig_s)
        for j in range(32):
            e.activation(out=SMB[:, 576:640][0:1, 0:64],
                         in_=CHK[0:1, j * 64:(j + 1) * 64],
                         func=AF.Identity, scale=0.2 / 64.0,
                         accum_out=ROWS[0:1, 1600 + j:1601 + j])
            s_p3s.inc(e.activation(out=SMB[:, 576:640][0:1, 0:64],
                                   in_=VO[:, 0:4096].bitcast(F32)[0:1, j * 64:(j + 1) * 64],
                                   func=AF.Identity, scale=0.5 / 64.0,
                                   accum_out=ROWS[0:1, 1632 + j:1633 + j]), 1)
    on("scalar", f)
    eac_s = s_p3s.n

    def f(e):
        e.wait_ge(s_p3s.h, eac_s)
        s_p3y.inc(e.dma_start(out=a2f_src[0:1, :], in_=ROWS[0:1, 1600:1632]), 16)
        s_p3y.inc(e.dma_start(out=a2f_src[1:2, :], in_=ROWS[0:1, 1632:1664]), 16)
    on("sync", f)
    a2f_y = s_p3y.n

    bst0 = s_bst.n

    def base_evac(s_m, s_e):
        def f(e):
            for fc in range(NC):
                e.wait_ge(s_m.h, s_m.base + fc + 1)
                src = PA if fc % 2 == 0 else PB
                s_e.inc(e.tensor_copy(out=HA[:, fc * T:(fc + 1) * T],
                                      in_=src[:, 0:T]), 1)
        on("vector", f)

        def g(e):
            for fc in range(NC):
                e.wait_ge(s_e.h, s_e.base + fc + 1)
                s_bst.inc(e.dma_start(
                    out=base_srcT.rearrange("(c p) t -> c p t", p=128)[fc],
                    in_=HA[:, fc * T:(fc + 1) * T]), 16)
        on("sync", g)

    _, s_bev = stream_gemm(
        "base", NC, lambda fc: rearr_w(m0T)[:, :, fc * 128:(fc + 1) * 128],
        xn_rhs, base_evac,
        wait_emit=lambda e: e.wait_ge(s_p3m.h, ea_m),
        sems=(b.reuse("s_fw1"), b.reuse("s_fd1"), b.reuse("s_fbm"),
              b.reuse("s_fbe")))
    bev0 = s_bev.base

    def f(e):
        for fc in range(NC):
            e.wait_ge(s_bev.h, bev0 + fc + 1)
            for j in range(32):
                s_p3s.inc(e.activation(out=SMB[:, 576:640],
                                       in_=HA[:, fc * T + j * 64:fc * T + (j + 1) * 64],
                                       func=AF.Identity, scale=1.0 / 64.0,
                                       accum_out=CH2[:, 256 + fc * 32 + j:
                                                     257 + fc * 32 + j]), 1)
    on("scalar", f)
    bm_s = s_p3s.n

    def f(e):
        e.wait_ge(s_p3s.h, bm_s)
        s_p3g.inc(e.dma_start(out=a2b_src[:, 0:512], in_=CH2[:, 0:512]), 16)
    on("gpsimd", f)
    a2b_g = s_p3g.n

    def f(e):
        e.wait_ge(s_p3g.h, a2b_g)
        e.wait_ge(s_p3y.h, a2f_y)
        e.wait_ge(s_bst.h, bst0 + 16 * NC)
        e.collective_compute(
            "AllGather", ALU.bypass, replica_groups=PAIRS,
            ins=[a2b_src], outs=[a2b_dst]).then_inc(s_cc.h, 1)
        s_cc.n += 1
        e.collective_compute(
            "AllGather", ALU.bypass, replica_groups=PAIRS,
            ins=[a2f_src], outs=[a2f_dst]).then_inc(s_cc.h, 1)
        s_cc.n += 1
        e.collective_compute(
            "AllGather", ALU.bypass, replica_groups=PAIRS,
            ins=[base_srcT], outs=[base_dstT]).then_inc(s_cc.h, 1)
        s_cc.n += 1
    on("gpsimd", f)
    cc_a2 = s_cc.n

    def f(e):
        e.wait_ge(s_cc.h, cc_a2)
        for r in range(2):
            s_p3y.inc(e.dma_start(
                out=KSB[:, :].rearrange("p (c r j) -> p c r j", r=2, j=32)[:, :, r],
                in_=a2b_dst[r][:, 0:256].rearrange("p (c j) -> p c j", c=8)), 16)
            s_p3y.inc(e.dma_start(
                out=BMC[:, :].rearrange("p (c r j) -> p c r j", r=2, j=32)[:, :, r],
                in_=a2b_dst[r][:, 256:512].rearrange("p (c j) -> p c j", c=8)), 16)
            s_p3y.inc(e.dma_start(out=ROWS[0:1, 1088 + r * 32:1120 + r * 32],
                                  in_=a2f_dst[r][0:1, :]), 16)
            s_p3y.inc(e.dma_start(out=ROWS[0:1, 1152 + r * 32:1184 + r * 32],
                                  in_=a2f_dst[r][1:2, :]), 16)
    on("sync", f)
    unp_y = s_p3y.n

    s_mkw = [b.reuse("qkv_w0"), b.reuse("qkv_w1")]
    s_mkm = b.reuse("s_aevm")
    s_mks = b.reuse("s_aex")
    s_mkv = b.reuse("s_aevv")
    mkw0 = [s_mkw[0].n, s_mkw[1].n]
    mkm0 = s_mkm.n
    mks0 = s_mks.n
    mkv0 = s_mkv.n

    def f(e):
        for fc in range(NC):
            if fc >= 2:
                e.wait_ge(s_mkm.h, mkm0 + 2 * (fc - 1))
            s_mkw[fc % 2].inc(e.dma_start(
                out=WB1[:, (fc % 2) * 1024:(fc % 2) * 1024 + 1024]
                    .rearrange("p (c f) -> p c f", c=8),
                in_=rearr_w(m0T)[:, :, fc * 128:(fc + 1) * 128]), 16)
    on("gpsimd", f)

    def f(e):
        for fc in range(NC):
            e.wait_ge(s_mkw[fc % 2].h, mkw0[fc % 2] + 16 * (fc // 2 + 1))
            wv = WB1[:, (fc % 2) * 1024:(fc % 2) * 1024 + 1024] \
                .rearrange("p (c f) -> p c f", c=8)
            ebs = EB[:, (fc % 2) * 1024:(fc % 2) * 1024 + 1024]
            if fc == 0:
                e.wait_ge(s_p3y.h, unp_y)
            dst = PA if fc % 2 == 0 else PB
            if fc >= 2:
                e.wait_ge(s_mkv.h, mkv0 + fc - 1)
            last = None
            for c in range(NC):
                last = e.matmul(dst[:, 0:64], wv[:, c, :],
                                KSB[:, c * 64:(c + 1) * 64],
                                start=(c == 0), stop=(c == NC - 1))
            s_mkm.inc(last, 1)
            e.wait_ge(s_mks.h, mks0 + fc + 1)
            for k in range(2):
                last = e.matmul(PA[0:1, 512:1024], ONE[:, 0:1],
                                ebs[:, k * 512:(k + 1) * 512],
                                start=(fc == 0 and k == 0),
                                stop=(fc == NC - 1 and k == 1))
            s_mkm.inc(last, 1)
    on("tensor", f)

    def f(e):
        for fc in range(NC):
            e.wait_ge(s_mkw[fc % 2].h, mkw0[fc % 2] + 16 * (fc // 2 + 1))
            if fc >= 2:
                e.wait_ge(s_mkm.h, mkm0 + 2 * fc - 2)
            ebs = EB[:, (fc % 2) * 1024:(fc % 2) * 1024 + 1024]
            s_mks.inc(e.activation(
                out=ebs, in_=WB1[:, (fc % 2) * 1024:(fc % 2) * 1024 + 1024],
                func=AF.Square), 1)
    on("scalar", f)

    def f(e):
        for fc in range(NC):
            e.wait_ge(s_mkm.h, mkm0 + 2 * fc + 1)
            src = PA if fc % 2 == 0 else PB
            s_mkv.inc(e.tensor_copy(out=SMB[:, fc * 64:(fc + 1) * 64],
                                    in_=src[:, 0:64]), 1)
    on("vector", f)
    mk_v = s_mkv.n

    def f(e):
        e.wait_ge(s_mkm.h, mkm0 + 2 * NC)
        s_mks.inc(e.activation(out=VO[:, 0:4096].bitcast(F32)[0:1, 0:512],
                               in_=PA[0:1, 512:1024], func=AF.Identity,
                               accum_out=row(R_M0N, 0, 1)), 1)
    on("scalar", f)
    m0n_s = s_mks.n

    def f(e):
        e.wait_ge(s_p3y.h, unp_y)
        e.wait_ge(s_mks.h, m0n_s)
        s_mks.inc(e.activation(out=EB[:, 0:512], in_=KSB[:], func=AF.Square), 1)
    on("scalar", f)
    sqk_s = s_mks.n

    def f(e):
        e.wait_ge(s_mks.h, sqk_s)
        e.wait_ge(s_mkv.h, mk_v)
        last = None
        for c in range(NC):
            e.matmul(PA[0:64, 1024:1088], KSB[:, c * 64:(c + 1) * 64],
                     KSB[:, c * 64:(c + 1) * 64],
                     start=(c == 0), stop=(c == NC - 1))
            e.matmul(PA[0:1, 1536:1600], ONE[:, 0:1],
                     EB[:, c * 64:(c + 1) * 64],
                     start=(c == 0), stop=(c == NC - 1))
            last = e.matmul(PA[0:1, 512:576], SMB[:, 524 + c * 3 + 2:524 + c * 3 + 3],
                            KSB[:, c * 64:(c + 1) * 64],
                            start=(c == 0), stop=(c == NC - 1))
        s_mkm.inc(last, 1)
    on("tensor", f)
    gk_m = s_mkm.n

    def f(e):
        e.wait_ge(s_mkm.h, gk_m)
        e.tensor_copy(out=PBG[0:64, 0:64], in_=PA[0:64, 1024:1088])
        s_mkv.inc(e.tensor_copy(out=row(R_KN2, 0, 64), in_=PA[0:1, 1536:1600]), 1)
    on("vector", f)
    gk_v = s_mkv.n

    def f(e):
        e.wait_ge(s_mkv.h, gk_v)
        s_mks.inc(e.activation(out=row(R_GATE, 0, 64), in_=PA[0:1, 512:576],
                               func=AF.Sigmoid, bias=SM[0:1, 63:64]), 1)
    on("scalar", f)
    gc_s = s_mks.n

    def f(e):
        e.wait_ge(s_mks.h, gc_s)
        e.tensor_scalar(out=ROWS[0:1, 1152:1216], in0=ROWS[0:1, 1152:1216],
                        scalar1=0.5, scalar2=None, op0=ALU.add)
        e.tensor_scalar(out=ROWS[0:1, 1664:1728], in0=ROWS[0:1, 1152:1216],
                        scalar1=-1.0, scalar2=None, op0=ALU.add)
        e.tensor_tensor(out=row(R_A, 0, 64), in0=row(R_GATE, 0, 64),
                        in1=ROWS[0:1, 1664:1728], op=ALU.mult)
        e.tensor_scalar(out=row(R_A, 0, 64), in0=row(R_A, 0, 64),
                        scalar1=1.0, scalar2=None, op0=ALU.add)
        s_mkv.inc(e.tensor_tensor(out=row(R_B, 0, 64), in0=row(R_GATE, 0, 64),
                                  in1=ROWS[0:1, 1088:1152], op=ALU.mult), 1)
    on("vector", f)
    ab_v = s_mkv.n

    def f(e):
        e.wait_ge(s_mkv.h, gk_v)
        last = None
        for nt in range(4):
            for c in range(NC):
                last = e.matmul(PA[0:64, nt * 512:(nt + 1) * 512],
                                KSB[:, c * 64:(c + 1) * 64],
                                XN[:, c * T + nt * 512:c * T + (nt + 1) * 512],
                                start=(c == 0), stop=(c == NC - 1))
        s_mkm.inc(last, 1)
    on("tensor", f)
    pt_m = s_mkm.n

    def f(e):
        e.wait_ge(s_mkm.h, pt_m)
        s_mkv.inc(e.tensor_copy(out=PT[:, 0:T], in_=PA[0:64, 0:T]), 1)
    on("vector", f)
    pt_v = s_mkv.n

    def f(e):
        e.wait_ge(s_mkm.h, pt_m)
        for j in range(32):
            s_mks.inc(e.activation(out=SMB[:, 576:640][0:64, 0:64],
                                   in_=PA[0:64, j * 64:(j + 1) * 64],
                                   func=AF.Identity, scale=1.0 / 64.0,
                                   accum_out=CH2[0:64, 512 + j:513 + j]), 1)
    on("scalar", f)
    pb_s = s_mks.n

    def f(e):
        e.wait_ge(s_mkv.h, pt_v)
        s_p3y.inc(e.dma_start(out=a3p_src, in_=PT[:, 0:T]), 16)
        e.wait_ge(s_mks.h, pb_s)
        s_p3y.inc(e.dma_start(out=a3b_src, in_=CH2[0:64, 512:544]), 16)
    on("sync", f)
    a3_y = s_p3y.n

    def f(e):
        e.wait_ge(s_p3y.h, a3_y)
        e.collective_compute(
            "AllGather", ALU.bypass, replica_groups=PAIRS,
            ins=[a3p_src], outs=[a3p_dst]).then_inc(s_cc.h, 1)
        s_cc.n += 1
        e.collective_compute(
            "AllGather", ALU.bypass, replica_groups=PAIRS,
            ins=[a3b_src], outs=[a3b_dst]).then_inc(s_cc.h, 1)
        s_cc.n += 1
    on("gpsimd", f)
    cc_a3 = s_cc.n

    def f(e):
        e.wait_ge(s_cc.h, cc_a3)
        for r in range(2):
            s_p3y.inc(e.dma_start(out=PT[:, r * T:(r + 1) * T],
                                  in_=a3p_dst[r]), 16)
    on("sync", f)

    def f(e):
        e.wait_ge(s_cc.h, cc_a3)
        for r in range(2):
            s_p3g.inc(e.dma_start(out=PBG[0:64, 64 + r * 32:96 + r * 32],
                                  in_=a3b_dst[r]), 16)
    on("gpsimd", f)
    barrier()
    ckpt("l1pre")
    dbg("ksb", KSB[:], [128, 512], BF16)
    dbg("bmc", BMC[:], [128, 512], BF16)
    dbg("m0k", SMB[:, 0:512], [128, 512], BF16)
    dbg("gpb", PBG[:], [64, 128], BF16)
    dbg("rows13", ROWS[0:1, 1088:1664], [1, 576], F32)
    dbg("ptb", PT[:], [64, 4096], BF16)

    s_scm = b.reuse("att_m")
    s_scs = b.reuse("s_aevs")
    s_scv = b.reuse("s_rv")
    s_scy = b.reuse("s_rm0")
    s_bw = [b.reuse("s_rm1"), b.reuse("s_rs0")]
    s_vg = b.reuse("wobv_w0")

    scm0 = s_scm.n
    scs0 = s_scs.n
    scv0 = s_scv.n
    scy0 = s_scy.n
    bw0 = [s_bw[0].n, s_bw[1].n]
    vg0 = s_vg.n

    def f(e):
        e.memset(UT[:], 0.0)
        e.memset(VT[:], 0.0)
        e.memset(SM[0:1, 102:103], 1.0)
        e.memset(SM[0:64, 103:104], 1.0)
        e.memset(SM[:, 105:106], 1.0)
        e.memset(ROWS[0:1, 1472:1473], 1.0)
        s_vg.inc(e.dma_start(
            out=MOW[:].rearrange("p (c f) -> p c f", c=8),
            in_=rearr_w(vg1T)), 16)
        s_vg.inc(e.dma_start(out=QP[:], in_=vg2q[:, 0:4096]), 16)
        s_vg.inc(e.dma_start(out=KHV[:], in_=vg2q[:, 4096:8192]), 16)
    on("gpsimd", f)
    vg_done = s_vg.n

    def f(e):
        s_scv.inc(e.tensor_copy(out=SM[0:1, 104:105], in_=row(R_M0N, 0, 1)), 1)
    on("vector", f)
    nm2_v = s_scv.n

    def bst_view(t):
        sl = WB1[:, (t % 2) * 1024:(t % 2) * 1024 + 512]
        return sl.rearrange("p (c f) -> p c f", c=8)

    def bst_src(t):
        return base_dstT[t // 32].rearrange("(c p) f -> p c f", p=128) \
            [:, :, (t % 32) * 64:(t % 32) * 64 + 64]

    def f(e):
        s_bw[0].inc(e.dma_start(out=bst_view(0), in_=bst_src(0)), 16)
        s_bw[1].inc(e.dma_start(out=bst_view(1), in_=bst_src(1)), 16)
    on("sync", f)

    NST = SCAN_STEPS
    c_b2 = [0] * (NST + 1)
    c_corr = [0] * NST
    c_hm = [0] * NST
    c_v1 = [0] * NST
    c_omk = [0] * NST
    c_dot = [0] * NST
    c_rep = [0] * NST
    c_vmm = [0] * NST
    v_wc = [0] * NST
    v_err = [0] * NST
    v_n2 = [0] * NST
    v_gup = [0] * NST
    v_rep = [0] * NST
    v_uc = [0] * NST
    v_vc = [0] * NST
    q_sil = [0] * NST
    q_sc = [0] * NST
    y_app = [0] * NST

    def vg2tile(k, m):
        tix = k * 8 + m
        buf = QP if tix < 32 else KHV
        off = (tix % 32) * 128
        return buf[:, off:off + 128]

    def f(e):
        e.wait_ge(s_vg.h, vg_done)
        e.wait_ge(s_bw[0].h, bw0[0] + 16)
        last = None
        for c in range(NC):
            for nt in range(2):
                last = e.matmul(PB[0:64, nt * 512:(nt + 1) * 512],
                                bst_view(0)[:, c, :],
                                MOW[:, c * 1024 + nt * 512:c * 1024 + (nt + 1) * 512],
                                start=(c == 0), stop=False)
        s_scm.inc(last, 1)
    on("tensor", f)
    c_b2[0] = s_scm.n

    for t in range(NST):
        half = (t % 2) * 1024
        nhalf = ((t + 1) % 2) * 1024

        def f(e, t=t, half=half, nhalf=nhalf):
            if t + 1 < NST:
                e.wait_ge(s_bw[(t + 1) % 2].h, bw0[(t + 1) % 2] + 16 * ((t + 1) // 2 + 1))
                if t >= 1:
                    e.wait_ge(s_scs.h, q_sil[t - 1])
                last = None
                for c in range(NC):
                    for nt in range(2):
                        last = e.matmul(PB[0:64, nhalf + nt * 512:nhalf + (nt + 1) * 512],
                                        bst_view(t + 1)[:, c, :],
                                        MOW[:, c * 1024 + nt * 512:c * 1024 + (nt + 1) * 512],
                                        start=(c == 0), stop=False)
                s_scm.inc(last, 1)
            if t >= 1:
                e.wait_ge(s_scy.h, y_app[t - 1])
            last = None
            for nt in range(2):
                last = e.matmul(PB[0:64, half + nt * 512:half + (nt + 1) * 512],
                                PT[:, t * 64:(t + 1) * 64],
                                VT[:, nt * 512:(nt + 1) * 512],
                                start=False, stop=True)
            s_scm.inc(last, 1)
        on("tensor", f)
        c_b2[t + 1] = s_scm.n - 1
        c_corr[t] = s_scm.n

        def f(e, t=t, half=half):
            e.wait_ge(s_scm.h, c_corr[t])
            if t >= 1:
                e.wait_ge(s_scv.h, v_rep[t - 1])
            s_scs.inc(e.activation(
                out=EB[0:64, half:half + 1024], in_=PB[0:64, half:half + 1024],
                func=AF.Silu, scale=SM[0:64, 103:104]), 1)
        on("scalar", f)
        q_sil[t] = s_scs.n

        def f(e, t=t, half=half):
            e.wait_ge(s_scs.h, q_sil[t])
            last = None
            for m in range(8):
                last = e.matmul(PA[:, 1536 + m:1537 + m],
                                EB[0:64, half + m * 128:half + (m + 1) * 128],
                                SMB[0:64, 548:549], start=True, stop=True)
            s_scm.inc(last, 1)
        on("tensor", f)
        c_hm[t] = s_scm.n

        def f(e, t=t):
            e.wait_ge(s_scm.h, c_hm[t])
            s_scv.inc(e.tensor_copy(out=SMB[:, 552:560], in_=PA[:, 1536:1544]), 1)
        on("vector", f)
        v_wc[t] = s_scv.n

        def f(e, t=t):
            e.wait_ge(s_scv.h, v_wc[t])
            last = None
            for m in range(8):
                for k in range(8):
                    last = e.matmul(PA[:, 1544 + m:1545 + m], vg2tile(k, m),
                                    SMB[:, 552 + k:553 + k],
                                    start=(k == 0), stop=(k == 7))
            s_scm.inc(last, 1)
            if t >= 1:
                e.wait_ge(s_scy.h, y_app[t - 1])
            for m in range(8):
                e.matmul(PA[:, 1552 + m:1553 + m], UT[:, m * 128:(m + 1) * 128],
                         PBG[0:64, 64 + t:65 + t], start=True, stop=True)
                last = e.matmul(PA[:, 1560 + m:1561 + m], UT[:, m * 128:(m + 1) * 128],
                                PBG[0:64, t:t + 1], start=True, stop=True)
            s_scm.inc(last, 1)
        on("tensor", f)
        c_v1[t] = s_scm.n - 1
        c_omk[t] = s_scm.n

        def f(e, t=t):
            e.wait_ge(s_scm.h, c_omk[t])
            bmcc = BMC[:, t::64]
            m0kc = SMB[:, 0:512][:, t::64]
            e.tensor_tensor(out=CH2[:, 544:552], in0=PA[:, 1552:1560],
                            in1=bmcc, op=ALU.add)
            e.tensor_scalar(out=CH2[:, 544:552], in0=CH2[:, 544:552],
                            scalar1=SM[:, 105:106], scalar2=None, op0=ALU.mult)
            e.tensor_tensor(out=CH2[:, 552:560], in0=PA[:, 1560:1568],
                            in1=m0kc, op=ALU.add)
            e.tensor_scalar(out=CH2[:, 552:560], in0=CH2[:, 552:560],
                            scalar1=SM[:, 105:106], scalar2=None, op0=ALU.mult)
            e.tensor_tensor(out=CH2[:, 560:568], in0=PA[:, 1544:1552],
                            in1=CH2[:, 544:552], op=ALU.add)
            e.tensor_tensor(out=CH2[:, 568:576], in0=CH2[:, 560:568],
                            in1=CH2[:, 552:560], op=ALU.subtract)
            e.scalar_tensor_tensor(out=CH2[:, 592:600], in0=CH2[:, 568:576],
                                   scalar=1.0, in1=CH2[:, 552:560],
                                   op0=ALU.mult, op1=ALU.mult,
                                   accum_out=CH2[:, 576:577])
            e.scalar_tensor_tensor(out=CH2[:, 600:608], in0=CH2[:, 568:576],
                                   scalar=1.0, in1=CH2[:, 568:576],
                                   op0=ALU.mult, op1=ALU.mult,
                                   accum_out=CH2[:, 577:578])
            s_scv.inc(e.tensor_copy(out=SMB[:, 640:642], in_=CH2[:, 576:578]), 1)
        on("vector", f)
        v_err[t] = s_scv.n

        def f(e, t=t):
            e.wait_ge(s_scv.h, v_err[t])
            e.matmul(PA[0:1, 1568:1569], SMB[:, 640:641], ONE[:, 0:1],
                     start=True, stop=True)
            last = e.matmul(PA[0:1, 1569:1570], SMB[:, 641:642], ONE[:, 0:1],
                            start=True, stop=True)
            s_scm.inc(last, 1)
        on("tensor", f)
        c_dot[t] = s_scm.n

        def f(e, t=t):
            e.wait_ge(s_scm.h, c_dot[t])
            ca = ROWS[0:1, 1280 + t:1281 + t]
            cb = ROWS[0:1, 1344 + t:1345 + t]
            kn = ROWS[0:1, 1408 + t:1409 + t]
            e.tensor_tensor(out=SM[0:1, 110:111], in0=ca, in1=ca, op=ALU.mult)
            e.tensor_tensor(out=SM[0:1, 110:111], in0=SM[0:1, 110:111],
                            in1=SM[0:1, 104:105], op=ALU.mult)
            e.tensor_tensor(out=SM[0:1, 111:112], in0=ca, in1=cb, op=ALU.mult)
            e.tensor_tensor(out=SM[0:1, 111:112], in0=SM[0:1, 111:112],
                            in1=PA[0:1, 1568:1569], op=ALU.mult)
            e.tensor_scalar(out=SM[0:1, 111:112], in0=SM[0:1, 111:112],
                            scalar1=2.0, scalar2=None, op0=ALU.mult)
            e.tensor_tensor(out=SM[0:1, 112:113], in0=cb, in1=cb, op=ALU.mult)
            e.tensor_tensor(out=SM[0:1, 112:113], in0=SM[0:1, 112:113],
                            in1=kn, op=ALU.mult)
            e.tensor_tensor(out=SM[0:1, 112:113], in0=SM[0:1, 112:113],
                            in1=PA[0:1, 1569:1570], op=ALU.mult)
            e.tensor_tensor(out=SM[0:1, 106:107], in0=SM[0:1, 110:111],
                            in1=SM[0:1, 111:112], op=ALU.add)
            s_scv.inc(e.tensor_tensor(out=SM[0:1, 106:107], in0=SM[0:1, 106:107],
                                      in1=SM[0:1, 112:113], op=ALU.add), 1)
        on("vector", f)
        v_n2[t] = s_scv.n

        def f(e, t=t):
            e.wait_ge(s_scv.h, v_n2[t])
            e.activation(out=SM[0:1, 113:114], in_=SM[0:1, 106:107],
                         func=AF.Sqrt, bias=SM[0:1, 97:98])
            e.activation(out=SM[0:1, 113:114], in_=SM[0:1, 113:114], func=AF.Ln)
            s_scs.inc(e.activation(out=SM[0:1, 113:114], in_=SM[0:1, 113:114],
                                   func=AF.Exp, scale=-1.0), 1)
        on("scalar", f)
        q_rc = s_scs.n

        def f(e, t=t, q_rc=q_rc):
            e.wait_ge(s_scs.h, q_rc)
            e.tensor_scalar(out=SM[0:1, 107:108], in0=SM[0:1, 113:114],
                            scalar1=30.0, scalar2=None, op0=ALU.mult)
            e.tensor_scalar(out=SM[0:1, 107:108], in0=SM[0:1, 107:108],
                            scalar1=1.0, scalar2=None, op0=ALU.min)
            e.tensor_tensor(out=SM[0:1, 108:109], in0=SM[0:1, 107:108],
                            in1=ROWS[0:1, 1280 + t:1281 + t], op=ALU.mult)
            e.tensor_tensor(out=SM[0:1, 108:109], in0=SM[0:1, 108:109],
                            in1=SM[0:1, 102:103], op=ALU.mult)
            e.tensor_tensor(out=SM[0:1, 114:115], in0=SM[0:1, 107:108],
                            in1=SM[0:1, 107:108], op=ALU.mult)
            e.tensor_tensor(out=SM[0:1, 104:105], in0=SM[0:1, 114:115],
                            in1=SM[0:1, 106:107], op=ALU.mult)
            e.tensor_copy(out=SM[0:1, 102:103], in_=SM[0:1, 108:109])
            if t + 1 < NST:
                e.tensor_copy(out=ROWS[0:1, 1472 + t + 1:1473 + t + 1],
                              in_=SM[0:1, 108:109])
            s_scv.inc(e.tensor_tensor(out=SM[0:1, 115:116],
                                      in0=SM[0:1, 107:108],
                                      in1=ROWS[0:1, 1344 + t:1345 + t],
                                      op=ALU.mult), 1)
        on("vector", f)
        v_gup[t] = s_scv.n

        def f(e, t=t):
            e.wait_ge(s_scv.h, v_gup[t])
            e.activation(out=SM[0:1, 116:117], in_=SM[0:1, 108:109], func=AF.Ln)
            e.activation(out=SM[0:1, 116:117], in_=SM[0:1, 116:117],
                         func=AF.Exp, scale=-1.0)
            e.copy(out=SMB[0:1, 643:644], in_=SM[0:1, 108:109])
            s_scs.inc(e.copy(out=SM[0:1, 117:118], in_=SM[0:1, 116:117]), 1)
        on("scalar", f)
        q_sc[t] = s_scs.n

        def f(e, t=t):
            e.wait_ge(s_scs.h, q_sc[t])
            e.tensor_tensor(out=SM[0:1, 118:119], in0=SM[0:1, 115:116],
                            in1=SM[0:1, 117:118], op=ALU.mult)
            s_scv.inc(e.tensor_copy(out=SMB[0:1, 642:643], in_=SM[0:1, 118:119]), 1)
        on("vector", f)
        v_rho = s_scv.n

        def f(e, t=t, v_rho=v_rho):
            e.wait_ge(s_scv.h, v_rho)
            e.matmul(PA[:, 1570:1571], ONE[0:1, :], SMB[0:1, 643:644],
                     start=True, stop=True)
            last = e.matmul(PA[:, 1571:1572], ONE[0:1, :], SMB[0:1, 642:643],
                            start=True, stop=True)
            s_scm.inc(last, 1)
        on("tensor", f)
        c_rep[t] = s_scm.n

        def f(e, t=t):
            e.wait_ge(s_scm.h, c_rep[t])
            e.tensor_copy(out=SM[0:64, 103:104], in_=PA[0:64, 1570:1571])
            e.tensor_copy(out=SM[:, 105:106], in_=PA[:, 1570:1571])
            e.tensor_copy(out=SM[:, 109:110], in_=PA[:, 1571:1572])
            s_scv.inc(e.tensor_scalar(out=SMB[:, 644:652], in0=CH2[:, 568:576],
                                      scalar1=SM[:, 109:110], scalar2=None,
                                      op0=ALU.mult), 2)
        on("vector", f)
        v_rep[t] = s_scv.n - 1
        v_uc[t] = s_scv.n

        def f(e, t=t):
            e.wait_ge(s_scv.h, v_uc[t])
            last = None
            for m in range(8):
                for k in range(8):
                    last = e.matmul(PA[:, 1572 + m:1573 + m],
                                    MOW[:, k * 1024 + m * 128:k * 1024 + (m + 1) * 128],
                                    SMB[:, 644 + k:645 + k],
                                    start=(k == 0), stop=(k == 7))
            s_scm.inc(last, 1)
        on("tensor", f)
        c_vmm[t] = s_scm.n

        def f(e, t=t):
            e.wait_ge(s_scm.h, c_vmm[t])
            s_scv.inc(e.tensor_copy(out=SMB[:, 652:660], in_=PA[:, 1572:1580]), 1)
        on("vector", f)
        v_vc[t] = s_scv.n

        def f(e, t=t):
            if t + 2 < NST:
                e.wait_ge(s_scm.h, c_b2[t + 1])
                s_bw[t % 2].inc(e.dma_start(out=bst_view(t + 2),
                                            in_=bst_src(t + 2)), 16)
            e.wait_ge(s_scv.h, v_vc[t])
            if t >= 1:
                e.wait_ge(s_scy.h, y_app[t - 1])
            s_scy.inc(e.dma_start(
                out=uvrow_hbm[0:1, :].rearrange("o (s p) -> p (o s)", p=128),
                in_=SMB[:, 644:652], allow_slow_non_contiguous=True), 16)
            s_scy.inc(e.dma_start(
                out=uvrow_hbm[1:2, :].rearrange("o (s p) -> p (o s)", p=128),
                in_=SMB[:, 652:660], allow_slow_non_contiguous=True), 16)
            e.wait_ge(s_scy.h, s_scy.n)
            s_scy.inc(e.dma_start(out=UT[t:t + 1, :], in_=uvrow_hbm[0:1, :]), 16)
            s_scy.inc(e.dma_start(out=VT[t:t + 1, :], in_=uvrow_hbm[1:2, :]), 16)
        on("sync", f)
        y_app[t] = s_scy.n

    barrier()
    ckpt("l1scan")
    dbg("ut", UT[:], [64, D], BF16)
    dbg("vt", VT[:], [64, D], BF16)
    dbg("gam", ROWS[0:1, 1472:1536], [1, 64], F32)
    dbg("sc0", CH2[:, 544:608], [128, 64], F32)
    dbg("wc0", SMB[:, 544:672], [128, 128], BF16)
    dbg("ht0", EB[0:64, 0:1024], [64, 1024], BF16)
    dbg("bst0", WB1[:, 0:512], [128, 512], BF16)
    dbg("mow0", MOW[:, 0:2048], [128, 2048], BF16)

    s_f5m = b.reuse("qkv_m")
    s_f5v = b.reuse("s_bv")
    s_f5s = b.reuse("s_bs")
    s_f5y = b.reuse("s_rs1")

    f5y0 = s_f5y.n
    f5v0 = s_f5v.n
    f5m0 = s_f5m.n

    half_par = None

    LMK = EB[0:64, 0:4096]

    def f(e):
        s_f5y.inc(e.dma_start(out=LMK, in_=l1msk), 16)
    on("sync", f)
    lmk_y = s_f5y.n

    def f(e):
        e.wait_ge(s_f5y.h, lmk_y)
        pass
    on("sync", f)

    def f(e):
        s_f5v.inc(e.tensor_copy(out=RW2[0:1, 0:64], in_=ROWS[0:1, 1472:1536]), 1)
    on("vector", f)
    gv = s_f5v.n

    def f(e):
        e.wait_ge(s_f5v.h, gv)
        s_f5y.inc(e.dma_start(out=uvrow_hbm[0:1, 0:64], in_=RW2[0:1, 0:64]), 16)
        e.wait_ge(s_f5y.h, s_f5y.n)
        s_f5y.inc(e.dma_start(
            out=ESEL[0:64, 0:1],
            in_=uvrow_hbm[0:1, 0:64].rearrange("o (p u) -> p (o u)", u=1)), 16)
    on("sync", f)
    gc_y = s_f5y.n

    PTm = PT
    def f(e):
        e.wait_ge(s_f5y.h, lmk_y)
        s_f5v.inc(e.tensor_tensor(out=PT[:, 0:4096],
                                  in0=PT[:, 0:4096], in1=LMK,
                                  op=ALU.mult), 1)
    on("vector", f)
    ptm_v = s_f5v.n

    def f(e):
        e.wait_ge(s_f5y.h, gc_y)
        e.wait_ge(s_f5v.h, ptm_v)
        last = None
        for nt in range(4):
            last = e.matmul(PA[0:1, nt * 512:(nt + 1) * 512], ESEL[0:64, 0:1],
                            LMK[:, 2048 + nt * 512:2048 + (nt + 1) * 512],
                            start=True, stop=True)
        s_f5m.inc(last, 1)
    on("tensor", f)
    gt_m = s_f5m.n

    def f(e):
        e.wait_ge(s_f5m.h, gt_m)
        s_f5v.inc(e.tensor_copy(out=RW2[0:1, 0:T], in_=PA[0:1, 0:T]), 1)
    on("vector", f)
    gt_v = s_f5v.n

    def f(e):
        e.wait_ge(s_f5v.h, gt_v)
        last = None
        for nt in range(4):
            last = e.matmul(PB[:, nt * 512:(nt + 1) * 512], ONE[0:1, :],
                            RW2[0:1, nt * 512:(nt + 1) * 512],
                            start=True, stop=True)
        s_f5m.inc(last, 1)
    on("tensor", f)
    gr_m = s_f5m.n

    def f(e):
        e.wait_ge(s_f5m.h, gr_m)
        s_f5v.inc(e.tensor_copy(out=RW2[:, 0:T], in_=PB[:, 0:T]), 1)
    on("vector", f)
    grp_v = s_f5v.n

    def ft_slice(hs):
        return [VO[:, 0:2048], VO[:, 2080:4128], VO[:, 4160:6208],
                EB[:, 0:2048], EB[:, 2048:4096], WB3[:, 0:2048],
                WB1[:, 0:2048],
                CH2[:, 0:1024].bitcast(BF16)[:, 0:2048]][hs]

    mb = s_f5m.n
    vb = s_f5v.n
    sb = s_f5s.n
    hid_m = [mb + hs + 1 for hs in range(8)]
    hid_v = [vb + hs + 1 for hs in range(8)]
    hid_s = [sb + hs + 1 for hs in range(8)]
    outs_m = [mb + 8 + fs + 1 for fs in range(8)]
    outs_v = [vb + 8 + fs + 1 for fs in range(8)]
    res_m = [mb + 16 + fs + 1 for fs in range(8)]
    comb_v = [vb + 16 + fs + 1 for fs in range(8)]
    s_fin = sem("s_fin")
    yb = s_f5y.n
    x1_y = [yb + 16 * (fs + 1) for fs in range(8)]

    def f(e):
        e.wait_ge(s_f5v.h, grp_v)
        for hs in range(8):
            if hs >= 2:
                e.wait_ge(s_f5v.h, hid_v[hs - 2])
            dst = PA if hs % 2 == 0 else PB
            last = None
            for nt in range(4):
                for k in range(9):
                    if k < 8:
                        lhsT = MOW[:, k * 1024 + hs * 128:k * 1024 + (hs + 1) * 128]
                        rhs = HA[:, k * T + nt * 512:k * T + (nt + 1) * 512]
                    else:
                        lhsT = VT[:, hs * 128:(hs + 1) * 128]
                        rhs = PT[:, nt * 512:(nt + 1) * 512]
                    last = e.matmul(dst[:, nt * 512:(nt + 1) * 512], lhsT, rhs,
                                    start=(k == 0), stop=(k == 8))
            s_f5m.inc(last, 1)
    on("tensor", f)

    def f(e):
        for hs in range(8):
            e.wait_ge(s_f5m.h, hid_m[hs])
            if hs >= 1:
                e.wait_ge(s_f5s.h, hid_s[hs - 1])
            src = PA if hs % 2 == 0 else PB
            s_f5v.inc(e.tensor_tensor(out=CHK[:, 0:T], in0=src[:, 0:T],
                                      in1=RW2[:, 0:T], op=ALU.mult), 1)
    on("vector", f)

    def f(e):
        for hs in range(8):
            e.wait_ge(s_f5v.h, hid_v[hs])
            s_f5s.inc(e.activation(out=ft_slice(hs), in_=CHK[:, 0:T],
                                   func=AF.Silu), 1)
    on("scalar", f)

    def f(e):
        e.wait_ge(s_f5v.h, hid_v[7])
        for fs in range(8):
            if fs >= 2:
                e.wait_ge(s_f5v.h, outs_v[fs - 2])
            dst = PA if fs % 2 == 0 else PB
            last = None
            for nt in range(4):
                last = e.matmul(dst[:, nt * 512:(nt + 1) * 512],
                                UT[:, fs * 128:(fs + 1) * 128],
                                PT[:, nt * 512:(nt + 1) * 512],
                                start=True, stop=True)
            s_f5m.inc(last, 1)
    on("tensor", f)

    def f(e):
        e.wait_ge(s_f5s.h, hid_s[7])
        for fs in range(8):
            e.wait_ge(s_f5m.h, outs_m[fs])
            src = PA if fs % 2 == 0 else PB
            e.tensor_tensor(out=src[:, 0:T], in0=src[:, 0:T],
                            in1=HA[:, fs * T:(fs + 1) * T], op=ALU.add)
            s_f5v.inc(e.tensor_tensor(out=XN[:, fs * T:(fs + 1) * T],
                                      in0=src[:, 0:T], in1=RW2[:, 0:T],
                                      op=ALU.mult), 1)
    on("vector", f)

    def f(e):
        e.wait_ge(s_f5v.h, outs_v[7])
        e.wait_ge(s_f5s.h, hid_s[7])
        for fs in range(8):
            if fs >= 2:
                e.wait_ge(s_f5v.h, comb_v[fs - 2])
            dst = PA if fs % 2 == 0 else PB
            last = None
            for nt in range(4):
                for k in range(8):
                    last = e.matmul(dst[:, nt * 512:(nt + 1) * 512],
                                    vg2tile(k, fs),
                                    ft_slice(k)[:, nt * 512:(nt + 1) * 512],
                                    start=(k == 0), stop=(k == 7))
            s_f5m.inc(last, 1)
    on("tensor", f)

    def f(e):
        for fs in range(8):
            e.wait_ge(s_f5m.h, res_m[fs])
            e.wait_ge(s_f5y.h, x1_y[fs])
            src = PA if fs % 2 == 0 else PB
            e.tensor_tensor(out=CHK[:, 0:T], in0=CHK[:, 0:T],
                            in1=src[:, 0:T], op=ALU.add)
            e.tensor_tensor(out=CHK[:, 0:T], in0=CHK[:, 0:T],
                            in1=XN[:, fs * T:(fs + 1) * T], op=ALU.add)
            s_f5v.inc(e.tensor_tensor(out=CHK[:, 0:T], in0=CHK[:, 0:T],
                                      in1=HB[:, fs * T:(fs + 1) * T],
                                      op=ALU.add), 1)
    on("vector", f)

    def f(e):
        e.wait_ge(s_f5s.h, hid_s[7])
        for fs in range(8):
            if fs >= 1:
                e.wait_ge(s_fin.h, 16 * fs)
            s_f5y.inc(e.dma_start(
                out=CHK[:, 0:T],
                in_=x1sp.rearrange("(c p) t -> c p t", p=128)[fs]), 16)
            e.wait_ge(s_f5v.h, comb_v[fs])
            s_fin.inc(e.dma_start(
                out=out_fm.rearrange("(c p) t -> c p t", p=128)[fs],
                in_=CHK[:, 0:T]), 16)
        e.wait_ge(s_fin.h, 16 * 8)
        e.wait_ge(s_dbg.h, s_dbg.n)
    on("sync", f)
    barrier()

    return b
def _mk_esel():
    es = np.zeros((64, 1024), BF)
    for base in (0, 32):
        for h in range(16):
            c = h // 2
            off = (h % 2) * 64
            es[base + h, c * 128 + off:c * 128 + off + 64] = 1.0
    return es


def _prep_inputs(inputs):
    f = np.ascontiguousarray
    x = inputs["L0_attn_x"]
    x_fm = f(x.transpose(0, 2, 1))
    shared = {
        "wqkvT": f(inputs["L0_attn_Wqkv"].T),
        "bqkv_c": f(inputs["L0_attn_bqkv"].reshape(24, 128).T),
        "woT": f(inputs["L0_attn_Wo"].T),
        "bo_c": f(inputs["L0_attn_bo"].reshape(8, 128).T),
        "wq": f(inputs["L0_attn_wq"].reshape(64, 1)),
        "wk": f(inputs["L0_attn_wk"].reshape(64, 1)),
        "gateWT": f(inputs["L0_ffn_gate_W"].T),
        "e0T": f(inputs["L0_ffn_expert_W"][0].T),
        "e1T": f(inputs["L0_ffn_expert_W"][1].T),
        "outWT": f(inputs["L0_ffn_out_W"].T),
        "m0T": f(inputs["L1_attn_M_init"].T),
        "vg1T": f(inputs["L1_attn_vg_w1"].T),
        "vg2T": f(inputs["L1_attn_vg_w2"].T),
        "vg2q": f(inputs["L1_attn_vg_w2"].T.reshape(8, 128, 8, 128)
                  .transpose(1, 0, 2, 3).reshape(128, 8192)),
        "eag": f(np.stack([inputs["L1_attn_eta_w"], inputs["L1_attn_alpha_w"],
                           inputs["L1_attn_gate_w"]], axis=1)),
        "eab": f(np.stack([inputs["L1_attn_eta_b"], inputs["L1_attn_alpha_b"],
                           inputs["L1_attn_gate_b"]]).reshape(3, 1)),
        "wgTi": f(inputs["L1_ffn_w_gate"].T.astype(np.int8)),
        "wuTi": f(inputs["L1_ffn_w_up"].T.astype(np.int8)),
        "wdTi": f(inputs["L1_ffn_w_down"].T.astype(np.int8)),
        "identf": np.eye(128, dtype=np.float32),
        "esel": _mk_esel(),
    }
    in_maps = []
    for c in range(8):
        bat, half = c // 2, c % 2
        m = dict(shared)
        m["x_fm"] = f(x_fm[bat, :, half * T:(half + 1) * T])
        m["parity"] = np.full((1, 1), float(half), np.float32)
        ch_i = (half * 2048 + np.arange(2048)) // 64
        jj = np.arange(64)[:, None]
        m["l1msk"] = np.concatenate(
            [(ch_i[None, :] > jj), (ch_i[None, :] == jj)],
            axis=1).astype(ml_dtypes.bfloat16)
        in_maps.append(m)
    return in_maps


def kernel(**inputs):
    b = build_program()
    in_maps = _prep_inputs(inputs)
    res = run_bass_kernel_spmd(b.nc, in_maps, core_ids=list(range(8)))
    out = np.empty((4, 4096, 1024), np.float32)
    for c in range(8):
        bat, half = c // 2, c % 2
        out[bat, half * T:(half + 1) * T, :] = res.results[c]["out_fm"].T
    return out


# version 03 (final)
# speedup vs baseline: 1.2093x; 1.2093x over previous
import numpy as np
import contextlib
import ml_dtypes
import concourse.bass as bass
import concourse.mybir as mybir
from concourse.bass_utils import run_bass_kernel_spmd

F32 = mybir.dt.float32
BF16 = mybir.dt.bfloat16
I32 = mybir.dt.int32
AF = mybir.ActivationFunctionType
ALU = mybir.AluOpType
AX = mybir.AxisListType

D = 1024
T = 2048
NC = 8
CH = 64
NCH = 64
HID = 3072

BF = ml_dtypes.bfloat16
DBG = set()
STOP_AFTER = None
SCAN_STEPS = 64
PAIRS = [[0, 1], [2, 3], [4, 5], [6, 7]]


_UPDATES = {}


class Sem:
    __slots__ = ("h", "n", "base")

    def __init__(self, h):
        self.h = h
        self.n = 0

    def inc(self, instr, k=1):
        instr.then_inc(self.h, k)
        self.n += k
        _UPDATES[id(instr)] = (instr, self, self.n)
        return self.n


_TRACKED = {
    "tensor_tensor", "tensor_scalar", "tensor_single_scalar", "tensor_copy",
    "tensor_add", "tensor_sub", "tensor_mul", "tensor_max", "reduce_sum",
    "reduce_max", "tensor_reduce", "scalar_tensor_tensor", "reciprocal",
    "memset", "activation", "copy", "mul", "add", "square", "sqrt",
    "dma_start",
}
_WRITE_KEYS = ("out", "accum_out", "ap")


def _ap_region(ap):
    try:
        name = ap.tensor.name
    except Exception:
        return None
    try:
        base = ap.base_partition()
        cnt = ap.partition_size()
    except Exception:
        base, cnt = 0, 128
    return (name, base, base + cnt)


def _regions_overlap(a, b):
    return a[0] == b[0] and a[1] < b[2] and b[1] < a[2]


class _EngineProxy:

    def __init__(self, eng, state):
        self._e = eng
        self._st = state

    def __getattr__(self, name):
        attr = getattr(self._e, name)
        if name not in _TRACKED or not callable(attr):
            return attr
        st = self._st

        def wrapper(*args, **kwargs):
            writes, reads = [], []
            import concourse.ap as _apmod
            APT = _apmod.AP
            for k, v in kwargs.items():
                if isinstance(v, APT):
                    r = _ap_region(v)
                    if r is None:
                        continue
                    (writes if k in _WRITE_KEYS else reads).append(r)
            for i, v in enumerate(args):
                if isinstance(v, APT):
                    r = _ap_region(v)
                    if r is None:
                        continue
                    if name == "dma_start" and "out" not in kwargs and i == 0:
                        writes.append(r)
                    elif i == 0 and "out" not in kwargs:
                        writes.append(r)
                    else:
                        reads.append(r)
            conflict = False
            for r in reads:
                if any(_regions_overlap(r, w) for w in st["writes"]):
                    conflict = True
            if not conflict:
                for w in writes:
                    if any(_regions_overlap(w, x)
                           for x in st["writes"] + st["reads"]):
                        conflict = True
            if conflict and st["last"] is not None:
                inst, k = st["last"]
                upd = _UPDATES.get(id(inst))
                if upd is not None and upd[0] is inst:
                    self._e.wait_ge(upd[1].h, upd[2])
                else:
                    assert k != 16, "dma without completion sem"
                    sem = st["sem"]
                    sem.inc(inst, k)
                    self._e.wait_ge(sem.h, sem.n)
                st["writes"], st["reads"] = [], []
            inst = attr(*args, **kwargs)
            st["last"] = (inst, 16 if name == "dma_start" else 1)
            st["writes"] += writes
            st["reads"] += reads
            return inst
        return wrapper


class B:
    def __init__(self):
        self.nc = bass.Bass("TRN2", target_bir_lowering=False, debug=False)
        self.es = contextlib.ExitStack()
        self.blk = None
        self.auto_state = {}
        self.sems = {}

    def din(self, name, shape, dtype=F32):
        return self.nc.dram_tensor(name, shape, dtype, kind="ExternalInput").ap()

    def dout(self, name, shape, dtype=F32):
        return self.nc.dram_tensor(name, shape, dtype, kind="ExternalOutput").ap()

    def dtmp(self, name, shape, dtype=F32):
        return self.nc.dram_tensor(name, shape, dtype).ap()

    def sbuf(self, name, shape, dtype):
        return self.es.enter_context(self.nc.sbuf_tensor(name, shape, dtype))

    def psum(self, name, shape, dtype=F32):
        return self.es.enter_context(self.nc.psum_tensor(name, shape, dtype))

    def sem(self, name):
        s = Sem(self.es.enter_context(self.nc.semaphore(name)))
        self.sems[name] = s
        return s

    def reuse(self, name):
        return self.sems[name]

    def on(self, engine_name, fn):
        if engine_name in ("vector", "scalar", "gpsimd"):
            st = self.auto_state.get(engine_name)
            if st is None:
                st = {"last": None, "writes": [], "reads": [],
                      "sem": self.sem("auto_" + engine_name)}
                self.auto_state[engine_name] = st

            def wrapped(e, fn=fn, st=st):
                fn(_EngineProxy(e, st))
            getattr(self.blk, engine_name)(wrapped)
        else:
            getattr(self.blk, engine_name)(fn)


class _Stop(Exception):
    pass


def build_program():
    try:
        return _build_program()
    except _Stop as s:
        return s.args[0]


def _build_program():
    b = B()
    nc = b.nc
    on = b.on
    sem = b.sem

    x_fm = b.din("x_fm", [D, T])
    wqkvT = b.din("wqkvT", [D, 3 * D])
    bqkv_c = b.din("bqkv_c", [128, 24])
    woT = b.din("woT", [D, D])
    bo_c = b.din("bo_c", [128, 8])
    wq_in = b.din("wq", [64, 1])
    wk_in = b.din("wk", [64, 1])
    gateWT = b.din("gateWT", [D, 8])
    e0T = b.din("e0T", [D, D])
    e1T = b.din("e1T", [D, D])
    outWT = b.din("outWT", [D, D])
    m0T = b.din("m0T", [D, D])
    vg1T = b.din("vg1T", [D, D])
    vg2T = b.din("vg2T", [D, D])
    vg2q = b.din("vg2q", [128, 8192])
    eag = b.din("eag", [D, 3])
    eab = b.din("eab", [3, 1])
    wgTi = b.din("wgTi", [D, HID], mybir.dt.int8)
    wuTi = b.din("wuTi", [D, HID], mybir.dt.int8)
    wdTi = b.din("wdTi", [HID, D], mybir.dt.int8)
    parity_in = b.din("parity", [1, 1])
    identf = b.din("identf", [128, 128])
    esel_in = b.din("esel", [64, 1024], BF16)
    l1msk = b.din("l1msk", [64, 4096], BF16)

    out_fm = b.dout("out_fm", [D, T])

    x1sp = b.dtmp("x1sp", [D, T])
    qt_sp = b.dtmp("qt_sp", [D, T], BF16)
    qkraw_sp = b.dtmp("qkraw_sp", [2 * D, T], BF16)
    ffnout_sp = b.dtmp("ffnout_sp", [D, T], BF16)
    moa_sp = b.dtmp("moa_sp", [D, T], BF16)

    kt_src = b.dtmp("kt_src", [D, T], BF16)
    v_src = b.dtmp("v_src", [T, D], BF16)
    kmax_src = b.dtmp("kmax_src", [1, 16])
    kt_dst = b.dtmp("kt_dst", [2, D, T], BF16)
    v_dst = b.dtmp("v_dst", [2, T, D], BF16)
    kmax_dst = b.dtmp("kmax_dst", [2, 1, 16])

    a2b_src = b.dtmp("a2b_src", [128, 512], BF16)
    a2b_dst = b.dtmp("a2b_dst", [2, 128, 512], BF16)
    a2f_src = b.dtmp("a2f_src", [2, 32])
    a2f_dst = b.dtmp("a2f_dst", [2, 2, 32])
    a3p_src = b.dtmp("a3p_src", [64, T], BF16)
    a3p_dst = b.dtmp("a3p_dst", [2, 64, T], BF16)
    a3b_src = b.dtmp("a3b_src", [64, 32])
    a3b_dst = b.dtmp("a3b_dst", [2, 64, 32])
    base_srcT = b.dtmp("base_srcT", [D, T], BF16)
    base_dstT = b.dtmp("base_dstT", [2, D, T], BF16)
    b2_srcT = b.dtmp("b2_srcT", [D, T], BF16)
    b2_dstT = b.dtmp("b2_dstT", [2, D, T], BF16)

    bmt_hbm = b.dtmp("bmt_hbm", [64, D], BF16)
    mkt_hbm = b.dtmp("mkt_hbm", [64, D], BF16)
    w1bt_hbm = b.dtmp("w1bt_hbm", [64, D], BF16)
    w1mt_hbm = b.dtmp("w1mt_hbm", [64, D], BF16)
    uvrow_hbm = b.dtmp("uvrow_hbm", [2, D], BF16)

    XN = b.sbuf("XN", [128, NC * T], BF16)
    HA = b.sbuf("HA", [128, NC * T], BF16)
    HB = b.sbuf("HB", [128, NC * T], BF16)
    MOW = b.sbuf("MOW", [128, 8192], BF16)
    QP = b.sbuf("QP", [128, 2 * T], BF16)
    WB1 = b.sbuf("WB1", [128, 2 * 1024], BF16)
    WB3 = b.sbuf("WB3", [128, 3072], BF16)
    EB = b.sbuf("EB", [128, 2 * T], BF16)
    KHV = b.sbuf("KHV", [128, 2 * T], BF16)
    VO = b.sbuf("VO", [128, 6208], BF16)
    CHK = b.sbuf("CHK", [128, T], F32)
    CH2 = b.sbuf("CH2", [128, 1024], F32)
    ROWS = b.sbuf("ROWS", [128, T], F32)
    RW2 = b.sbuf("RW2", [128, T], BF16)
    SM = b.sbuf("SM", [128, 128], F32)
    SMB = b.sbuf("SMB", [128, 1024], BF16)
    ONE = b.sbuf("ONE", [128, 128], BF16)
    IDF = b.sbuf("IDF", [128, 128], F32)
    ESEL = b.sbuf("ESEL", [64, 1024], BF16)
    PT = b.sbuf("PT", [64, 4096], BF16)
    UT = b.sbuf("UT", [64, D], BF16)
    VT = b.sbuf("VT", [64, D], BF16)
    PBG = b.sbuf("PBG", [64, 128], BF16)
    KSB = b.sbuf("KSB", [128, 512], BF16)
    BMC = b.sbuf("BMC", [128, 512], BF16)

    PA = b.psum("PA", [128, 2048], F32)
    PB = b.psum("PB", [128, 2048], F32)

    b.blk = b.es.enter_context(nc.Block())

    s_bar = sem("s_bar")
    ENGS = ["tensor", "scalar", "vector", "gpsimd", "sync"]

    def barrier():
        target = s_bar.n + 5

        def ft(e):
            s_bar.inc(e.ldweights(ONE[0:1, 0:128]), 1)
            e.wait_ge(s_bar.h, target)
        b.on("tensor", ft)

        def fs(e):
            s_bar.inc(e.copy(out=SM[0:1, 98:99], in_=SM[0:1, 96:97]), 1)
            e.wait_ge(s_bar.h, target)
        b.on("scalar", fs)

        def fv(e):
            s_bar.inc(e.tensor_copy(out=SM[0:1, 99:100], in_=SM[0:1, 96:97]), 1)
            e.wait_ge(s_bar.h, target)
        b.on("vector", fv)

        def fg(e):
            s_bar.inc(e.memset(SM[0:1, 100:101], 0.0), 1)
            e.wait_ge(s_bar.h, target)
        b.on("gpsimd", fg)

        def fy(e):
            e.sem_inc(s_bar.h, 1)
            e.wait_ge(s_bar.h, target)
        b.on("sync", fy)
        s_bar.n = target
        for st in b.auto_state.values():
            st["writes"], st["reads"] = [], []

    s_st = sem("s_st")
    s_dbg = sem("s_dbg")
    s_cc = sem("s_cc")

    _CH = {n: sem("ch_" + n) for n in ("vector", "scalar", "tensor", "gpsimd")}

    def _mk_chain(name):
        s = _CH[name]

        def chain(e, instr):
            s.inc(instr, 1)
            e.wait_ge(s.h, s.n)
        return chain
    chv = _mk_chain("vector")
    chs = _mk_chain("scalar")

    def dbg(name, src_ap, shape, dtype):
        if name not in DBG:
            return
        d = b.dout("dbg_" + name, shape, dtype)

        def f(e):
            s_dbg.inc(e.dma_start(out=d, in_=src_ap), 16)
        on("sync", f)

    (R_M, R_ISG, R_MIV, R_TMP, R_TMP2, R_QM, R_KM, R_BND,
     R_ETA, R_ALP, R_GATE, R_A, R_B, R_KN2, R_GAM, R_NM,
     R_G, R_S1, R_S2, R_S3, R_ERR, R_T1, R_T2, R_T3, R_VN,
     R_M0N, R_NRM, R_WT) = range(28)
    ROWMAP = {
        R_M: (0, 0), R_TMP: (32, 0), R_ISG: (64, 0), R_MIV: (96, 0),
        R_QM: (0, 1024), R_KM: (0, 1056), R_TMP2: (0, 1072),
        R_BND: (0, 1120),
        R_ETA: (0, 1088), R_ALP: (0, 1152), R_GATE: (0, 1216),
        R_A: (0, 1280), R_B: (0, 1344), R_KN2: (0, 1408), R_GAM: (0, 1472),
        R_NM: (0, 1536), R_G: (0, 1540), R_S1: (0, 1544), R_S2: (0, 1548),
        R_S3: (0, 1552), R_M0N: (0, 1556), R_NSC: (0, 1560),
        R_ERR: (32, 0), R_T1: (32, 1024), R_T2: (64, 0), R_T3: (64, 1024),
        R_VN: (96, 0), R_NRM: (96, 0), R_WT: (96, 1024),
    } if False else None

    _RM = {
        R_M: (0, 0), R_TMP: (32, 0), R_ISG: (64, 0), R_MIV: (96, 0),
        R_QM: (0, 1024), R_KM: (0, 1056), R_TMP2: (0, 1072),
        R_BND: (0, 1120),
        R_ETA: (0, 1088), R_ALP: (0, 1152), R_GATE: (0, 1216),
        R_A: (0, 1280), R_B: (0, 1344), R_KN2: (0, 1408), R_GAM: (0, 1472),
        R_NM: (0, 1536), R_G: (0, 1540), R_S1: (0, 1544), R_S2: (0, 1548),
        R_S3: (0, 1552), R_M0N: (0, 1556),
        R_ERR: (32, 0), R_T1: (32, 1024), R_T2: (64, 0), R_T3: (64, 1024),
        R_VN: (96, 0), R_NRM: (96, 0), R_WT: (96, 1024),
    }

    def row(i, lo=0, hi=T):
        p, c0 = _RM[i]
        assert c0 + hi <= T
        return ROWS[p:p + 1, c0 + lo:c0 + hi]

    def rows(i, lo=0, hi=T, step=1):
        p, c0 = _RM[i]
        return ROWS[p:p + 1, c0 + lo:c0 + hi:step]

    def brow(i, lo=0, hi=T):
        assert i == 0
        return RW2[0:1, lo:hi]

    def rearr_w(w):
        return w.rearrange("(c p) f -> p c f", p=128)

    def xn_rhs(c, nt):
        return XN[:, c * T + nt * 512:c * T + (nt + 1) * 512]

    def stream_gemm(tag, n_out, w_view, rhs_fn, evac_emit, nt_count=4,
                    nt_size=512, wait_emit=None, sems=None):
        if sems is None:
            s_wp = [sem(tag + "_w0"), sem(tag + "_w1")]
            s_m = sem(tag + "_m")
            s_e = sem(tag + "_e")
        else:
            s_wp, s_m, s_e = [sems[0], sems[1]], sems[2], sems[3]
        wb = [s_wp[0].n, s_wp[1].n]
        mb = s_m.n
        ebb = s_e.n
        s_m.base = mb
        s_e.base = ebb

        def f(e):
            if wait_emit is not None:
                wait_emit(e)
            for fc in range(n_out):
                if fc >= 2:
                    e.wait_ge(s_m.h, mb + fc - 1)
                s_wp[fc % 2].inc(e.dma_start(
                    out=WB1[:, (fc % 2) * 1024:(fc % 2) * 1024 + 1024]
                        .rearrange("p (c f) -> p c f", c=8),
                    in_=w_view(fc)), 16)
        on("gpsimd", f)

        def f(e):
            for fc in range(n_out):
                e.wait_ge(s_wp[fc % 2].h, wb[fc % 2] + 16 * (fc // 2 + 1))
                if fc >= 2:
                    e.wait_ge(s_e.h, ebb + fc - 1)
                wv = WB1[:, (fc % 2) * 1024:(fc % 2) * 1024 + 1024] \
                    .rearrange("p (c f) -> p c f", c=8)
                dst = PA if fc % 2 == 0 else PB
                last = None
                for nt in range(nt_count):
                    for c in range(NC):
                        last = e.matmul(dst[:, nt * nt_size:(nt + 1) * nt_size],
                                        wv[:, c, :], rhs_fn(c, nt),
                                        start=(c == 0), stop=(c == NC - 1))
                s_m.inc(last, 1)
        on("tensor", f)
        evac_emit(s_m, s_e)
        return s_m, s_e

    s_ld = sem("s_ld")
    s_c0g = sem("s_c0g")
    s_c0s = sem("s_c0s")
    s_c0v = sem("s_c0v")

    def f(e):
        s_ld.inc(e.dma_start(out=SM[0:64, 0:1], in_=wq_in), 16)
        s_ld.inc(e.dma_start(out=SM[64:128, 0:1], in_=wq_in), 16)
        s_ld.inc(e.dma_start(out=SM[0:64, 1:2], in_=wk_in), 16)
        s_ld.inc(e.dma_start(out=SM[64:128, 1:2], in_=wk_in), 16)
        s_ld.inc(e.dma_start(out=SM[0:3, 2:3], in_=eab), 16)
        s_ld.inc(e.dma_start(out=SM[0:1, 3:4], in_=parity_in), 16)
        s_ld.inc(e.dma_start(out=SM[:, 8:32], in_=bqkv_c), 16)
        s_ld.inc(e.dma_start(out=SM[:, 32:40], in_=bo_c), 16)
        s_ld.inc(e.dma_start(out=IDF[:], in_=identf), 16)
        s_ld.inc(e.dma_start(out=ESEL[:], in_=esel_in), 16)
    on("sync", f)
    ld0 = s_ld.n

    def f(e):
        e.memset(ONE[:], 1.0)
        e.memset(SM[:, 96:97], 1e-5)
        e.memset(SM[:, 97:98], 1e-6)
        s_c0g.inc(e.memset(SMB[:, 0:512], 0.0), 1)
    on("gpsimd", f)
    c0a = s_c0g.n


    def f(e):
        e.wait_ge(s_ld.h, ld0)
        e.activation(out=SM[:, 4:5], in_=SM[:, 0:1], func=AF.Square)
        e.activation(out=SM[:, 5:6], in_=SM[:, 1:2], func=AF.Square)
        e.copy(out=SMB[:, 512:513], in_=SM[:, 0:1])
        s_c0s.inc(e.copy(out=SMB[:, 513:514], in_=SM[:, 1:2]), 1)
    on("scalar", f)
    c0b = s_c0s.n

    def f(e):
        e.wait_ge(s_c0g.h, c0a)
        e.wait_ge(s_c0s.h, c0b)
        for c in range(NC):
            bq, bk = c * 32, 256 + c * 32
            e.tensor_copy(out=SMB[0:64, bq + 2 * c:bq + 2 * c + 1], in_=SM[0:64, 4:5])
            e.tensor_copy(out=SMB[64:128, bq + 2 * c + 1:bq + 2 * c + 2],
                          in_=SM[64:128, 4:5])
            e.tensor_copy(out=SMB[0:64, bk + 2 * c:bk + 2 * c + 1], in_=SM[0:64, 5:6])
            s_c0v.inc(e.tensor_copy(out=SMB[64:128, bk + 2 * c + 1:bk + 2 * c + 2],
                                   in_=SM[64:128, 5:6]), 1)
    on("vector", f)
    c0c = s_c0v.n

    def f(e):
        e.wait_ge(s_c0v.h, c0c)
        last = None
        for c in range(NC):
            for half in (0, 1):
                col = c * 32 + 16 + 2 * c + half
                lo, hi = (0, 64) if half == 0 else (64, 128)
                e.memset(SMB[lo:hi, col:col + 1], 1.0 / 64.0)
                last = e.memset(SMB[lo:hi, 256 + col:256 + col + 1], 1.0 / 64.0)
        s_c0g.inc(last, 1)
    on("gpsimd", f)

    def emit_ln(x_hbm, pref):
        sxp = [sem(pref + "x0"), sem(pref + "x1")]
        sv = sem(pref + "v")
        sq = sem(pref + "q")
        sm = sem(pref + "m")
        xv = x_hbm.rearrange("(c p) t -> c p t", p=128)

        def f(e):
            for c in range(NC):
                if c >= 2:
                    e.wait_ge(sv.h, c - 1)
                buf = (EB if c % 2 == 0 else KHV)[:, 0:2 * T].bitcast(F32)[:, 0:T]
                sxp[c % 2].inc(e.dma_start(out=buf, in_=xv[c]), 16)
        on("sync", f)

        def f(e):
            for c in range(NC):
                e.wait_ge(sxp[c % 2].h, 16 * (c // 2 + 1))
                buf = (EB if c % 2 == 0 else KHV)[:, 0:2 * T].bitcast(F32)[:, 0:T]
                sv.inc(e.tensor_copy(out=XN[:, c * T:(c + 1) * T], in_=buf), 1)
        on("vector", f)

        def f(e):
            for c in range(NC):
                e.wait_ge(sv.h, c + 1)
                if c >= 2:
                    e.wait_ge(sm.h, 2 * (c - 1))
                buf = CHK[:, 0:T].bitcast(BF16)[:, 0:T] if c % 2 == 0 else CH2[:, 0:1024].bitcast(BF16)[:, 0:T]
                sq.inc(e.activation(out=buf, in_=XN[:, c * T:(c + 1) * T],
                                    func=AF.Square), 1)
        on("scalar", f)

        def f(e):
            for c in range(NC):
                e.wait_ge(sv.h, c + 1)
                for nt in range(4):
                    e.matmul(PA[:, nt * 512:(nt + 1) * 512], ONE[:],
                             XN[:, c * T + nt * 512:c * T + (nt + 1) * 512],
                             start=(c == 0), stop=(c == NC - 1))
                e.wait_ge(sq.h, c + 1)
                buf = CHK[:, 0:T].bitcast(BF16)[:, 0:T] if c % 2 == 0 else CH2[:, 0:1024].bitcast(BF16)[:, 0:T]
                last = None
                for nt in range(4):
                    last = e.matmul(PB[:, nt * 512:(nt + 1) * 512], ONE[:],
                                    buf[:, nt * 512:(nt + 1) * 512],
                                    start=(c == 0), stop=(c == NC - 1))
                sm.inc(last, 2)
        on("tensor", f)
        m1 = sm.n
        r_m = ROWS[0:1, 0:T]
        r_var = CHK[0:1, 0:T]
        r_isg = EB[0:1, 0:2 * T].bitcast(F32)[:, 0:T]
        r_miv = KHV[0:1, 0:2 * T].bitcast(F32)[:, 0:T]
        rb_isg = QP[0:1, 0:T]
        rb_miv = QP[0:1, T:2 * T]

        def f(e):
            e.wait_ge(sm.h, m1)
            e.mul(out=r_m, in_=PA[0:1, 0:T], mul=1.0 / 1024.0)
            sq.inc(e.mul(out=r_var, in_=PB[0:1, 0:T], mul=1.0 / 1024.0), 1)
        on("scalar", f)
        q1 = sq.n

        def f(e):
            e.wait_ge(sq.h, q1)
            e.tensor_tensor(out=r_isg, in0=r_m, in1=r_m, op=ALU.mult)
            sv.inc(e.tensor_sub(out=r_var, in0=r_var, in1=r_isg), 1)
        on("vector", f)
        v1 = sv.n

        def f(e):
            e.wait_ge(sv.h, v1)
            e.activation(out=r_isg, in_=r_var, func=AF.Ln,
                                bias=SM[0:1, 96:97])
            sq.inc(e.activation(out=r_isg, in_=r_isg, func=AF.Exp,
                                scale=-0.5), 1)
        on("scalar", f)
        q2 = sq.n

        def f(e):
            e.wait_ge(sq.h, q2)
            e.tensor_tensor(out=r_miv, in0=r_m, in1=r_isg, op=ALU.mult)
            e.tensor_copy(out=rb_isg, in_=r_isg)
            sv.inc(e.tensor_copy(out=rb_miv, in_=r_miv), 1)
        on("vector", f)
        v2 = sv.n

        def f(e):
            e.wait_ge(sv.h, v2)
            last = None
            for nt in range(4):
                e.matmul(PA[:, nt * 512:(nt + 1) * 512], ONE[0:1, :],
                         rb_isg[:, nt * 512:(nt + 1) * 512], start=True, stop=True)
                last = e.matmul(PB[:, nt * 512:(nt + 1) * 512], ONE[0:1, :],
                                rb_miv[:, nt * 512:(nt + 1) * 512], start=True,
                                stop=True)
            sm.inc(last, 1)
        on("tensor", f)
        m2 = sm.n

        def f(e):
            e.wait_ge(sm.h, m2)
            for c in range(NC):
                e.tensor_tensor(out=CHK[:, 0:T], in0=XN[:, c * T:(c + 1) * T],
                                       in1=PA[:, 0:T], op=ALU.mult)
                sv.inc(e.tensor_tensor(out=XN[:, c * T:(c + 1) * T],
                                       in0=CHK[:, 0:T], in1=PB[:, 0:T],
                                       op=ALU.subtract), 1)
        on("vector", f)


    def ckpt(name):
        if STOP_AFTER == name:
            def f(e):
                e.wait_ge(s_st.h, s_st.n)
                e.wait_ge(s_dbg.h, s_dbg.n)
                s_fin2 = sem("s_fin2")
                for dc in range(NC):
                    v = XN[:, dc * T:(dc + 1) * T].bitcast(F32)[:, 0:T // 2]
                    s_fin2.inc(e.dma_start(
                        out=out_fm.rearrange("(c p) t -> c p t", p=128)[dc][:, 0:T // 2],
                        in_=v), 16)
                    s_fin2.inc(e.dma_start(
                        out=out_fm.rearrange("(c p) t -> c p t", p=128)[dc][:, T // 2:T],
                        in_=v), 16)
                e.wait_ge(s_fin2.h, s_fin2.n)
            on("sync", f)
            barrier()
            raise _Stop(b)

    emit_ln(x_fm, "l0")
    barrier()
    ckpt("ln0")
    dbg("xn", XN[:, 0:T], [128, T], BF16)

    s_qkp = [sem("s_qk0"), sem("s_qk1")]

    def qk_evac(s_m, s_e):
        def f(e):
            for fc in range(16):
                e.wait_ge(s_m.h, fc + 1)
                if fc >= 2:
                    e.wait_ge(s_qkp[fc % 2].h, 16 * (fc // 2))
                src = PA if fc % 2 == 0 else PB
                stg = EB[:, (fc % 2) * T:(fc % 2) * T + T]
                e.activation(out=stg, in_=src[:, 0:T], func=AF.Identity,
                                    bias=SM[:, 8 + fc:9 + fc], scale=1.0)
                sqdst = (HA if fc < 8 else HB)[:, (fc % 8) * T:(fc % 8) * T + T]
                s_e.inc(e.activation(out=sqdst, in_=stg, func=AF.Square), 1)
        on("scalar", f)
        e0 = s_e.n - 16

        def f(e):
            for fc in range(16):
                e.wait_ge(s_e.h, e0 + fc + 1)
                stg = EB[:, (fc % 2) * T:(fc % 2) * T + T]
                s_qkp[fc % 2].inc(e.dma_start(
                    out=qkraw_sp.rearrange("(f p) t -> f p t", p=128)[fc],
                    in_=stg), 16)
        on("sync", f)

    stream_gemm("qkv", 16, lambda fc: rearr_w(wqkvT)[:, :, fc * 128:(fc + 1) * 128],
                xn_rhs, qk_evac)
    barrier()
    ckpt("qkv")

    s_p4m = sem("s_p4m")
    s_p4s = sem("s_p4s")
    s_p4v = sem("s_p4v")

    def f(e):
        last = None
        for ts in range(16):
            for c in range(NC):
                last = e.matmul(PA[:, ts * 32:(ts + 1) * 32],
                                HA[:, c * T + ts * 128:c * T + (ts + 1) * 128],
                                SMB[:, c * 32:(c + 1) * 32],
                                start=(c == 0), stop=(c == NC - 1))
                last = e.matmul(PB[:, ts * 32:(ts + 1) * 32],
                                HB[:, c * T + ts * 128:c * T + (ts + 1) * 128],
                                SMB[:, 256 + c * 32:256 + (c + 1) * 32],
                                start=(c == 0), stop=(c == NC - 1))
        s_p4m.inc(last, 1)
    on("tensor", f)
    p4_mm = s_p4m.n

    def f(e):
        e.wait_ge(s_p4m.h, p4_mm)
        for ts in range(16):
            e.activation(out=CHK[:, ts * 16:(ts + 1) * 16],
                         in_=PA[:, ts * 32 + 16:ts * 32 + 32], func=AF.Ln, bias=SM[:, 97:98])
            last_ln = e.activation(out=CH2[:, ts * 16:(ts + 1) * 16],
                         in_=PB[:, ts * 32 + 16:ts * 32 + 32], func=AF.Ln, bias=SM[:, 97:98])
        last_ln
        for ts in range(16):
            e.activation(out=CHK[:, 256 + ts * 16:256 + (ts + 1) * 16],
                         in_=CHK[:, ts * 16:(ts + 1) * 16], func=AF.Exp, scale=-0.5)
            e.activation(out=CH2[:, 256 + ts * 16:256 + (ts + 1) * 16],
                         in_=CH2[:, ts * 16:(ts + 1) * 16], func=AF.Exp, scale=-0.5)
            e.activation(out=CHK[:, 512 + ts * 16:512 + (ts + 1) * 16],
                         in_=CHK[:, ts * 16:(ts + 1) * 16], func=AF.Exp, scale=-1.0)
            s_p4s.inc(e.activation(out=CH2[:, 512 + ts * 16:512 + (ts + 1) * 16],
                                  in_=CH2[:, ts * 16:(ts + 1) * 16], func=AF.Exp,
                                  scale=-1.0), 1)
    on("scalar", f)
    p4_sc = s_p4s.n

    def f(e):
        e.wait_ge(s_p4s.h, p4_sc)
        for ts in range(16):
            e.tensor_tensor(out=CHK[:, 512 + ts * 16:512 + (ts + 1) * 16],
                            in0=CHK[:, 512 + ts * 16:512 + (ts + 1) * 16],
                            in1=PA[:, ts * 32:ts * 32 + 16], op=ALU.mult)
            e.tensor_tensor(out=CH2[:, 512 + ts * 16:512 + (ts + 1) * 16],
                            in0=CH2[:, 512 + ts * 16:512 + (ts + 1) * 16],
                            in1=PB[:, ts * 32:ts * 32 + 16], op=ALU.mult)
        e.tensor_copy(out=SM[0:1, 6:7], in_=SM[0:1, 96:97])
        for buf in (CHK, CH2):
            n = 16
            while n > 1:
                last_l = None
                for i in range(n // 2):
                    a0 = 512 + i * 16
                    a1 = 512 + (i + n // 2) * 16
                    last_l = e.tensor_tensor(out=buf[:, a0:a0 + 16],
                                             in0=buf[:, a0:a0 + 16],
                                             in1=buf[:, a1:a1 + 16], op=ALU.max)
                last_l
                n //= 2
        s_p4v.inc(e.tensor_copy(out=SM[0:1, 6:7], in_=SM[0:1, 96:97]), 1)
    on("vector", f)
    p4_ve = s_p4v.n

    def f(e):
        e.wait_ge(s_p4v.h, p4_ve)
        last = None
        for ts in range(16):
            last = e.transpose(PA[0:16, ts * 128:(ts + 1) * 128],
                               CHK[:, 256 + ts * 16:256 + (ts + 1) * 16], IDF[:])
            last = e.transpose(PB[0:16, ts * 128:(ts + 1) * 128],
                               CH2[:, 256 + ts * 16:256 + (ts + 1) * 16], IDF[:])
        s_p4m.inc(last, 1)
    on("tensor", f)
    p4_tp = s_p4m.n

    def f(e):
        e.wait_ge(s_p4m.h, p4_tp)
        e.tensor_copy(out=RW2[0:16, 0:T], in_=PA[0:16, 0:2048])
        s_p4v.inc(e.tensor_copy(out=QP[0:16, 0:T], in_=PB[0:16, 0:2048]), 1)
    on("vector", f)
    p4_v2 = s_p4v.n

    def f(e):
        e.wait_ge(s_p4v.h, p4_v2)
        last = e.transpose(PA[0:16, 0:128], CHK[:, 512:528], IDF[:])
        last = e.transpose(PB[0:16, 0:128], CH2[:, 512:528], IDF[:])
        s_p4m.inc(last, 1)
    on("tensor", f)
    p4_t2 = s_p4m.n

    def f(e):
        e.wait_ge(s_p4m.h, p4_t2)
        e.reduce_max(SM[0:16, 60:61], PA[0:16, 0:128], axis=AX.X)
        s_p4v.inc(e.reduce_max(SM[0:16, 61:62], PB[0:16, 0:128], axis=AX.X), 1)
    on("vector", f)
    p4_v3 = s_p4v.n

    def f(e):
        e.wait_ge(s_p4v.h, p4_v3)
        last = e.transpose(PA[0:1, 0:16], SM[0:16, 60:61], IDF[0:16, 0:16])
        last = e.transpose(PA[0:1, 64:80], SM[0:16, 61:62], IDF[0:16, 0:16])
        s_p4m.inc(last, 1)
    on("tensor", f)
    p4_t3 = s_p4m.n

    def f(e):
        e.wait_ge(s_p4m.h, p4_t3)
        e.tensor_copy(out=row(R_QM, 0, 16), in_=PA[0:1, 0:16])
        s_p4v.inc(e.tensor_copy(out=row(R_KM, 0, 16), in_=PA[0:1, 64:80]), 1)
    on("vector", f)
    p4_v4 = s_p4v.n

    def f(e):
        e.wait_ge(s_p4v.h, p4_v4)
        s_st.inc(e.dma_start(out=kmax_src, in_=row(R_KM, 0, 16)), 16)
    on("sync", f)
    barrier()
    ckpt("p4")
    dbg("irmsq", RW2[0:16, 0:T], [16, T], BF16)
    dbg("qn2max", row(R_QM, 0, 16), [1, 16], F32)
    dbg("kn2max", row(R_KM, 0, 16), [1, 16], F32)

    s_v = sem("s_v")
    s_vm = sem("s_vm")
    s_vev = sem("s_vev")
    s_vstp = [sem("s_vst0"), sem("s_vst1")]

    def f(e):
        e.wait_ge(s_p4m.h, p4_mm)
        s_v.inc(e.dma_start(
            out=HA[:, 0:8192].rearrange("p (c f) -> p c f", c=NC),
            in_=rearr_w(wqkvT)[:, :, 2048:3072]), 16)
    on("gpsimd", f)
    v_w = s_v.n

    def f(e):
        e.wait_ge(s_v.h, v_w)
        wv = HA[:, 0:8192].rearrange("p (c f) -> p c f", c=NC)
        for ts in range(16):
            if ts >= 2:
                e.wait_ge(s_vev.h, ts - 1)
            dst = PA if ts % 2 == 0 else PB
            last = None
            for nt in range(2):
                for c in range(NC):
                    last = e.matmul(dst[:, nt * 512:(nt + 1) * 512],
                                    XN[:, c * T + ts * 128:c * T + (ts + 1) * 128],
                                    wv[:, c, nt * 512:(nt + 1) * 512],
                                    start=(c == 0), stop=(c == NC - 1))
            s_vm.inc(last, 1)
    on("tensor", f)

    def f(e):
        for ts in range(16):
            e.wait_ge(s_vm.h, ts + 1)
            if ts >= 2:
                e.wait_ge(s_vstp[ts % 2].h, 16 * (ts // 2))
            src = PA if ts % 2 == 0 else PB
            stg = KHV[:, (ts % 2) * T:(ts % 2) * T + 1024]
            s_vev.inc(e.tensor_copy(out=stg, in_=src[:, 0:1024]), 1)
    on("vector", f)

    def f(e):
        for ts in range(16):
            e.wait_ge(s_vev.h, ts + 1)
            stg = KHV[:, (ts % 2) * T:(ts % 2) * T + 1024]
            s_vstp[ts % 2].inc(e.dma_start(
                out=v_src.rearrange("(s p) f -> s p f", p=128)[ts], in_=stg), 16)
    on("sync", f)
    barrier()
    ckpt("p3")

    s_p5p = [sem("s_p5a"), sem("s_p5b")]
    s_p5m = sem("s_p5m")
    s_p5v = sem("s_p5v")
    s_p5sp = [sem("s_p5s0"), sem("s_p5s1")]

    def f(e):
        e.wait_ge(s_qkp[0].h, s_qkp[0].n)
        e.wait_ge(s_qkp[1].h, s_qkp[1].n)
        for fc in range(16):
            if fc >= 2:
                e.wait_ge(s_p5v.h, 2 * (fc - 1))
                e.wait_ge(s_p5sp[fc % 2].h, 16 * (fc // 2))
            s_p5p[fc % 2].inc(e.dma_start(
                out=EB[:, (fc % 2) * T:(fc % 2) * T + T],
                in_=qkraw_sp.rearrange("(f p) t -> f p t", p=128)[fc]), 16)
    on("sync", f)

    def f(e):
        for fc in range(16):
            c = fc % 8
            if fc >= 2:
                e.wait_ge(s_p5v.h, 2 * (fc - 1))
            sel = ESEL[0:16, :]
            rows_ = RW2[0:16, :] if fc < 8 else QP[0:16, :]
            dst = PA if fc % 2 == 0 else PB
            last = None
            for nt in range(4):
                last = e.matmul(dst[:, nt * 512:(nt + 1) * 512],
                                sel[:, c * 128:(c + 1) * 128],
                                rows_[:, nt * 512:(nt + 1) * 512],
                                start=True, stop=True)
            s_p5m.inc(last, 1)
    on("tensor", f)

    def f(e):
        for fc in range(16):
            e.wait_ge(s_p5m.h, fc + 1)
            e.wait_ge(s_p5p[fc % 2].h, 16 * (fc // 2 + 1))
            if fc >= 2:
                e.wait_ge(s_p5sp[fc % 2].h, 16 * (fc // 2))
            src = PA if fc % 2 == 0 else PB
            stg = EB[:, (fc % 2) * T:(fc % 2) * T + T]
            wcol = SM[:, 0:1] if fc < 8 else SM[:, 1:2]
            e.tensor_tensor(out=stg, in0=stg, in1=src[:, 0:T], op=ALU.mult)
            s_p5v.inc(e.tensor_scalar(out=stg, in0=stg, scalar1=wcol,
                                      scalar2=None, op0=ALU.mult), 2)
    on("vector", f)
    p5v0 = s_p5v.n - 32

    def f(e):
        for fc in range(16):
            e.wait_ge(s_p5v.h, p5v0 + 2 * (fc + 1))
            stg = EB[:, (fc % 2) * T:(fc % 2) * T + T]
            if fc < 8:
                dst = qt_sp.rearrange("(f p) t -> f p t", p=128)[fc]
            else:
                dst = kt_src.rearrange("(f p) t -> f p t", p=128)[fc - 8]
            s_p5sp[fc % 2].inc(e.dma_start(out=dst, in_=stg), 16)
    on("gpsimd", f)
    barrier()
    ckpt("p5")

    def f(e):
        e.wait_ge(s_p5sp[0].h, s_p5sp[0].n)
        e.wait_ge(s_p5sp[1].h, s_p5sp[1].n)
        e.wait_ge(s_vstp[0].h, s_vstp[0].n)
        e.wait_ge(s_vstp[1].h, s_vstp[1].n)
        e.wait_ge(s_st.h, s_st.n)
        s_cc.inc(e.collective_compute("AllGather", ALU.bypass,
                                      replica_groups=PAIRS,
                                      ins=[kt_src], outs=[kt_dst]), 1)
        s_cc.inc(e.collective_compute("AllGather", ALU.bypass,
                                      replica_groups=PAIRS,
                                      ins=[v_src], outs=[v_dst]), 1)
        s_cc.inc(e.collective_compute("AllGather", ALU.bypass,
                                      replica_groups=PAIRS,
                                      ins=[kmax_src], outs=[kmax_dst]), 1)
    on("gpsimd", f)
    cc_l0 = s_cc.n

    s_gw = sem("s_gw")
    s_gm = sem("s_gm")
    s_gv = sem("s_gv")
    s_gg = sem("s_gg")
    s_g2 = sem("s_g2")

    def f(e):
        s_gw.inc(e.dma_start(out=SMB[:, 540:604].rearrange("p (c f) -> p c f", c=NC),
                             in_=rearr_w(gateWT)), 16)
    on("gpsimd", f)
    g_w = s_gw.n

    def f(e):
        e.wait_ge(s_gw.h, g_w)
        last = None
        for ts in range(16):
            for c in range(NC):
                last = e.matmul(PA[:, ts * 8:(ts + 1) * 8],
                                XN[:, c * T + ts * 128:c * T + (ts + 1) * 128],
                                SMB[:, 540 + c * 8:540 + (c + 1) * 8],
                                start=(c == 0), stop=(c == NC - 1))
        s_gm.inc(last, 1)
    on("tensor", f)
    g_mm = s_gm.n

    def f(e):
        for ts in range(16):
            e.wait_ge(s_gm.h, g_mm)
            if ts > 0:
                e.wait_ge(s_g2.h, 2 * ts)
            tile = PA[:, ts * 8:(ts + 1) * 8]
            e.reduce_max(SM[:, 56:57], tile, axis=AX.X)
            s_gv.inc(e.tensor_scalar(out=CH2[:, ts * 8:(ts + 1) * 8], in0=tile,
                                     scalar1=SM[:, 56:57], scalar2=None,
                                     op0=ALU.subtract), 1)
            e.wait_ge(s_g2.h, 2 * ts + 2)
            tile2 = CHK[:, ts * 8:(ts + 1) * 8]
            e.tensor_scalar(out=tile2, in0=tile2, scalar1=SM[:, 57:58],
                                   scalar2=None, op0=ALU.mult)
            e.tensor_scalar(out=CH2[:, 256 + ts * 8:264 + ts * 8],
                                   in0=tile2, scalar1=0.3, scalar2=None,
                                   op0=ALU.is_gt)
            e.reduce_sum(SM[:, 58:59], CH2[:, 256 + ts * 8:264 + ts * 8],
                                axis=AX.X)
            e.tensor_tensor(out=SM[:, 80 + ts:81 + ts], in0=SM[:, 58:59],
                                   in1=tile2[:, 1:2], op=ALU.mult)
            e.tensor_scalar(out=SM[:, 58:59], in0=SM[:, 58:59],
                                   scalar1=-1.0, scalar2=8.0, op0=ALU.mult,
                                   op1=ALU.add)
            s_gv.inc(e.tensor_tensor(out=SM[:, 64 + ts:65 + ts], in0=SM[:, 58:59],
                                     in1=tile2[:, 0:1], op=ALU.mult), 1)
    on("vector", f)

    def f(e):
        for ts in range(16):
            e.wait_ge(s_gv.h, 2 * ts + 1)
            act1 = e.activation(out=CHK[:, ts * 8:(ts + 1) * 8],
                                in_=CH2[:, ts * 8:(ts + 1) * 8], func=AF.Exp,
                                accum_out=SM[:, 57:58])
            s_g2.inc(act1, 1)
            act1
            e.activation(out=SM[:, 57:58], in_=SM[:, 57:58], func=AF.Ln)
            s_g2.inc(e.activation(out=SM[:, 57:58], in_=SM[:, 57:58], func=AF.Exp,
                                  scale=-1.0), 1)
    on("scalar", f)
    g_done = s_gv.n

    def f(e):
        e.wait_ge(s_gv.h, g_done)
        last = e.transpose(PB[0:16, 0:128], SM[:, 64:80], IDF[:])
        last = e.transpose(PB[0:16, 128:256], SM[:, 80:96], IDF[:])
        s_gm.inc(last, 1)
    on("tensor", f)
    g_tp = s_gm.n

    def f(e):
        e.wait_ge(s_gm.h, g_tp)
        s_gv.inc(e.tensor_copy(out=QP[0:16, 0:256], in_=PB[0:16, 0:256]), 1)
    on("vector", f)
    g_cp = s_gv.n

    def f(e):
        e.wait_ge(s_gv.h, g_cp)
        s_gg.inc(e.dma_start(out=RW2[0:1, 0:T], in_=QP[0:16, 0:128]), 16)
        s_gg.inc(e.dma_start(out=RW2[32:33, 0:T], in_=QP[0:16, 128:256]), 16)
    on("gpsimd", f)
    g_fl = s_gg.n

    def f(e):
        e.wait_ge(s_gg.h, g_fl)
        last = None
        for nt in range(4):
            e.matmul(PA[:, nt * 512:(nt + 1) * 512], ONE[0:1, :],
                     RW2[0:1, nt * 512:(nt + 1) * 512], start=True, stop=True)
            last = e.matmul(PB[:, nt * 512:(nt + 1) * 512], ONE[32:33, :],
                            RW2[32:33, nt * 512:(nt + 1) * 512], start=True,
                            stop=True)
        s_gm.inc(last, 1)
    on("tensor", f)
    g_r0 = s_gm.n

    def f(e):
        e.wait_ge(s_gm.h, g_r0)
        e.tensor_copy(out=QP[:, 0:T], in_=PA[:, 0:T])
        s_gv.inc(e.tensor_copy(out=QP[:, T:2 * T], in_=PB[:, 0:T]), 1)
    on("vector", f)
    barrier()
    ckpt("gating")
    dbg("srep0", QP[:, 0:T], [128, T], BF16)

    s_ex = sem("s_ex")
    s_exwp = [sem("s_exw0"), sem("s_exw1")]
    s_exv = sem("s_exv")

    def f(e):
        for hc in range(NC):
            for which in range(2):
                idx = 2 * hc + which
                if idx >= 2:
                    e.wait_ge(s_ex.h, idx - 1)
                src = e0T if which == 0 else e1T
                s_exwp[which].inc(e.dma_start(
                    out=WB1[:, which * 1024:which * 1024 + 1024]
                        .rearrange("p (c f) -> p c f", c=NC),
                    in_=rearr_w(src)[:, :, hc * 128:(hc + 1) * 128]), 16)
    on("gpsimd", f)

    def f(e):
        for hc in range(NC):
            for which in range(2):
                idx = 2 * hc + which
                e.wait_ge(s_exwp[which].h, 16 * (hc + 1))
                if hc >= 1 and which == 0:
                    e.wait_ge(s_exv.h, 3 * hc)
                wv = WB1[:, which * 1024:which * 1024 + 1024] \
                    .rearrange("p (c f) -> p c f", c=NC)
                dst = PA if which == 0 else PB
                last = None
                for nt in range(4):
                    for c in range(NC):
                        last = e.matmul(dst[:, nt * 512:(nt + 1) * 512], wv[:, c, :],
                                        XN[:, c * T + nt * 512:c * T + (nt + 1) * 512],
                                        start=(c == 0), stop=(c == NC - 1))
                s_ex.inc(last, 1)
    on("tensor", f)
    ex_mm0 = s_ex.n - 16

    def f(e):
        for hc in range(NC):
            e.wait_ge(s_ex.h, ex_mm0 + 2 * hc + 2)
            e.tensor_tensor(out=CHK[:, 0:T], in0=QP[:, 0:T], in1=PA[:, 0:T],
                            op=ALU.mult)
            e.tensor_tensor(out=VO[:, 0:4096].bitcast(F32)[:, 0:T],
                                   in0=QP[:, T:2 * T], in1=PB[:, 0:T],
                                   op=ALU.mult)
            s_exv.inc(e.tensor_add(out=HA[:, hc * T:(hc + 1) * T], in0=CHK[:, 0:T],
                                   in1=VO[:, 0:4096].bitcast(F32)[:, 0:T]), 3)
    on("vector", f)
    barrier()
    ckpt("moef")
    dbg("moeF", HA[:, 0:T], [128, T], BF16)

    s_mop = [sem("s_mo0"), sem("s_mo1")]

    def mo_evac(s_m, s_e):
        def f(e):
            for dc in range(NC):
                e.wait_ge(s_m.h, dc + 1)
                if dc >= 2:
                    e.wait_ge(s_mop[dc % 2].h, 16 * (dc // 2))
                src = PA if dc % 2 == 0 else PB
                s_e.inc(e.tensor_copy(out=EB[:, (dc % 2) * T:(dc % 2) * T + T],
                                      in_=src[:, 0:T]), 1)
        on("vector", f)
        e0 = s_e.n - NC

        def f(e):
            for dc in range(NC):
                e.wait_ge(s_e.h, e0 + dc + 1)
                s_mop[dc % 2].inc(e.dma_start(
                    out=moa_sp.rearrange("(c p) t -> c p t", p=128)[dc],
                    in_=EB[:, (dc % 2) * T:(dc % 2) * T + T]), 16)
        on("sync", f)

    stream_gemm("mow", NC, lambda dc: rearr_w(outWT)[:, :, dc * 128:(dc + 1) * 128],
                lambda c, nt: HA[:, c * T + nt * 512:c * T + (nt + 1) * 512],
                mo_evac)
    barrier()
    ckpt("moe")

    s_b = sem("s_b")
    s_bv = sem("s_bv")
    s_bs = sem("s_bs")
    s_bm = sem("s_bm")

    def f(e):
        e.wait_ge(s_cc.h, cc_l0)
        s_b.inc(e.dma_start(out=row(R_TMP2, 0, 16), in_=kmax_dst[0]), 16)
        s_b.inc(e.dma_start(out=row(R_TMP2, 16, 32), in_=kmax_dst[1]), 16)
    on("sync", f)
    b_ld = s_b.n

    def f(e):
        e.wait_ge(s_b.h, b_ld)
        e.tensor_tensor(out=row(R_TMP2, 0, 16), in0=row(R_TMP2, 0, 16),
                               in1=row(R_TMP2, 16, 32), op=ALU.max)
        s_bv.inc(e.tensor_tensor(out=row(R_BND, 0, 16), in0=row(R_QM, 0, 16),
                                 in1=row(R_TMP2, 0, 16), op=ALU.mult), 1)
    on("vector", f)
    b_v = s_bv.n

    def f(e):
        e.wait_ge(s_bv.h, b_v)
        s_bs.inc(e.activation(out=row(R_BND, 0, 16), in_=row(R_BND, 0, 16),
                              func=AF.Sqrt), 1)
    on("scalar", f)
    b_s = s_bs.n

    def f(e):
        e.wait_ge(s_bs.h, b_s)
        e.tensor_tensor(out=row(R_TMP2, 0, 8), in0=rows(R_BND, 0, 16, 2),
                               in1=rows(R_BND, 1, 16, 2), op=ALU.max)
        e.tensor_scalar(out=row(R_TMP2, 0, 8), in0=row(R_TMP2, 0, 8),
                               scalar1=-1.0, scalar2=60.0, op0=ALU.mult,
                               op1=ALU.add)
        s_bv.inc(e.tensor_copy(out=RW2[0:1, 0:8], in_=row(R_TMP2, 0, 8)), 1)
    on("vector", f)
    b_v2 = s_bv.n

    def f(e):
        e.wait_ge(s_bv.h, b_v2)
        last = None
        for pr in range(8):
            last = e.matmul(PA[:, 1536 + pr:1537 + pr], ONE[0:1, :],
                            RW2[0:1, pr:pr + 1], start=True, stop=True)
        s_bm.inc(last, 1)
    on("tensor", f)
    b_m = s_bm.n

    def f(e):
        e.wait_ge(s_bm.h, b_m)
        s_bv.inc(e.tensor_copy(out=SM[:, 48:56], in_=PA[:, 1536:1544]), 1)
    on("vector", f)
    barrier()
    ckpt("bnd")
    dbg("bnd", SM[:, 48:56], [128, 8], F32)

    s_apf = sem("s_apf")
    s_amm = sem("s_amm")
    s_aex = sem("s_aex")
    s_aevs = sem("s_aevs")
    s_aevv = sem("s_aevv")
    s_aevm = sem("s_aevm")

    s_vo = sem("s_vo")

    def f(e):
        e.memset(VO[:], 0.0)
        last = None
        for kt in range(32):
            last = e.memset(VO[:, kt * 65 + 64:kt * 65 + 65], 1.0)
            last = e.memset(VO[:, 2112 + kt * 128:2112 + kt * 128 + 1], 1.0)
        s_vo.inc(last, 1)
    on("gpsimd", f)

    KHVv = KHV[:, 0:4096]

    for pr in range(8):
        h0, h1 = 2 * pr, 2 * pr + 1

        def f(e, pr=pr, h0=h0, h1=h1):
            e.wait_ge(s_cc.h, cc_l0)
            e.wait_ge(s_vo.h, 1)
            e.wait_ge(s_p5sp[0].h, s_p5sp[0].n)
            e.wait_ge(s_p5sp[1].h, s_p5sp[1].n)
            if pr >= 1:
                e.wait_ge(s_aevv.h, 4 * pr)
                e.wait_ge(s_amm.h, 66 * pr)
            s_apf.inc(e.dma_start(
                out=KHVv[0:64, :].rearrange("p (g t) -> p g t", g=2),
                in_=kt_dst[:, h0 * 64:(h0 + 1) * 64, :].transpose([1, 0, 2])), 16)
            s_apf.inc(e.dma_start(
                out=KHVv[64:128, :].rearrange("p (g t) -> p g t", g=2),
                in_=kt_dst[:, h1 * 64:(h1 + 1) * 64, :].transpose([1, 0, 2])), 16)
            s_apf.inc(e.dma_start(
                out=QP[:, 0:T],
                in_=qt_sp.rearrange("(x p) t -> x p t", p=128)[pr]), 16)
            s_apf.inc(e.dma_start(
                out=VO[:, 0:2080].rearrange("p (kt f) -> p kt f", f=65)[:, :, 0:64],
                in_=v_dst.rearrange("g (s p) f -> p (g s) f", p=128)
                    [:, :, h0 * 64:(h0 + 1) * 64]), 16)
            e.wait_ge(s_apf.h, s_apf.n)
            s_apf.inc(e.dma_start(
                out=VO[:, 2112:6208].rearrange("p (kt f) -> p kt f", f=128)
                    [:, :, 64:128],
                in_=v_dst.rearrange("g (s p) f -> p (g s) f", p=128)
                    [:, :, h1 * 64:(h1 + 1) * 64]), 16)
        on("sync", f)
        pf_now = s_apf.n

        for qw in range(2):
            ex0 = s_aex.n
            mm0 = s_amm.n
            aev0 = s_aevv.n

            def f(e, pr=pr, qw=qw, ex0=ex0, pf_now=pf_now, aev0=aev0):
                e.wait_ge(s_apf.h, pf_now)
                e.wait_ge(s_aevv.h, aev0)
                for kt in range(32):
                    if kt >= 1:
                        e.wait_ge(s_aex.h, ex0 + kt)
                    qs0 = QP[0:64, qw * 1024:(qw + 1) * 1024]
                    qs1 = QP[64:128, qw * 1024:(qw + 1) * 1024]
                    last = None
                    for nt in range(2):
                        e.matmul(PA[:, nt * 512:(nt + 1) * 512],
                                 KHVv[0:64, kt * 128:(kt + 1) * 128],
                                 qs0[:, nt * 512:(nt + 1) * 512],
                                 start=True, stop=True, tile_position=(0, 0))
                        last = e.matmul(PA[:, 1024 + nt * 512:1024 + (nt + 1) * 512],
                                        KHVv[64:128, kt * 128:(kt + 1) * 128],
                                        qs1[:, nt * 512:(nt + 1) * 512],
                                        start=True, stop=True,
                                        tile_position=(64, 0))
                    s_amm.inc(last, 1)
                    if kt >= 1:
                        ek = kt - 1
                        ebuf = EB[:, (ek % 2) * T:(ek % 2) * T + T]
                        for nt in range(2):
                            e.matmul(PB[0:65, nt * 512:(nt + 1) * 512],
                                     VO[:, ek * 65:ek * 65 + 65],
                                     ebuf[:, nt * 512:(nt + 1) * 512],
                                     start=(ek == 0), stop=False)
                            e.matmul(PB[:, 1024 + nt * 512:1024 + (nt + 1) * 512],
                                     VO[:, 2112 + ek * 128:2112 + (ek + 1) * 128],
                                     ebuf[:, 1024 + nt * 512:1024 + (nt + 1) * 512],
                                     start=(ek == 0), stop=False)
                e.wait_ge(s_aex.h, ex0 + 32)
                ek = 31
                ebuf = EB[:, (ek % 2) * T:(ek % 2) * T + T]
                last = None
                for nt in range(2):
                    e.matmul(PB[0:65, nt * 512:(nt + 1) * 512],
                             VO[:, ek * 65:ek * 65 + 65],
                             ebuf[:, nt * 512:(nt + 1) * 512],
                             start=False, stop=True)
                    last = e.matmul(PB[:, 1024 + nt * 512:1024 + (nt + 1) * 512],
                                    VO[:, 2112 + ek * 128:2112 + (ek + 1) * 128],
                                    ebuf[:, 1024 + nt * 512:1024 + (nt + 1) * 512],
                                    start=False, stop=True)
                s_amm.inc(last, 1)
            on("tensor", f)
            mm_after = s_amm.n

            def f(e, pr=pr, qw=qw, mm0=mm0):
                if pr == 0 and qw == 0:
                    e.wait_ge(s_mop[0].h, s_mop[0].n)
                    e.wait_ge(s_mop[1].h, s_mop[1].n)
                for kt in range(32):
                    e.wait_ge(s_amm.h, mm0 + kt + 1)
                    ebuf = EB[:, (kt % 2) * T:(kt % 2) * T + T]
                    s_aex.inc(e.activation(out=ebuf, in_=PA[:, 0:2048], func=AF.Exp,
                                           bias=SM[:, 48 + pr:49 + pr], scale=1.0), 1)
            on("scalar", f)

            def f(e, mm_after=mm_after):
                e.wait_ge(s_amm.h, mm_after)
                e.activation(out=CHK[64:65, 0:1024], in_=PB[64:65, 0:1024],
                                    func=AF.Sqrt)
                e.activation(out=CH2[64:65, 0:1024], in_=CHK[64:65, 0:1024],
                             func=AF.Ln)
                e.activation(out=CHK[64:65, 1024:2048], in_=CH2[64:65, 0:1024],
                             func=AF.Exp, scale=-1.0)
                e.activation(out=CH2[0:1, 0:1024], in_=PB[0:1, 1024:2048],
                                    func=AF.Sqrt)
                e.activation(out=CHK[0:1, 0:1024], in_=CH2[0:1, 0:1024],
                             func=AF.Ln)
                s_aevs.inc(e.activation(out=CHK[0:1, 0:1024], in_=CHK[0:1, 0:1024],
                                        func=AF.Exp, scale=-1.0), 1)
            on("scalar", f)
            a1 = s_aevs.n

            def f(e, a1=a1):
                e.wait_ge(s_aevs.h, a1)
                e.tensor_tensor(out=RW2[64:65, 0:1024],
                                in0=CHK[64:65, 1024:2048],
                                in1=CHK[64:65, 1024:2048], op=ALU.mult)
                s_aevv.inc(e.tensor_tensor(out=RW2[0:1, 1024:2048],
                                           in0=CHK[0:1, 0:1024],
                                           in1=CHK[0:1, 0:1024], op=ALU.mult), 1)
            on("vector", f)
            a2 = s_aevv.n

            def f(e, a2=a2):
                e.wait_ge(s_aevv.h, a2)
                e.matmul(PA[:, 0:512], ONE[64:65, :], RW2[64:65, 0:512],
                         start=True, stop=True)
                e.matmul(PA[:, 512:1024], ONE[64:65, :], RW2[64:65, 512:1024],
                         start=True, stop=True)
                e.matmul(PA[:, 1024:1536], ONE[0:1, :], RW2[0:1, 1024:1536],
                         start=True, stop=True)
                last = e.matmul(PA[:, 1536:2048], ONE[0:1, :], RW2[0:1, 1536:2048],
                                start=True, stop=True)
                s_aevm.inc(last, 1)
            on("tensor", f)
            a3 = s_aevm.n

            dbg_here = pr == 0 and qw == 0 and ("avps" in DBG)
            if dbg_here:
                d_pb = b.dout("dbg_avps", [128, 2048], F32)
                d_e = b.dout("dbg_e31", [128, 2048], BF16)
                d_khv = b.dout("dbg_khv", [128, 4096], BF16)
                d_qp = b.dout("dbg_qp", [128, 2048], BF16)
                d_vo = b.dout("dbg_vo", [128, 6208], BF16)
                d_inv = b.dout("dbg_inv", [128, 2048], BF16)
                s_dbgv = sem("s_dbgv")

                def fdbg(e):
                    e.wait_ge(s_dbgv.h, 1)
                    s_dbg.inc(e.dma_start(out=d_pb, in_=CHK[:, 0:2048]), 16)
                    s_dbg.inc(e.dma_start(out=d_e, in_=EB[:, T:2 * T]), 16)
                    s_dbg.inc(e.dma_start(out=d_khv, in_=KHVv), 16)
                    s_dbg.inc(e.dma_start(out=d_qp, in_=QP[:, 0:T]), 16)
                    s_dbg.inc(e.dma_start(out=d_vo, in_=VO[:]), 16)
                    s_dbg.inc(e.dma_start(out=d_inv, in_=RW2[:, 0:2048]), 16)
                on("sync", fdbg)

            def f(e, pr=pr, qw=qw, a3=a3, dbg_here=dbg_here):
                e.wait_ge(s_aevm.h, a3)
                if dbg_here:
                    s_dbgv.inc(e.tensor_copy(out=CHK[:, 0:2048],
                                             in_=PB[:, 0:2048]), 1)
                    e.wait_ge(s_dbg.h, 96)
                z0 = HB[0:64, pr * T + qw * 1024:pr * T + (qw + 1) * 1024]
                z1 = HB[64:128, pr * T + qw * 1024:pr * T + (qw + 1) * 1024]
                e.tensor_copy(out=z0, in_=PB[0:64, 0:1024])
                e.tensor_tensor(out=z0, in0=z0, in1=PA[0:64, 0:1024], op=ALU.mult)
                e.tensor_copy(out=z1, in_=PB[64:128, 1024:2048])
                s_aevv.inc(e.tensor_tensor(out=z1, in0=z1, in1=PA[64:128, 1024:2048],
                                           op=ALU.mult), 2)
            on("vector", f)

    barrier()
    ckpt("attn")
    dbg("zt", HB[:, 0:T], [128, T], BF16)

    s_wb = sem("s_wb")

    def f(e):
        s_wb.inc(e.tensor_copy(out=SMB[:, 516:524], in_=SM[:, 24:32]), 1)
    on("vector", f)
    wb_v = s_wb.n

    def wobv_evac(s_m, s_e):
        def f(e):
            for dc in range(NC):
                e.wait_ge(s_m.h, dc + 1)
                src = PA if dc % 2 == 0 else PB
                s_e.inc(e.tensor_copy(out=SM[:, 40 + dc:41 + dc], in_=src[:, 0:1]), 1)
        on("vector", f)

    _, s_wbe = stream_gemm(
        "wobv", NC, lambda dc: rearr_w(woT)[:, :, dc * 128:(dc + 1) * 128],
        lambda c, nt: SMB[:, 516 + c:517 + c], wobv_evac,
        nt_count=1, nt_size=1,
        wait_emit=lambda e: e.wait_ge(s_wb.h, wb_v))

    def f(e):
        e.wait_ge(s_wbe.h, NC)
        s_wb.inc(e.tensor_add(out=SM[:, 40:48], in0=SM[:, 40:48],
                              in1=SM[:, 32:40]), 1)
    on("vector", f)
    barrier()

    s_rxp = [sem("s_rx0"), sem("s_rx1")]
    s_rmp = [sem("s_rm0"), sem("s_rm1")]
    s_rv = sem("s_rv")
    s_rsp = [sem("s_rs0"), sem("s_rs1")]

    def att_evac(s_m, s_e):
        def f(e):
            e.wait_ge(s_mop[0].h, s_mop[0].n)
            e.wait_ge(s_mop[1].h, s_mop[1].n)
            for dc in range(NC + 2):
                if 2 <= dc:
                    sdc = dc - 2
                    e.wait_ge(s_rv.h, sdc + 1)
                    xbuf = CHK[:, 0:T] if sdc % 2 == 0 else VO[:, 0:4096].bitcast(F32)[:, 0:T]
                    s_rsp[sdc % 2].inc(e.dma_start(
                        out=x1sp.rearrange("(c p) t -> c p t", p=128)[sdc],
                        in_=xbuf), 16)
                if dc < NC:
                    if dc >= 2:
                        e.wait_ge(s_rsp[dc % 2].h, 16 * (dc // 2))
                    xbuf = CHK[:, 0:T] if dc % 2 == 0 else VO[:, 0:4096].bitcast(F32)[:, 0:T]
                    s_rxp[dc % 2].inc(e.dma_start(
                        out=xbuf,
                        in_=x_fm.rearrange("(c p) t -> c p t", p=128)[dc]), 16)
                    s_rmp[dc % 2].inc(e.dma_start(
                        out=KHV[:, (dc % 2) * T:(dc % 2) * T + T],
                        in_=moa_sp.rearrange("(c p) t -> c p t", p=128)[dc]), 16)
        on("sync", f)

        def f(e):
            for dc in range(NC):
                e.wait_ge(s_m.h, dc + 1)
                if dc >= 1:
                    e.wait_ge(s_rv.h, dc)
                src = PA if dc % 2 == 0 else PB
                s_e.inc(e.activation(out=EB[:, 0:2 * T].bitcast(F32)[:, 0:T],
                                     in_=src[:, 0:T], func=AF.Identity,
                                     bias=SM[:, 40 + dc:41 + dc], scale=1.0), 1)
        on("scalar", f)
        sc0 = s_e.n - NC

        def f(e):
            for dc in range(NC):
                e.wait_ge(s_e.h, sc0 + dc + 1)
                e.wait_ge(s_rxp[dc % 2].h, 16 * (dc // 2 + 1))
                e.wait_ge(s_rmp[dc % 2].h, 16 * (dc // 2 + 1))
                if dc >= 2:
                    e.wait_ge(s_rsp[dc % 2].h, 16 * (dc // 2))
                xbuf = CHK[:, 0:T] if dc % 2 == 0 else VO[:, 0:4096].bitcast(F32)[:, 0:T]
                att = EB[:, 0:2 * T].bitcast(F32)[:, 0:T]
                e.tensor_add(out=att, in0=att, in1=xbuf)
                s_rv.inc(e.scalar_tensor_tensor(
                    out=xbuf, in0=KHV[:, (dc % 2) * T:(dc % 2) * T + T],
                    scalar=1.0, in1=att, op0=ALU.mult, op1=ALU.add), 1)
        on("vector", f)

    stream_gemm("att", NC, lambda dc: rearr_w(woT)[:, :, dc * 128:(dc + 1) * 128],
                lambda c, nt: HB[:, c * T + nt * 512:c * T + (nt + 1) * 512],
                att_evac)
    barrier()
    ckpt("wo")

    rsp_tot = (s_rsp[0].n, s_rsp[1].n)

    def f(e):
        e.wait_ge(s_rsp[0].h, rsp_tot[0])
        e.wait_ge(s_rsp[1].h, rsp_tot[1])
    on("sync", f)
    emit_ln(x1sp, "l1")
    barrier()
    ckpt("ln1")
    dbg("xnl1", XN[:, 0:T], [128, T], BF16)

    s_fwp = [sem("s_fw0"), sem("s_fw1")]
    s_fdp = [sem("s_fd0"), sem("s_fd1")]
    s_fam = sem("s_fam")
    s_fae = sem("s_fae")
    s_fav = sem("s_fav")
    s_fbm = sem("s_fbm")
    s_fbe = sem("s_fbe")

    wgv = wgTi.rearrange("(c p) h -> p c h", p=128)
    wuv = wuTi.rearrange("(c p) h -> p c h", p=128)
    wdv = wdTi.rearrange("(k p) d -> k p d", p=128)

    def ffn_wslot(idx):
        if idx % 2 == 0:
            return WB3[:, 0:1024], WB3[:, 1024:2048]
        return WB3[:, 2048:3072], WB1[:, 0:1024]

    def f(e):
        for tg in range(4):
            for hs in range(24):
                idx = tg * 24 + hs
                if idx >= 2:
                    e.wait_ge(s_fam.h, idx - 1)
                wg_s, wu_s = ffn_wslot(idx)
                s_fwp[idx % 2].inc(e.dma_start(
                    out=wg_s.rearrange("p (c f) -> p c f", c=8),
                    in_=wgv[:, :, hs * 128:(hs + 1) * 128]), 16)
                s_fwp[idx % 2].inc(e.dma_start(
                    out=wu_s.rearrange("p (c f) -> p c f", c=8),
                    in_=wuv[:, :, hs * 128:(hs + 1) * 128]), 16)
            for hs in range(24):
                e.wait_ge(s_fam.h, tg * 24 + 24)
                if tg > 0 or hs >= 2:
                    e.wait_ge(s_fbm.h, tg * 24 + hs - 1)
                wd_s = WB1[:, 1024:2048] if hs % 2 == 0 else WB3[:, 0:1024]
                s_fdp[hs % 2].inc(e.dma_start(
                    out=wd_s, in_=wdv[hs]), 16)
    on("gpsimd", f)

    def f(e):
        for tg in range(4):
            for hs in range(24):
                idx = tg * 24 + hs
                e.wait_ge(s_fwp[idx % 2].h, 32 * (idx // 2 + 1))
                if idx >= 2:
                    e.wait_ge(s_fav.h, idx - 1)
                if hs == 0 and tg >= 1:
                    e.wait_ge(s_fbe.h, tg)
                wg_s, wu_s = ffn_wslot(idx)
                wgt = wg_s.rearrange("p (c f) -> p c f", c=8)
                wut = wu_s.rearrange("p (c f) -> p c f", c=8)
                g_ps = PA[:, (idx % 2) * 1024:(idx % 2) * 1024 + 512]
                u_ps = PA[:, (idx % 2) * 1024 + 512:(idx % 2) * 1024 + 1024]
                last = None
                for c in range(NC):
                    rhs = XN[:, c * T + tg * 512:c * T + (tg + 1) * 512]
                    e.matmul(g_ps, wgt[:, c, :], rhs,
                             start=(c == 0), stop=(c == NC - 1))
                    last = e.matmul(u_ps, wut[:, c, :], rhs,
                                    start=(c == 0), stop=(c == NC - 1))
                s_fam.inc(last, 1)
            for hs in range(24):
                e.wait_ge(s_fdp[hs % 2].h, 16 * (tg * 12 + hs // 2 + 1))
                e.wait_ge(s_fav.h, tg * 24 + hs + 1)
                if hs == 0:
                    e.wait_ge(s_fav.h, tg * 24 + 24)
                    if tg >= 1:
                        e.wait_ge(s_fbe.h, tg)
                wd_s = WB1[:, 1024:2048] if hs % 2 == 0 else WB3[:, 0:1024]
                last = None
                for fs in range(NC):
                    dst = (PA if fs < 4 else PB)[:, (fs % 4) * 512:
                                                 (fs % 4) * 512 + 512]
                    last = e.matmul(dst, wd_s[:, fs * 128:(fs + 1) * 128],
                                    HA[:, hs * 512:(hs + 1) * 512],
                                    start=(hs == 0), stop=(hs == 23))
                s_fbm.inc(last, 1)
    on("tensor", f)

    def f(e):
        for tg in range(4):
            for hs in range(24):
                idx = tg * 24 + hs
                e.wait_ge(s_fam.h, idx + 1)
                if idx >= 2:
                    e.wait_ge(s_fav.h, idx - 1)
                g_ps = PA[:, (idx % 2) * 1024:(idx % 2) * 1024 + 512]
                s_fae.inc(e.activation(
                    out=EB[:, (idx % 2) * 512:(idx % 2) * 512 + 512],
                    in_=g_ps, func=AF.Silu, scale=1.0 / 127.0), 1)
    on("scalar", f)

    def f(e):
        for tg in range(4):
            for hs in range(24):
                idx = tg * 24 + hs
                e.wait_ge(s_fae.h, idx + 1)
                if tg >= 1 and hs == 0:
                    e.wait_ge(s_fbm.h, tg * 24)
                u_ps = PA[:, (idx % 2) * 1024 + 512:(idx % 2) * 1024 + 1024]
                s_fav.inc(e.tensor_tensor(
                    out=HA[:, hs * 512:(hs + 1) * 512],
                    in0=u_ps, in1=EB[:, (idx % 2) * 512:(idx % 2) * 512 + 512],
                    op=ALU.mult), 1)
            def g(e, tg=tg):
                e.wait_ge(s_fbm.h, tg * 24 + 24)
                last = None
                for fs in range(NC):
                    src = (PA if fs < 4 else PB)[:, (fs % 4) * 512:
                                                 (fs % 4) * 512 + 512]
                    last = e.tensor_scalar(
                        out=HB[:, fs * T + tg * 512:fs * T + (tg + 1) * 512],
                        in0=src, scalar1=1.0 / 16129.0, scalar2=None,
                        op0=ALU.mult)
                s_fbe.inc(last, 1)
            g(e)
    on("vector", f)
    barrier()
    ckpt("l1ffn")
    dbg("ffn", HB[:, 0:T], [128, T], BF16)

    s_p3g = b.reuse("s_fw0")
    s_p3s = b.reuse("s_fae")
    s_p3v = b.reuse("s_fav")
    s_p3m = b.reuse("s_fam")
    s_p3y = b.reuse("s_apf")
    s_bst = b.reuse("s_rx0")

    def f(e):
        s_p3y.inc(e.dma_start(out=SM[0:1, 61:62], in_=eab[0:1, :]), 16)
        s_p3y.inc(e.dma_start(out=SM[0:1, 62:63], in_=eab[1:2, :]), 16)
        s_p3y.inc(e.dma_start(out=SM[0:1, 63:64], in_=eab[2:3, :]), 16)
    on("sync", f)

    def f(e):
        s_p3g.inc(e.dma_start(
            out=SMB[:, 524:548].rearrange("p (c e) -> p c e", c=8),
            in_=eag.rearrange("(c p) e -> p c e", p=128)), 16)
        s_p3g.inc(e.memset(SMB[0:64, 548:549], 1.0 / 64.0), 1)
    on("gpsimd", f)
    p3c_g = s_p3g.n
    p3c_y = s_p3y.n

    nrm_m0 = s_p3m.n
    nrm_s0 = s_p3s.n

    def f(e):
        for c in range(NC):
            if c >= 2:
                e.wait_ge(s_p3m.h, nrm_m0 + c - 1)
            ebs = EB[:, (c % 2) * T:(c % 2) * T + T]
            s_p3s.inc(e.activation(out=ebs, in_=XN[:, c * T:(c + 1) * T],
                                   func=AF.Square), 1)
    on("scalar", f)

    def f(e):
        for c in range(NC):
            e.wait_ge(s_p3s.h, nrm_s0 + c + 1)
            ebs = EB[:, (c % 2) * T:(c % 2) * T + T]
            last = None
            for nt in range(4):
                last = e.matmul(PA[0:1, nt * 512:(nt + 1) * 512], ONE[:, 0:1],
                                ebs[:, nt * 512:(nt + 1) * 512],
                                start=(c == 0), stop=(c == NC - 1))
            s_p3m.inc(last, 1)
    on("tensor", f)
    nrm_m = s_p3m.n

    def f(e):
        e.wait_ge(s_p3m.h, nrm_m)
        s_p3s.inc(e.activation(out=CHK[0:1, 0:T], in_=PA[0:1, 0:T],
                               func=AF.Sqrt), 1)
    on("scalar", f)
    wt_s = s_p3s.n

    def f(e):
        e.wait_ge(s_p3s.h, wt_s)
        s_p3v.inc(e.tensor_scalar(out=CHK[0:1, 0:T], in0=CHK[0:1, 0:T],
                                  scalar1=1e-5, scalar2=None, op0=ALU.max), 1)
    on("vector", f)
    wt_v = s_p3v.n

    def f(e):
        e.wait_ge(s_p3v.h, wt_v)
        e.activation(out=CHK[0:1, 0:T], in_=CHK[0:1, 0:T], func=AF.Ln,
                     scale=64.0)
        s_p3s.inc(e.activation(out=CHK[0:1, 0:T], in_=CHK[0:1, 0:T],
                               func=AF.Exp, scale=-1.0), 1)
    on("scalar", f)
    wt_s2 = s_p3s.n

    def f(e):
        e.wait_ge(s_p3s.h, wt_s2)
        s_p3v.inc(e.tensor_copy(out=RW2[0:1, 0:T], in_=CHK[0:1, 0:T]), 1)
    on("vector", f)
    wt_v2 = s_p3v.n

    def f(e):
        e.wait_ge(s_p3v.h, wt_v2)
        last = None
        for nt in range(4):
            last = e.matmul(PB[:, nt * 512:(nt + 1) * 512], ONE[0:1, :],
                            RW2[0:1, nt * 512:(nt + 1) * 512],
                            start=True, stop=True)
        s_p3m.inc(last, 1)
    on("tensor", f)
    wtr_m = s_p3m.n

    def f(e):
        e.wait_ge(s_p3m.h, wtr_m)
        for c in range(NC):
            if c >= 2:
                e.wait_ge(s_p3s.h, wt_s2 + 32 * (c - 1))
            ebs = EB[:, (c % 2) * T:(c % 2) * T + T]
            s_p3v.inc(e.tensor_tensor(out=ebs, in0=XN[:, c * T:(c + 1) * T],
                                      in1=PB[:, 0:T], op=ALU.mult), 1)
    on("vector", f)
    xw_v = s_p3v.n - NC

    def f(e):
        e.wait_ge(s_p3g.h, p3c_g)
        for c in range(NC):
            e.wait_ge(s_p3v.h, xw_v + c + 1)
            ebs = EB[:, (c % 2) * T:(c % 2) * T + T]
            for j in range(32):
                s_p3s.inc(e.activation(out=SMB[:, 576:640],
                                       in_=ebs[:, j * 64:(j + 1) * 64],
                                       func=AF.Identity,
                                       accum_out=CH2[:, c * 32 + j:c * 32 + j + 1]), 1)
    on("scalar", f)
    kacc_s = s_p3s.n

    def f(e):
        e.wait_ge(s_p3s.h, kacc_s)
        last = None
        for ei in range(2):
            dst = PA if ei == 0 else PB
            for c in range(NC):
                col = 524 + c * 3 + ei
                for nt in range(4):
                    last = e.matmul(dst[0:1, nt * 512:(nt + 1) * 512],
                                    SMB[:, col:col + 1],
                                    XN[:, c * T + nt * 512:c * T + (nt + 1) * 512],
                                    start=(c == 0), stop=(c == NC - 1))
        s_p3m.inc(last, 1)
    on("tensor", f)
    ea_m = s_p3m.n

    def f(e):
        e.wait_ge(s_p3m.h, ea_m)
        e.wait_ge(s_p3y.h, p3c_y)
        e.activation(out=CHK[0:1, 0:T], in_=PA[0:1, 0:T], func=AF.Sigmoid,
                     bias=SM[0:1, 61:62])
        s_p3s.inc(e.activation(out=VO[:, 0:4096].bitcast(F32)[0:1, 0:T],
                               in_=PB[0:1, 0:T], func=AF.Sigmoid,
                               bias=SM[0:1, 62:63]), 1)
    on("scalar", f)
    sig_s = s_p3s.n

    def f(e):
        e.wait_ge(s_p3s.h, sig_s)
        for j in range(32):
            e.activation(out=SMB[:, 576:640][0:1, 0:64],
                         in_=CHK[0:1, j * 64:(j + 1) * 64],
                         func=AF.Identity, scale=0.2 / 64.0,
                         accum_out=ROWS[0:1, 1600 + j:1601 + j])
            s_p3s.inc(e.activation(out=SMB[:, 576:640][0:1, 0:64],
                                   in_=VO[:, 0:4096].bitcast(F32)[0:1, j * 64:(j + 1) * 64],
                                   func=AF.Identity, scale=0.5 / 64.0,
                                   accum_out=ROWS[0:1, 1632 + j:1633 + j]), 1)
    on("scalar", f)
    eac_s = s_p3s.n

    def f(e):
        e.wait_ge(s_p3s.h, eac_s)
        s_p3y.inc(e.dma_start(out=a2f_src[0:1, :], in_=ROWS[0:1, 1600:1632]), 16)
        s_p3y.inc(e.dma_start(out=a2f_src[1:2, :], in_=ROWS[0:1, 1632:1664]), 16)
    on("sync", f)
    a2f_y = s_p3y.n

    bst0 = s_bst.n

    def base_evac(s_m, s_e):
        def f(e):
            for fc in range(NC):
                e.wait_ge(s_m.h, s_m.base + fc + 1)
                src = PA if fc % 2 == 0 else PB
                s_e.inc(e.tensor_copy(out=HA[:, fc * T:(fc + 1) * T],
                                      in_=src[:, 0:T]), 1)
        on("vector", f)

        def g(e):
            for fc in range(NC):
                e.wait_ge(s_e.h, s_e.base + fc + 1)
                s_bst.inc(e.dma_start(
                    out=base_srcT.rearrange("(c p) t -> c p t", p=128)[fc],
                    in_=HA[:, fc * T:(fc + 1) * T]), 16)
        on("sync", g)

    _, s_bev = stream_gemm(
        "base", NC, lambda fc: rearr_w(m0T)[:, :, fc * 128:(fc + 1) * 128],
        xn_rhs, base_evac,
        wait_emit=lambda e: e.wait_ge(s_p3m.h, ea_m),
        sems=(b.reuse("s_fw1"), b.reuse("s_fd1"), b.reuse("s_fbm"),
              b.reuse("s_fbe")))
    bev0 = s_bev.base

    def f(e):
        for fc in range(NC):
            e.wait_ge(s_bev.h, bev0 + fc + 1)
            for j in range(32):
                s_p3s.inc(e.activation(out=SMB[:, 576:640],
                                       in_=HA[:, fc * T + j * 64:fc * T + (j + 1) * 64],
                                       func=AF.Identity, scale=1.0 / 64.0,
                                       accum_out=CH2[:, 256 + fc * 32 + j:
                                                     257 + fc * 32 + j]), 1)
    on("scalar", f)
    bm_s = s_p3s.n

    def f(e):
        e.wait_ge(s_p3s.h, bm_s)
        s_p3g.inc(e.dma_start(out=a2b_src[:, 0:512], in_=CH2[:, 0:512]), 16)
    on("gpsimd", f)
    a2b_g = s_p3g.n

    def f(e):
        e.wait_ge(s_p3g.h, a2b_g)
        e.wait_ge(s_p3y.h, a2f_y)
        e.wait_ge(s_bst.h, bst0 + 16 * NC)
        e.collective_compute(
            "AllGather", ALU.bypass, replica_groups=PAIRS,
            ins=[a2b_src], outs=[a2b_dst]).then_inc(s_cc.h, 1)
        s_cc.n += 1
        e.collective_compute(
            "AllGather", ALU.bypass, replica_groups=PAIRS,
            ins=[a2f_src], outs=[a2f_dst]).then_inc(s_cc.h, 1)
        s_cc.n += 1
        e.collective_compute(
            "AllGather", ALU.bypass, replica_groups=PAIRS,
            ins=[base_srcT], outs=[base_dstT]).then_inc(s_cc.h, 1)
        s_cc.n += 1
    on("gpsimd", f)
    cc_a2 = s_cc.n

    def f(e):
        e.wait_ge(s_cc.h, cc_a2)
        for r in range(2):
            s_p3y.inc(e.dma_start(
                out=KSB[:, :].rearrange("p (c r j) -> p c r j", r=2, j=32)[:, :, r],
                in_=a2b_dst[r][:, 0:256].rearrange("p (c j) -> p c j", c=8)), 16)
            s_p3y.inc(e.dma_start(
                out=BMC[:, :].rearrange("p (c r j) -> p c r j", r=2, j=32)[:, :, r],
                in_=a2b_dst[r][:, 256:512].rearrange("p (c j) -> p c j", c=8)), 16)
            s_p3y.inc(e.dma_start(out=ROWS[0:1, 1088 + r * 32:1120 + r * 32],
                                  in_=a2f_dst[r][0:1, :]), 16)
            s_p3y.inc(e.dma_start(out=ROWS[0:1, 1152 + r * 32:1184 + r * 32],
                                  in_=a2f_dst[r][1:2, :]), 16)
    on("sync", f)
    unp_y = s_p3y.n

    s_mkw = [b.reuse("qkv_w0"), b.reuse("qkv_w1")]
    s_mkm = b.reuse("s_aevm")
    s_mks = b.reuse("s_aex")
    s_mkv = b.reuse("s_aevv")
    mkw0 = [s_mkw[0].n, s_mkw[1].n]
    mkm0 = s_mkm.n
    mks0 = s_mks.n
    mkv0 = s_mkv.n

    def f(e):
        for fc in range(NC):
            if fc >= 2:
                e.wait_ge(s_mkm.h, mkm0 + 2 * (fc - 1))
            s_mkw[fc % 2].inc(e.dma_start(
                out=WB1[:, (fc % 2) * 1024:(fc % 2) * 1024 + 1024]
                    .rearrange("p (c f) -> p c f", c=8),
                in_=rearr_w(m0T)[:, :, fc * 128:(fc + 1) * 128]), 16)
    on("gpsimd", f)

    def f(e):
        for fc in range(NC):
            e.wait_ge(s_mkw[fc % 2].h, mkw0[fc % 2] + 16 * (fc // 2 + 1))
            wv = WB1[:, (fc % 2) * 1024:(fc % 2) * 1024 + 1024] \
                .rearrange("p (c f) -> p c f", c=8)
            ebs = EB[:, (fc % 2) * 1024:(fc % 2) * 1024 + 1024]
            if fc == 0:
                e.wait_ge(s_p3y.h, unp_y)
            dst = PA if fc % 2 == 0 else PB
            if fc >= 2:
                e.wait_ge(s_mkv.h, mkv0 + fc - 1)
            last = None
            for c in range(NC):
                last = e.matmul(dst[:, 0:64], wv[:, c, :],
                                KSB[:, c * 64:(c + 1) * 64],
                                start=(c == 0), stop=(c == NC - 1))
            s_mkm.inc(last, 1)
            e.wait_ge(s_mks.h, mks0 + fc + 1)
            for k in range(2):
                last = e.matmul(PA[0:1, 512:1024], ONE[:, 0:1],
                                ebs[:, k * 512:(k + 1) * 512],
                                start=(fc == 0 and k == 0),
                                stop=(fc == NC - 1 and k == 1))
            s_mkm.inc(last, 1)
    on("tensor", f)

    def f(e):
        for fc in range(NC):
            e.wait_ge(s_mkw[fc % 2].h, mkw0[fc % 2] + 16 * (fc // 2 + 1))
            if fc >= 2:
                e.wait_ge(s_mkm.h, mkm0 + 2 * fc - 2)
            ebs = EB[:, (fc % 2) * 1024:(fc % 2) * 1024 + 1024]
            s_mks.inc(e.activation(
                out=ebs, in_=WB1[:, (fc % 2) * 1024:(fc % 2) * 1024 + 1024],
                func=AF.Square), 1)
    on("scalar", f)

    def f(e):
        for fc in range(NC):
            e.wait_ge(s_mkm.h, mkm0 + 2 * fc + 1)
            src = PA if fc % 2 == 0 else PB
            s_mkv.inc(e.tensor_copy(out=SMB[:, fc * 64:(fc + 1) * 64],
                                    in_=src[:, 0:64]), 1)
    on("vector", f)
    mk_v = s_mkv.n

    def f(e):
        e.wait_ge(s_mkm.h, mkm0 + 2 * NC)
        s_mks.inc(e.activation(out=VO[:, 0:4096].bitcast(F32)[0:1, 0:512],
                               in_=PA[0:1, 512:1024], func=AF.Identity,
                               accum_out=row(R_M0N, 0, 1)), 1)
    on("scalar", f)
    m0n_s = s_mks.n

    def f(e):
        e.wait_ge(s_p3y.h, unp_y)
        e.wait_ge(s_mks.h, m0n_s)
        s_mks.inc(e.activation(out=EB[:, 0:512], in_=KSB[:], func=AF.Square), 1)
    on("scalar", f)
    sqk_s = s_mks.n

    def f(e):
        e.wait_ge(s_mks.h, sqk_s)
        e.wait_ge(s_mkv.h, mk_v)
        last = None
        for c in range(NC):
            e.matmul(PA[0:64, 1024:1088], KSB[:, c * 64:(c + 1) * 64],
                     KSB[:, c * 64:(c + 1) * 64],
                     start=(c == 0), stop=(c == NC - 1))
            e.matmul(PA[0:1, 1536:1600], ONE[:, 0:1],
                     EB[:, c * 64:(c + 1) * 64],
                     start=(c == 0), stop=(c == NC - 1))
            last = e.matmul(PA[0:1, 512:576], SMB[:, 524 + c * 3 + 2:524 + c * 3 + 3],
                            KSB[:, c * 64:(c + 1) * 64],
                            start=(c == 0), stop=(c == NC - 1))
        s_mkm.inc(last, 1)
    on("tensor", f)
    gk_m = s_mkm.n

    def f(e):
        e.wait_ge(s_mkm.h, gk_m)
        e.tensor_copy(out=PBG[0:64, 0:64], in_=PA[0:64, 1024:1088])
        s_mkv.inc(e.tensor_copy(out=row(R_KN2, 0, 64), in_=PA[0:1, 1536:1600]), 1)
    on("vector", f)
    gk_v = s_mkv.n

    def f(e):
        e.wait_ge(s_mkv.h, gk_v)
        s_mks.inc(e.activation(out=row(R_GATE, 0, 64), in_=PA[0:1, 512:576],
                               func=AF.Sigmoid, bias=SM[0:1, 63:64]), 1)
    on("scalar", f)
    gc_s = s_mks.n

    def f(e):
        e.wait_ge(s_mks.h, gc_s)
        e.tensor_scalar(out=ROWS[0:1, 1152:1216], in0=ROWS[0:1, 1152:1216],
                        scalar1=0.5, scalar2=None, op0=ALU.add)
        e.tensor_scalar(out=ROWS[0:1, 1664:1728], in0=ROWS[0:1, 1152:1216],
                        scalar1=-1.0, scalar2=None, op0=ALU.add)
        e.tensor_tensor(out=row(R_A, 0, 64), in0=row(R_GATE, 0, 64),
                        in1=ROWS[0:1, 1664:1728], op=ALU.mult)
        e.tensor_scalar(out=row(R_A, 0, 64), in0=row(R_A, 0, 64),
                        scalar1=1.0, scalar2=None, op0=ALU.add)
        s_mkv.inc(e.tensor_tensor(out=row(R_B, 0, 64), in0=row(R_GATE, 0, 64),
                                  in1=ROWS[0:1, 1088:1152], op=ALU.mult), 1)
    on("vector", f)
    ab_v = s_mkv.n

    def f(e):
        e.wait_ge(s_mkv.h, gk_v)
        last = None
        for nt in range(4):
            for c in range(NC):
                last = e.matmul(PA[0:64, nt * 512:(nt + 1) * 512],
                                KSB[:, c * 64:(c + 1) * 64],
                                XN[:, c * T + nt * 512:c * T + (nt + 1) * 512],
                                start=(c == 0), stop=(c == NC - 1))
        s_mkm.inc(last, 1)
    on("tensor", f)
    pt_m = s_mkm.n

    def f(e):
        e.wait_ge(s_mkm.h, pt_m)
        s_mkv.inc(e.tensor_copy(out=PT[:, 0:T], in_=PA[0:64, 0:T]), 1)
    on("vector", f)
    pt_v = s_mkv.n

    def f(e):
        e.wait_ge(s_mkm.h, pt_m)
        for j in range(32):
            s_mks.inc(e.activation(out=SMB[:, 576:640][0:64, 0:64],
                                   in_=PA[0:64, j * 64:(j + 1) * 64],
                                   func=AF.Identity, scale=1.0 / 64.0,
                                   accum_out=CH2[0:64, 512 + j:513 + j]), 1)
    on("scalar", f)
    pb_s = s_mks.n

    def f(e):
        e.wait_ge(s_mkv.h, pt_v)
        s_p3y.inc(e.dma_start(out=a3p_src, in_=PT[:, 0:T]), 16)
        e.wait_ge(s_mks.h, pb_s)
        s_p3y.inc(e.dma_start(out=a3b_src, in_=CH2[0:64, 512:544]), 16)
    on("sync", f)
    a3_y = s_p3y.n

    def f(e):
        e.wait_ge(s_p3y.h, a3_y)
        e.collective_compute(
            "AllGather", ALU.bypass, replica_groups=PAIRS,
            ins=[a3p_src], outs=[a3p_dst]).then_inc(s_cc.h, 1)
        s_cc.n += 1
        e.collective_compute(
            "AllGather", ALU.bypass, replica_groups=PAIRS,
            ins=[a3b_src], outs=[a3b_dst]).then_inc(s_cc.h, 1)
        s_cc.n += 1
    on("gpsimd", f)
    cc_a3 = s_cc.n

    def f(e):
        e.wait_ge(s_cc.h, cc_a3)
        for r in range(2):
            s_p3y.inc(e.dma_start(out=PT[:, r * T:(r + 1) * T],
                                  in_=a3p_dst[r]), 16)
    on("sync", f)

    def f(e):
        e.wait_ge(s_cc.h, cc_a3)
        for r in range(2):
            s_p3g.inc(e.dma_start(out=PBG[0:64, 64 + r * 32:96 + r * 32],
                                  in_=a3b_dst[r]), 16)
    on("gpsimd", f)
    barrier()
    ckpt("l1pre")
    dbg("ksb", KSB[:], [128, 512], BF16)
    dbg("bmc", BMC[:], [128, 512], BF16)
    dbg("m0k", SMB[:, 0:512], [128, 512], BF16)
    dbg("gpb", PBG[:], [64, 128], BF16)
    dbg("rows13", ROWS[0:1, 1088:1664], [1, 576], F32)
    dbg("ptb", PT[:], [64, 4096], BF16)

    s_scm = b.reuse("att_m")
    s_scs = b.reuse("s_aevs")
    s_scv = b.reuse("s_rv")
    s_scy = b.reuse("s_rm0")
    s_bw = [b.reuse("s_rm1"), b.reuse("s_rs0")]
    s_vg = b.reuse("wobv_w0")

    scm0 = s_scm.n
    scs0 = s_scs.n
    scv0 = s_scv.n
    scy0 = s_scy.n
    bw0 = [s_bw[0].n, s_bw[1].n]
    vg0 = s_vg.n

    def f(e):
        e.memset(UT[:], 0.0)
        e.memset(VT[:], 0.0)
        e.memset(SM[0:1, 102:103], 1.0)
        e.memset(SM[0:64, 103:104], 1.0)
        e.memset(SM[:, 105:106], 1.0)
        s_vg.inc(e.memset(ROWS[0:1, 1472:1473], 1.0), 1)
        s_vg.inc(e.dma_start(
            out=MOW[:].rearrange("p (c f) -> p c f", c=8),
            in_=rearr_w(vg1T)), 16)
        s_vg.inc(e.dma_start(out=QP[:], in_=vg2q[:, 0:4096]), 16)
        s_vg.inc(e.dma_start(out=KHV[:], in_=vg2q[:, 4096:8192]), 16)
    on("gpsimd", f)
    vg_done = s_vg.n

    def f(e):
        s_scv.inc(e.tensor_copy(out=SM[0:1, 104:105], in_=row(R_M0N, 0, 1)), 1)
    on("vector", f)
    nm2_v = s_scv.n

    def bst_view(t):
        sl = WB1[:, (t % 2) * 1024:(t % 2) * 1024 + 512]
        return sl.rearrange("p (c f) -> p c f", c=8)

    def bst_src(t):
        return base_dstT[t // 32].rearrange("(c p) f -> p c f", p=128) \
            [:, :, (t % 32) * 64:(t % 32) * 64 + 64]

    def f(e):
        s_bw[0].inc(e.dma_start(out=bst_view(0), in_=bst_src(0)), 16)
        s_bw[1].inc(e.dma_start(out=bst_view(1), in_=bst_src(1)), 16)
    on("sync", f)

    NST = SCAN_STEPS
    c_b2 = [0] * (NST + 1)
    c_corr = [0] * NST
    c_hm = [0] * NST
    c_v1 = [0] * NST
    c_omk = [0] * NST
    c_dot = [0] * NST
    c_rep = [0] * NST
    c_vmm = [0] * NST
    v_wc = [0] * NST
    v_err = [0] * NST
    v_n2 = [0] * NST
    v_gup = [0] * NST
    v_rep = [0] * NST
    v_uc = [0] * NST
    v_vc = [0] * NST
    q_sil = [0] * NST
    q_sc = [0] * NST
    y_app = [0] * NST

    def vg2tile(k, m):
        tix = k * 8 + m
        buf = QP if tix < 32 else KHV
        off = (tix % 32) * 128
        return buf[:, off:off + 128]

    def f(e):
        e.wait_ge(s_vg.h, vg_done)
        e.wait_ge(s_bw[0].h, bw0[0] + 16)
        last = None
        for c in range(NC):
            for nt in range(2):
                last = e.matmul(PB[0:64, nt * 512:(nt + 1) * 512],
                                bst_view(0)[:, c, :],
                                MOW[:, c * 1024 + nt * 512:c * 1024 + (nt + 1) * 512],
                                start=(c == 0), stop=False)
        s_scm.inc(last, 1)
    on("tensor", f)
    c_b2[0] = s_scm.n

    for t in range(NST):
        half = (t % 2) * 1024
        nhalf = ((t + 1) % 2) * 1024

        def f(e, t=t):
            if t == 0:
                e.wait_ge(s_vg.h, vg0 + 1)
            s_scv.inc(e.tensor_tensor(out=SM[0:1, 119:120],
                                      in0=ROWS[0:1, 1280 + t:1281 + t],
                                      in1=SM[0:1, 102:103], op=ALU.mult), 1)
        on("vector", f)
        v_rt1 = s_scv.n

        def f(e, t=t, v_rt1=v_rt1):
            e.wait_ge(s_scv.h, v_rt1)
            e.activation(out=SM[0:1, 116:117], in_=SM[0:1, 119:120], func=AF.Ln)
            s_scs.inc(e.activation(out=SM[0:1, 116:117], in_=SM[0:1, 116:117],
                                   func=AF.Exp, scale=-1.0), 1)
        on("scalar", f)
        q_rr = s_scs.n

        def f(e, t=t, q_rr=q_rr):
            e.wait_ge(s_scs.h, q_rr)
            if t >= 1:
                e.wait_ge(s_scm.h, c_corr[t - 1])
            e.tensor_tensor(out=SM[0:1, 118:119],
                            in0=ROWS[0:1, 1344 + t:1345 + t],
                            in1=SM[0:1, 116:117], op=ALU.mult)
            s_scv.inc(e.tensor_copy(out=SMB[0:1, 642:643],
                                    in_=SM[0:1, 118:119]), 1)
        on("vector", f)
        v_rho = s_scv.n

        def f(e, t=t, half=half, nhalf=nhalf, v_rho=v_rho):
            if t + 1 < NST:
                e.wait_ge(s_bw[(t + 1) % 2].h, bw0[(t + 1) % 2] + 16 * ((t + 1) // 2 + 1))
                if t >= 1:
                    e.wait_ge(s_scs.h, q_sil[t - 1])
                    e.wait_ge(s_scv.h, v_vc[t - 1])
                last = None
                for c in range(NC):
                    for nt in range(2):
                        last = e.matmul(PB[0:64, nhalf + nt * 512:nhalf + (nt + 1) * 512],
                                        bst_view(t + 1)[:, c, :],
                                        MOW[:, c * 1024 + nt * 512:c * 1024 + (nt + 1) * 512],
                                        start=(c == 0), stop=False)
                s_scm.inc(last, 1)
            if t >= 1:
                e.wait_ge(s_scy.h, y_app[t - 1])
            last = None
            for nt in range(2):
                last = e.matmul(PB[0:64, half + nt * 512:half + (nt + 1) * 512],
                                PT[:, t * 64:(t + 1) * 64],
                                VT[:, nt * 512:(nt + 1) * 512],
                                start=False, stop=True)
            e.wait_ge(s_scv.h, v_rho)
            last = e.matmul(PA[:, 1571:1572], ONE[0:1, :], SMB[0:1, 642:643],
                            start=True, stop=True)
            s_scm.inc(last, 1)
        on("tensor", f)
        c_b2[t + 1] = s_scm.n - 1
        c_corr[t] = s_scm.n

        def f(e, t=t):
            e.wait_ge(s_scm.h, c_corr[t])
            s_scv.inc(e.tensor_copy(out=SM[:, 109:110], in_=PA[:, 1571:1572]), 1)
        on("vector", f)
        v_r128 = s_scv.n

        def f(e, t=t, half=half):
            e.wait_ge(s_scm.h, c_corr[t])
            if t >= 1:
                e.wait_ge(s_scv.h, v_rep[t - 1])
            s_scs.inc(e.activation(
                out=EB[0:64, half:half + 1024], in_=PB[0:64, half:half + 1024],
                func=AF.Silu, scale=SM[0:64, 103:104]), 1)
        on("scalar", f)
        q_sil[t] = s_scs.n

        def f(e, t=t, half=half):
            e.wait_ge(s_scs.h, q_sil[t])
            last = None
            for m in range(8):
                last = e.matmul(PA[:, 1536 + m:1537 + m],
                                EB[0:64, half + m * 128:half + (m + 1) * 128],
                                SMB[0:64, 548:549], start=True, stop=True)
            s_scm.inc(last, 1)
        on("tensor", f)
        c_hm[t] = s_scm.n

        def f(e, t=t):
            e.wait_ge(s_scm.h, c_hm[t])
            s_scv.inc(e.tensor_copy(out=SMB[:, 552:560], in_=PA[:, 1536:1544]), 1)
        on("vector", f)
        v_wc[t] = s_scv.n

        def f(e, t=t):
            e.wait_ge(s_scv.h, v_wc[t])
            last = None
            for m in range(8):
                for k in range(8):
                    last = e.matmul(PA[:, 1544 + m:1545 + m], vg2tile(k, m),
                                    SMB[:, 552 + k:553 + k],
                                    start=(k == 0), stop=(k == 7))
            s_scm.inc(last, 1)
            if t >= 1:
                e.wait_ge(s_scy.h, y_app[t - 1])
            for m in range(8):
                e.matmul(PA[:, 1552 + m:1553 + m], UT[:, m * 128:(m + 1) * 128],
                         PBG[0:64, 64 + t:65 + t], start=True, stop=True)
                last = e.matmul(PA[:, 1560 + m:1561 + m], UT[:, m * 128:(m + 1) * 128],
                                PBG[0:64, t:t + 1], start=True, stop=True)
            s_scm.inc(last, 1)
        on("tensor", f)
        c_v1[t] = s_scm.n - 1
        c_omk[t] = s_scm.n

        def f(e, t=t, v_r128=v_r128):
            e.wait_ge(s_scm.h, c_omk[t])
            bmcc = BMC[:, t::64]
            m0kc = SMB[:, 0:512][:, t::64]
            e.tensor_tensor(out=CH2[:, 544:552], in0=PA[:, 1552:1560],
                            in1=bmcc, op=ALU.add)
            e.tensor_scalar(out=CH2[:, 544:552], in0=CH2[:, 544:552],
                            scalar1=SM[:, 105:106], scalar2=None, op0=ALU.mult)
            e.tensor_tensor(out=CH2[:, 552:560], in0=PA[:, 1560:1568],
                            in1=m0kc, op=ALU.add)
            e.tensor_scalar(out=CH2[:, 552:560], in0=CH2[:, 552:560],
                            scalar1=SM[:, 105:106], scalar2=None, op0=ALU.mult)
            e.tensor_tensor(out=CH2[:, 560:568], in0=PA[:, 1544:1552],
                            in1=CH2[:, 544:552], op=ALU.add)
            e.tensor_tensor(out=CH2[:, 568:576], in0=CH2[:, 560:568],
                            in1=CH2[:, 552:560], op=ALU.subtract)
            e.wait_ge(s_scv.h, v_r128)
            e.tensor_scalar(out=CH2[:, 580:588], in0=CH2[:, 568:576],
                            scalar1=SM[:, 109:110], scalar2=None, op0=ALU.mult)
            s_scv.inc(e.tensor_copy(out=SMB[:, 644:652], in_=CH2[:, 580:588]), 1)
        on("vector", f)
        v_uc[t] = s_scv.n

        def f(e, t=t, half=half):
            e.wait_ge(s_scv.h, v_uc[t])
            last = None
            for s in range(8):
                last = e.transpose(PA[0:1, s * 128:(s + 1) * 128],
                                   CH2[:, 580 + s:581 + s], IDF[:])
            s_scm.inc(last, 1)
            last = None
            for m in range(8):
                for k in range(8):
                    last = e.matmul(PA[:, 1572 + m:1573 + m],
                                    MOW[:, k * 1024 + m * 128:k * 1024 + (m + 1) * 128],
                                    SMB[:, 644 + k:645 + k],
                                    start=(k == 0), stop=(k == 7))
            s_scm.inc(last, 1)
        on("tensor", f)
        c_ut = s_scm.n - 1
        c_vmm[t] = s_scm.n

        def f(e, t=t, c_ut=c_ut):
            e.wait_ge(s_scm.h, c_vmm[t])
            e.scalar_tensor_tensor(out=CH2[:, 592:600], in0=CH2[:, 568:576],
                                   scalar=1.0, in1=CH2[:, 552:560],
                                   op0=ALU.mult, op1=ALU.mult,
                                   accum_out=CH2[:, 576:577])
            e.scalar_tensor_tensor(out=CH2[:, 600:608], in0=CH2[:, 568:576],
                                   scalar=1.0, in1=CH2[:, 568:576],
                                   op0=ALU.mult, op1=ALU.mult,
                                   accum_out=CH2[:, 577:578])
            e.tensor_copy(out=SMB[:, 640:642], in_=CH2[:, 576:578])
            s_scv.inc(e.tensor_copy(out=CH2[:, 608:616], in_=PA[:, 1572:1580]), 1)
        on("vector", f)
        v_err[t] = s_scv.n

        def f(e, t=t, half=half):
            e.wait_ge(s_scv.h, v_err[t])
            last = None
            for s in range(8):
                last = e.transpose(PB[0:1, half + s * 128:half + (s + 1) * 128],
                                   CH2[:, 608 + s:609 + s], IDF[:])
            e.matmul(PA[0:1, 1568:1569], SMB[:, 640:641], ONE[:, 0:1],
                     start=True, stop=True)
            last = e.matmul(PA[0:1, 1569:1570], SMB[:, 641:642], ONE[:, 0:1],
                            start=True, stop=True)
            s_scm.inc(last, 1)
        on("tensor", f)
        c_dot[t] = s_scm.n

        def f(e, t=t, half=half):
            e.wait_ge(s_scm.h, c_dot[t])
            e.tensor_copy(out=RW2[0:1, 0:1024], in_=PA[0:1, 0:1024])
            s_scv.inc(e.tensor_copy(out=RW2[0:1, 1024:2048],
                                    in_=PB[0:1, half:half + 1024]), 1)
        on("vector", f)
        v_vc[t] = s_scv.n

        def f(e, t=t):
            if t + 2 < NST:
                e.wait_ge(s_scm.h, c_b2[t + 1])
                s_bw[t % 2].inc(e.dma_start(out=bst_view(t + 2),
                                            in_=bst_src(t + 2)), 16)
            e.wait_ge(s_scv.h, v_vc[t])
            if t >= 1:
                e.wait_ge(s_scy.h, y_app[t - 1])
            s_scy.inc(e.dma_start(out=uvrow_hbm[0:1, :], in_=RW2[0:1, 0:1024]), 16)
            s_scy.inc(e.dma_start(out=uvrow_hbm[1:2, :], in_=RW2[0:1, 1024:2048]), 16)
            e.wait_ge(s_scy.h, s_scy.n)
            s_scy.inc(e.dma_start(out=UT[t:t + 1, :], in_=uvrow_hbm[0:1, :]), 16)
            s_scy.inc(e.dma_start(out=VT[t:t + 1, :], in_=uvrow_hbm[1:2, :]), 16)
        on("sync", f)
        y_app[t] = s_scy.n

        def f(e, t=t):
            e.wait_ge(s_scm.h, c_dot[t])
            ca = ROWS[0:1, 1280 + t:1281 + t]
            cb = ROWS[0:1, 1344 + t:1345 + t]
            kn = ROWS[0:1, 1408 + t:1409 + t]
            e.tensor_tensor(out=SM[0:1, 110:111], in0=ca, in1=ca, op=ALU.mult)
            e.tensor_tensor(out=SM[0:1, 110:111], in0=SM[0:1, 110:111],
                            in1=SM[0:1, 104:105], op=ALU.mult)
            e.tensor_tensor(out=SM[0:1, 111:112], in0=ca, in1=cb, op=ALU.mult)
            e.tensor_tensor(out=SM[0:1, 111:112], in0=SM[0:1, 111:112],
                            in1=PA[0:1, 1568:1569], op=ALU.mult)
            e.tensor_scalar(out=SM[0:1, 111:112], in0=SM[0:1, 111:112],
                            scalar1=2.0, scalar2=None, op0=ALU.mult)
            e.tensor_tensor(out=SM[0:1, 112:113], in0=cb, in1=cb, op=ALU.mult)
            e.tensor_tensor(out=SM[0:1, 112:113], in0=SM[0:1, 112:113],
                            in1=kn, op=ALU.mult)
            e.tensor_tensor(out=SM[0:1, 112:113], in0=SM[0:1, 112:113],
                            in1=PA[0:1, 1569:1570], op=ALU.mult)
            e.tensor_tensor(out=SM[0:1, 106:107], in0=SM[0:1, 110:111],
                            in1=SM[0:1, 111:112], op=ALU.add)
            s_scv.inc(e.tensor_tensor(out=SM[0:1, 106:107], in0=SM[0:1, 106:107],
                                      in1=SM[0:1, 112:113], op=ALU.add), 1)
        on("vector", f)
        v_n2[t] = s_scv.n

        def f(e, t=t):
            e.wait_ge(s_scv.h, v_n2[t])
            e.activation(out=SM[0:1, 113:114], in_=SM[0:1, 106:107],
                         func=AF.Sqrt, bias=SM[0:1, 97:98])
            e.activation(out=SM[0:1, 113:114], in_=SM[0:1, 113:114], func=AF.Ln)
            s_scs.inc(e.activation(out=SM[0:1, 113:114], in_=SM[0:1, 113:114],
                                   func=AF.Exp, scale=-1.0), 1)
        on("scalar", f)
        q_rc = s_scs.n

        def f(e, t=t, q_rc=q_rc):
            e.wait_ge(s_scs.h, q_rc)
            e.tensor_scalar(out=SM[0:1, 107:108], in0=SM[0:1, 113:114],
                            scalar1=30.0, scalar2=None, op0=ALU.mult)
            e.tensor_scalar(out=SM[0:1, 107:108], in0=SM[0:1, 107:108],
                            scalar1=1.0, scalar2=None, op0=ALU.min)
            e.tensor_tensor(out=SM[0:1, 108:109], in0=SM[0:1, 107:108],
                            in1=ROWS[0:1, 1280 + t:1281 + t], op=ALU.mult)
            e.tensor_tensor(out=SM[0:1, 108:109], in0=SM[0:1, 108:109],
                            in1=SM[0:1, 102:103], op=ALU.mult)
            e.tensor_tensor(out=SM[0:1, 114:115], in0=SM[0:1, 107:108],
                            in1=SM[0:1, 107:108], op=ALU.mult)
            e.tensor_tensor(out=SM[0:1, 104:105], in0=SM[0:1, 114:115],
                            in1=SM[0:1, 106:107], op=ALU.mult)
            e.tensor_copy(out=SM[0:1, 102:103], in_=SM[0:1, 108:109])
            if t + 1 < NST:
                e.tensor_copy(out=ROWS[0:1, 1472 + t + 1:1473 + t + 1],
                              in_=SM[0:1, 108:109])
            s_scv.inc(e.tensor_copy(out=SMB[0:1, 643:644],
                                    in_=SM[0:1, 108:109]), 1)
        on("vector", f)
        v_gup[t] = s_scv.n

        def f(e, t=t):
            e.wait_ge(s_scv.h, v_gup[t])
            last = e.matmul(PA[:, 1570:1571], ONE[0:1, :], SMB[0:1, 643:644],
                            start=True, stop=True)
            s_scm.inc(last, 1)
        on("tensor", f)
        c_rep[t] = s_scm.n

        def f(e, t=t):
            e.wait_ge(s_scm.h, c_rep[t])
            e.tensor_copy(out=SM[0:64, 103:104], in_=PA[0:64, 1570:1571])
            s_scv.inc(e.tensor_copy(out=SM[:, 105:106], in_=PA[:, 1570:1571]), 1)
        on("vector", f)
        v_rep[t] = s_scv.n

    barrier()
    ckpt("l1scan")
    dbg("ut", UT[:], [64, D], BF16)
    dbg("vt", VT[:], [64, D], BF16)
    dbg("gam", ROWS[0:1, 1472:1536], [1, 64], F32)
    dbg("sc0", CH2[:, 544:608], [128, 64], F32)
    dbg("wc0", SMB[:, 544:672], [128, 128], BF16)
    dbg("ht0", EB[0:64, 0:1024], [64, 1024], BF16)
    dbg("bst0", WB1[:, 0:512], [128, 512], BF16)
    dbg("mow0", MOW[:, 0:2048], [128, 2048], BF16)

    s_f5m = b.reuse("qkv_m")
    s_f5v = b.reuse("s_bv")
    s_f5s = b.reuse("s_bs")
    s_f5y = b.reuse("s_rs1")

    f5y0 = s_f5y.n
    f5v0 = s_f5v.n
    f5m0 = s_f5m.n

    half_par = None

    LMK = EB[0:64, 0:4096]

    def f(e):
        s_f5y.inc(e.dma_start(out=LMK, in_=l1msk), 16)
    on("sync", f)
    lmk_y = s_f5y.n

    def f(e):
        e.wait_ge(s_f5y.h, lmk_y)
        pass
    on("sync", f)

    def f(e):
        s_f5v.inc(e.tensor_copy(out=RW2[0:1, 0:64], in_=ROWS[0:1, 1472:1536]), 1)
    on("vector", f)
    gv = s_f5v.n

    def f(e):
        e.wait_ge(s_f5v.h, gv)
        s_f5y.inc(e.dma_start(out=uvrow_hbm[0:1, 0:64], in_=RW2[0:1, 0:64]), 16)
        e.wait_ge(s_f5y.h, s_f5y.n)
        s_f5y.inc(e.dma_start(
            out=ESEL[0:64, 0:1],
            in_=uvrow_hbm[0:1, 0:64].rearrange("o (p u) -> p (o u)", u=1)), 16)
    on("sync", f)
    gc_y = s_f5y.n

    PTm = PT
    def f(e):
        e.wait_ge(s_f5y.h, lmk_y)
        s_f5v.inc(e.tensor_tensor(out=PT[:, 0:4096],
                                  in0=PT[:, 0:4096], in1=LMK,
                                  op=ALU.mult), 1)
    on("vector", f)
    ptm_v = s_f5v.n

    def f(e):
        e.wait_ge(s_f5y.h, gc_y)
        e.wait_ge(s_f5v.h, ptm_v)
        last = None
        for nt in range(4):
            last = e.matmul(PA[0:1, nt * 512:(nt + 1) * 512], ESEL[0:64, 0:1],
                            LMK[:, 2048 + nt * 512:2048 + (nt + 1) * 512],
                            start=True, stop=True)
        s_f5m.inc(last, 1)
    on("tensor", f)
    gt_m = s_f5m.n

    def f(e):
        e.wait_ge(s_f5m.h, gt_m)
        s_f5v.inc(e.tensor_copy(out=RW2[0:1, 0:T], in_=PA[0:1, 0:T]), 1)
    on("vector", f)
    gt_v = s_f5v.n

    def f(e):
        e.wait_ge(s_f5v.h, gt_v)
        last = None
        for nt in range(4):
            last = e.matmul(PB[:, nt * 512:(nt + 1) * 512], ONE[0:1, :],
                            RW2[0:1, nt * 512:(nt + 1) * 512],
                            start=True, stop=True)
        s_f5m.inc(last, 1)
    on("tensor", f)
    gr_m = s_f5m.n

    def f(e):
        e.wait_ge(s_f5m.h, gr_m)
        s_f5v.inc(e.tensor_copy(out=RW2[:, 0:T], in_=PB[:, 0:T]), 1)
    on("vector", f)
    grp_v = s_f5v.n

    def ft_slice(hs):
        return [VO[:, 0:2048], VO[:, 2080:4128], VO[:, 4160:6208],
                EB[:, 0:2048], EB[:, 2048:4096], WB3[:, 0:2048],
                WB1[:, 0:2048],
                CH2[:, 0:1024].bitcast(BF16)[:, 0:2048]][hs]

    mb = s_f5m.n
    vb = s_f5v.n
    sb = s_f5s.n
    hid_m = [mb + hs + 1 for hs in range(8)]
    hid_v = [vb + hs + 1 for hs in range(8)]
    hid_s = [sb + hs + 1 for hs in range(8)]
    outs_m = [mb + 8 + fs + 1 for fs in range(8)]
    outs_v = [vb + 8 + fs + 1 for fs in range(8)]
    res_m = [mb + 16 + fs + 1 for fs in range(8)]
    comb_v = [vb + 16 + fs + 1 for fs in range(8)]
    s_fin = sem("s_fin")
    yb = s_f5y.n
    x1_y = [yb + 16 * (fs + 1) for fs in range(8)]

    def f(e):
        e.wait_ge(s_f5v.h, grp_v)
        for hs in range(8):
            if hs >= 2:
                e.wait_ge(s_f5v.h, hid_v[hs - 2])
            dst = PA if hs % 2 == 0 else PB
            last = None
            for nt in range(4):
                for k in range(9):
                    if k < 8:
                        lhsT = MOW[:, k * 1024 + hs * 128:k * 1024 + (hs + 1) * 128]
                        rhs = HA[:, k * T + nt * 512:k * T + (nt + 1) * 512]
                    else:
                        lhsT = VT[:, hs * 128:(hs + 1) * 128]
                        rhs = PT[:, nt * 512:(nt + 1) * 512]
                    last = e.matmul(dst[:, nt * 512:(nt + 1) * 512], lhsT, rhs,
                                    start=(k == 0), stop=(k == 8))
            s_f5m.inc(last, 1)
    on("tensor", f)

    def f(e):
        for hs in range(8):
            e.wait_ge(s_f5m.h, hid_m[hs])
            if hs >= 1:
                e.wait_ge(s_f5s.h, hid_s[hs - 1])
            src = PA if hs % 2 == 0 else PB
            s_f5v.inc(e.tensor_tensor(out=CHK[:, 0:T], in0=src[:, 0:T],
                                      in1=RW2[:, 0:T], op=ALU.mult), 1)
    on("vector", f)

    def f(e):
        for hs in range(8):
            e.wait_ge(s_f5v.h, hid_v[hs])
            s_f5s.inc(e.activation(out=ft_slice(hs), in_=CHK[:, 0:T],
                                   func=AF.Silu), 1)
    on("scalar", f)

    def f(e):
        e.wait_ge(s_f5v.h, hid_v[7])
        for fs in range(8):
            if fs >= 2:
                e.wait_ge(s_f5v.h, outs_v[fs - 2])
            dst = PA if fs % 2 == 0 else PB
            last = None
            for nt in range(4):
                last = e.matmul(dst[:, nt * 512:(nt + 1) * 512],
                                UT[:, fs * 128:(fs + 1) * 128],
                                PT[:, nt * 512:(nt + 1) * 512],
                                start=True, stop=True)
            s_f5m.inc(last, 1)
    on("tensor", f)

    def f(e):
        e.wait_ge(s_f5s.h, hid_s[7])
        for fs in range(8):
            e.wait_ge(s_f5m.h, outs_m[fs])
            src = PA if fs % 2 == 0 else PB
            e.tensor_tensor(out=src[:, 0:T], in0=src[:, 0:T],
                            in1=HA[:, fs * T:(fs + 1) * T], op=ALU.add)
            s_f5v.inc(e.tensor_tensor(out=XN[:, fs * T:(fs + 1) * T],
                                      in0=src[:, 0:T], in1=RW2[:, 0:T],
                                      op=ALU.mult), 1)
    on("vector", f)

    def f(e):
        e.wait_ge(s_f5v.h, outs_v[7])
        e.wait_ge(s_f5s.h, hid_s[7])
        for fs in range(8):
            if fs >= 2:
                e.wait_ge(s_f5v.h, comb_v[fs - 2])
            dst = PA if fs % 2 == 0 else PB
            last = None
            for nt in range(4):
                for k in range(8):
                    last = e.matmul(dst[:, nt * 512:(nt + 1) * 512],
                                    vg2tile(k, fs),
                                    ft_slice(k)[:, nt * 512:(nt + 1) * 512],
                                    start=(k == 0), stop=(k == 7))
            s_f5m.inc(last, 1)
    on("tensor", f)

    def f(e):
        for fs in range(8):
            e.wait_ge(s_f5m.h, res_m[fs])
            e.wait_ge(s_f5y.h, x1_y[fs])
            src = PA if fs % 2 == 0 else PB
            e.tensor_tensor(out=CHK[:, 0:T], in0=CHK[:, 0:T],
                            in1=src[:, 0:T], op=ALU.add)
            e.tensor_tensor(out=CHK[:, 0:T], in0=CHK[:, 0:T],
                            in1=XN[:, fs * T:(fs + 1) * T], op=ALU.add)
            s_f5v.inc(e.tensor_tensor(out=CHK[:, 0:T], in0=CHK[:, 0:T],
                                      in1=HB[:, fs * T:(fs + 1) * T],
                                      op=ALU.add), 1)
    on("vector", f)

    def f(e):
        e.wait_ge(s_f5s.h, hid_s[7])
        for fs in range(8):
            if fs >= 1:
                e.wait_ge(s_fin.h, 16 * fs)
            s_f5y.inc(e.dma_start(
                out=CHK[:, 0:T],
                in_=x1sp.rearrange("(c p) t -> c p t", p=128)[fs]), 16)
            e.wait_ge(s_f5v.h, comb_v[fs])
            s_fin.inc(e.dma_start(
                out=out_fm.rearrange("(c p) t -> c p t", p=128)[fs],
                in_=CHK[:, 0:T]), 16)
        e.wait_ge(s_fin.h, 16 * 8)
        e.wait_ge(s_dbg.h, s_dbg.n)
    on("sync", f)
    barrier()

    return b
def _mk_esel():
    es = np.zeros((64, 1024), BF)
    for base in (0, 32):
        for h in range(16):
            c = h // 2
            off = (h % 2) * 64
            es[base + h, c * 128 + off:c * 128 + off + 64] = 1.0
    return es


def _prep_inputs(inputs):
    f = np.ascontiguousarray
    x = inputs["L0_attn_x"]
    x_fm = f(x.transpose(0, 2, 1))
    shared = {
        "wqkvT": f(inputs["L0_attn_Wqkv"].T),
        "bqkv_c": f(inputs["L0_attn_bqkv"].reshape(24, 128).T),
        "woT": f(inputs["L0_attn_Wo"].T),
        "bo_c": f(inputs["L0_attn_bo"].reshape(8, 128).T),
        "wq": f(inputs["L0_attn_wq"].reshape(64, 1)),
        "wk": f(inputs["L0_attn_wk"].reshape(64, 1)),
        "gateWT": f(inputs["L0_ffn_gate_W"].T),
        "e0T": f(inputs["L0_ffn_expert_W"][0].T),
        "e1T": f(inputs["L0_ffn_expert_W"][1].T),
        "outWT": f(inputs["L0_ffn_out_W"].T),
        "m0T": f(inputs["L1_attn_M_init"].T),
        "vg1T": f(inputs["L1_attn_vg_w1"].T),
        "vg2T": f(inputs["L1_attn_vg_w2"].T),
        "vg2q": f(inputs["L1_attn_vg_w2"].T.reshape(8, 128, 8, 128)
                  .transpose(1, 0, 2, 3).reshape(128, 8192)),
        "eag": f(np.stack([inputs["L1_attn_eta_w"], inputs["L1_attn_alpha_w"],
                           inputs["L1_attn_gate_w"]], axis=1)),
        "eab": f(np.stack([inputs["L1_attn_eta_b"], inputs["L1_attn_alpha_b"],
                           inputs["L1_attn_gate_b"]]).reshape(3, 1)),
        "wgTi": f(inputs["L1_ffn_w_gate"].T.astype(np.int8)),
        "wuTi": f(inputs["L1_ffn_w_up"].T.astype(np.int8)),
        "wdTi": f(inputs["L1_ffn_w_down"].T.astype(np.int8)),
        "identf": np.eye(128, dtype=np.float32),
        "esel": _mk_esel(),
    }
    in_maps = []
    for c in range(8):
        bat, half = c // 2, c % 2
        m = dict(shared)
        m["x_fm"] = f(x_fm[bat, :, half * T:(half + 1) * T])
        m["parity"] = np.full((1, 1), float(half), np.float32)
        ch_i = (half * 2048 + np.arange(2048)) // 64
        jj = np.arange(64)[:, None]
        m["l1msk"] = np.concatenate(
            [(ch_i[None, :] > jj), (ch_i[None, :] == jj)],
            axis=1).astype(ml_dtypes.bfloat16)
        in_maps.append(m)
    return in_maps


def kernel(**inputs):
    b = build_program()
    in_maps = _prep_inputs(inputs)
    res = run_bass_kernel_spmd(b.nc, in_maps, core_ids=list(range(8)))
    out = np.empty((4, 4096, 1024), np.float32)
    for c in range(8):
        bat, half = c // 2, c % 2
        out[bat, half * T:(half + 1) * T, :] = res.results[c]["out_fm"].T
    return out
